# version 05
# baseline (speedup 1.0000x reference)
import os
import numpy as np
import ml_dtypes
import concourse.bass as bass
import concourse.mybir as mybir
from concourse.bass_utils import run_bass_kernel_spmd

F32 = mybir.dt.float32
F16 = mybir.dt.float16
BF16 = mybir.dt.bfloat16
I32 = mybir.dt.int32
ALU = mybir.AluOpType
AF = mybir.ActivationFunctionType
AXX = mybir.AxisListType.X
bfdt = ml_dtypes.bfloat16

DM, SEQ, DIN, DST, HEADS, PD, CHUNK = 2048, 2048, 4096, 128, 64, 64, 128
DIP, CONVD, MLP = 8512, 4352, 16384
SC, NSC = 256, 8
CPS = SC // CHUNK
TH = 1024
NEG = -1.0e30

DBG = []


class Prog:
    ENGS = ("sync", "scalar", "vector", "tensor", "gpsimd")

    def __init__(self, nc):
        self.nc = nc
        self.items = {e: [] for e in self.ENGS}
        self.sems = {}
        self.counts = {}
        for s in ("dma", "dmg", "pe", "dve", "act", "ph"):
            self.sems[s] = nc._stack.enter_context(nc.semaphore(s))
            self.counts[s] = 0

    def ensure(self, s):
        if s not in self.sems:
            self.sems[s] = self.nc._stack.enter_context(self.nc.semaphore(s))
            self.counts[s] = 0

    def do(self, eng, fn, waits=(), incs=()):
        waits = tuple(w for w in waits if w is not None)
        for s, v in waits:
            self.ensure(s)
            assert v <= self.counts[s], f"wait {s}>={v} not yet produced ({self.counts[s]})"
        toks = []
        for s, a in incs:
            self.ensure(s)
            self.counts[s] += a
            toks.append((s, self.counts[s]))
        self.items[eng].append((fn, waits, tuple(incs)))
        return toks

    def dma_flush_waits(self):
        return [(s, self.counts[s]) for s in self.sems
                if s.startswith("D_") and self.counts[s] > 0]

    def barrier(self):
        self.do("sync", None, waits=self.dma_flush_waits())
        base = self.counts["ph"]
        for e in self.ENGS:
            self.do(e, None, incs=[("ph", 1)])
        tgt = base + len(self.ENGS)
        for e in self.ENGS:
            self.do(e, None, waits=[("ph", tgt)])
        return [("ph", tgt)]

    def emit(self, block):
        for e in self.ENGS:
            items = self.items[e]
            if not items:
                continue

            def mk(items):
                def body(proxy):
                    for fn, waits, incs in items:
                        for s, v in waits:
                            proxy.wait_ge(self.sems[s], v)
                        inst = fn(proxy) if fn is not None else None
                        if inst is None:
                            for s, a in incs:
                                proxy.sem_inc(self.sems[s], a)
                        else:
                            for s, a in incs:
                                inst = inst.then_inc(self.sems[s], a)
                return body
            getattr(block, e)(mk(items))


def _perm(n):
    return np.concatenate([np.concatenate([np.arange(256 * i, 256 * i + 256, 2),
                                           np.arange(256 * i + 1, 256 * i + 256, 2)])
                           for i in range(n // 256)])


def host_prep(inp):
    f32 = np.float32
    d = {}
    dperm = _perm(DM)
    nperm = _perm(MLP)
    d["winT"] = np.ascontiguousarray(np.asarray(inp["L0_attn_W_in"], f32).T[dperm, :])
    d["woutT"] = np.ascontiguousarray(np.asarray(inp["L0_attn_W_out"], f32).T)
    wconv = np.asarray(inp["L0_attn_W_conv"], f32).reshape(CONVD, 4)
    d["wconv"] = np.ascontiguousarray(wconv.reshape(34, 128, 4).transpose(1, 0, 2)).reshape(128, 34 * 4)
    d["bconv"] = np.ascontiguousarray(np.asarray(inp["L0_attn_b_conv"], f32).reshape(34, 128).T)
    d["alog"] = np.ascontiguousarray(np.asarray(inp["L0_attn_A_log"], f32).reshape(HEADS, 1))
    d["dvec"] = np.ascontiguousarray(np.asarray(inp["L0_attn_D"], f32).reshape(1, HEADS))
    d["dtbias"] = np.ascontiguousarray(np.asarray(inp["L0_attn_dt_bias"], f32).reshape(HEADS, 1))
    d["nwc"] = np.ascontiguousarray(np.asarray(inp["L0_attn_norm_weight"], f32).reshape(32, 128).T)
    d["gwp"] = np.ascontiguousarray(np.asarray(inp["L0_ffn_gate_W_packed"])[nperm, :].T.astype(np.int32))
    d["uwp"] = np.ascontiguousarray(np.asarray(inp["L0_ffn_up_W_packed"])[nperm, :].T.astype(np.int32))
    d["gsc"] = np.ascontiguousarray(np.repeat(np.asarray(inp["L0_ffn_gate_scales"], f32)[nperm, :].T, 16, axis=0))
    d["usc"] = np.ascontiguousarray(np.repeat(np.asarray(inp["L0_ffn_up_scales"], f32)[nperm, :].T, 16, axis=0))
    d["dwp"] = np.ascontiguousarray(np.asarray(inp["L0_ffn_down_W_packed"]).T.astype(np.int32))
    d["dsc"] = np.ascontiguousarray(np.repeat(np.asarray(inp["L0_ffn_down_scales"], f32).T, 16, axis=0))
    d["eyeb"] = np.eye(128, dtype=bfdt)
    d["eyef"] = np.eye(128, dtype=np.float32)
    s, l = np.meshgrid(np.arange(128), np.arange(128), indexing="ij")
    d["maskb"] = np.where(s <= l, 0.0, NEG).astype(f32)
    return d


IN_SPECS = [
    ("x", [SEQ, DM], F32), ("xT", [DM, SEQ], F32),
    ("winT", [DM, DIP], F32), ("woutT", [DIN, DM], F32),
    ("wconv", [128, 34 * 4], F32), ("bconv", [128, 34], F32),
    ("alog", [HEADS, 1], F32), ("dvec", [1, HEADS], F32), ("dtbias", [HEADS, 1], F32),
    ("nwc", [128, 32], F32),
    ("gwp", [1024, MLP], I32), ("uwp", [1024, MLP], I32),
    ("gsc", [1024, MLP], F32), ("usc", [1024, MLP], F32),
    ("dwp", [MLP // 2, DM], I32), ("dsc", [MLP // 2, DM], F32),
    ("eyeb", [128, 128], BF16), ("eyef", [128, 128], F32),
    ("maskb", [128, 128], F32),
]

SCRATCH_SPECS = [
    ("xnt_d", [DM, SEQ], BF16), ("winT_bf", [DM, DIP], BF16),
    ("woutT_bf", [DIN, DM], BF16), ("acc_d", [DM, SEQ], F32),
    ("dacs_d", [HEADS, SC], F32),
    ("gdec", [DM, MLP], F16), ("udec", [DM, MLP], F16), ("ddec", [MLP, DM], F16),
    ("xn_d", [SEQ, DM], BF16), ("st_d", [128, 128], F32), ("y_d", [SEQ, DIN], F32),
]


class K:
    def __init__(self):
        from contextlib import ExitStack
        nc = bass.Bass("TRN2", target_bir_lowering=False)
        nc._stack = ExitStack()
        self.nc = nc
        self.A = {}
        for name, shape, dt in IN_SPECS:
            self.A[name] = nc.dram_tensor(name, shape, dt, kind="ExternalInput").ap()
        self.A["out"] = nc.dram_tensor("out", [DM, SEQ], F32, kind="ExternalOutput").ap()
        for name, shape, dt in SCRATCH_SPECS:
            kind = "ExternalOutput" if name in DBG else "Internal"
            self.A[name] = nc.dram_tensor(name, shape, dt, kind=kind).ap()
        self.P = Prog(nc)
        self.last = {}

    def T(self, n, s, d, stack=None):
        return (stack or self.nc._stack).enter_context(self.nc.sbuf_tensor(n, s, d))

    def PT(self, n, s, d, stack=None):
        return (stack or self.nc._stack).enter_context(self.nc.psum_tensor(n, s, d))

    def w(self, key):
        t = self.last.get(key)
        return [t] if t is not None else []

    def setw(self, key, toks):
        if toks:
            self.last[key] = toks[-1]


def build():
    k = K()
    nc, P, A, D = k.nc, k.P, k.A, k.P.do

    k.eye_b = k.T("c_eye_b", [128, 128], BF16)
    k.eye_f = k.T("c_eye_f", [128, 128], F32)
    k.mask = k.T("c_mask", [128, 128], F32)
    k.eps = k.T("c_eps", [128, 1], F32)
    k.one = k.T("c_one", [128, 1], F32)
    k.negtwo = k.T("c_negtwo", [128, 1], F32)
    k.negsix = k.T("c_negsix", [128, 1], F32)
    k.onesf = k.T("c_onesf", [1, 128], F32)
    for tile, src in [(k.eye_b, "eyeb"), (k.eye_f, "eyef"), (k.mask, "maskb")]:
        D("sync", lambda e, t=tile, s=src: e.dma_start(out=t[:], in_=A[s][:]), incs=[("D_cst", 16)])
    for tile, val in [(k.eps, 1.0e-5), (k.one, 1.0), (k.negtwo, -2.0), (k.negsix, -6.0)]:
        D("vector", lambda e, t=tile, v=val: e.memset(t[:], v), incs=[("dve", 1)])
    D("vector", lambda e: e.memset(k.onesf[:], 1.0), incs=[("dve", 1)])
    k.consts = [("D_cst", P.counts["D_cst"]), ("dve", P.counts["dve"])]

    lim = int(os.environ.get("PH_LIMIT", "3"))
    tok = phase_ln(k)
    if lim >= 1:
        tok = phase_wconv(k, tok)
    if lim >= 2:
        tok = phase_mamba(k, tok)
    if lim >= 3:
        phase_ffn(k, tok)
    else:
        phase_stub_out(k, tok)

    with nc.Block() as block:
        P.emit(block)
    return nc


def phase_stub_out(k, prev):
    from contextlib import ExitStack
    P, A, D = k.P, k.A, k.P.do
    st = ExitStack()
    z = k.T("stub_z", [128, SEQ], F32, st)
    tok = D("vector", lambda e: e.memset(z[:], 0.0), waits=prev, incs=[("dve", 1)])
    out_dst = A["out"].rearrange("(d p) t -> d p t", p=128)
    for i in range(16):
        tok = D("sync", lambda e, i=i: e.dma_start(out=out_dst[i], in_=z[:]),
                waits=tok, incs=[("D_outs", 16)])
    D("sync", lambda e: None, waits=P.dma_flush_waits())
    st.close()


def phase_ln(k):
    from contextlib import ExitStack
    P, A, D = k.P, k.A, k.P.do
    st = ExitStack()
    p0x = k.T("p0_x", [128, 2 * DM], F32, st)
    p0sq = k.T("p0_sq", [128, DM], F32, st)
    p0xn = k.T("p0_xn", [128, DM], BF16, st)
    p0st = k.T("p0_st", [128, 8], F32, st)
    p0tr = k.T("p0_tr", [128, 16 * 128], BF16, st)
    p0std = k.T("p0_std", [128, 128], F32, st)
    p0psA = k.PT("p0_psA", [128, 1024], BF16, st)
    p0psB = k.PT("p0_psB", [128, 1024], BF16, st)
    x_t = A["x"].rearrange("(a p) d -> a p d", p=128)
    xnt_dst = A["xnt_d"].rearrange("(c p) t -> c p t", p=128)
    prev_st = None
    prev_xn = None
    prev_ev = None
    for i in range(16):
        buf = p0x[:, (i % 2) * DM:(i % 2) * DM + DM]
        tok_ld = D("sync", lambda e, b=buf, i=i: e.dma_start(out=b, in_=x_t[i]),
                   waits=([prev_xn] if i >= 2 else []), incs=[(f"D_p0x{i % 2}", 16)])
        t_sum, t_sq, t_mean, t_var, t_rstd, t_m2 = (p0st[:, j:j + 1] for j in range(6))
        tok_su = D("vector", lambda e, b=buf: e.reduce_sum(t_sum, b, axis=AXX), waits=tok_ld,
                   incs=[("dve", 1)])
        tok_sq = D("scalar", lambda e, b=buf: e.activation(p0sq[:], b, AF.Square, accum_out=t_sq),
                   waits=tok_ld, incs=[("act", 1)])
        tok_mn = D("vector", lambda e: e.tensor_scalar(t_mean, t_sum, 1.0 / DM, None, ALU.mult),
                   waits=tok_sq + tok_su, incs=[("dve", 1)])
        tok_m2 = D("vector", lambda e: e.tensor_tensor(t_m2, t_mean, t_mean, ALU.mult),
                   waits=tok_mn, incs=[("dve", 1)])
        tok_va = D("vector", lambda e: e.tensor_scalar(t_var, t_sq, 1.0 / DM, None, ALU.mult),
                   incs=[("dve", 1)])
        tok_v = D("vector", lambda e: e.tensor_tensor(t_var, t_var, t_m2, ALU.subtract),
                  waits=tok_m2 + tok_va, incs=[("dve", 1)])
        tok_ln = D("scalar", lambda e: e.activation(t_rstd, t_var, AF.Ln, bias=k.eps[:]),
                   waits=tok_v + k.consts, incs=[("act", 1)])
        tok_r = D("scalar", lambda e: e.activation(t_rstd, t_rstd, AF.Exp, scale=-0.5),
                  waits=tok_ln, incs=[("act", 1)])
        tok_xn = D("vector", lambda e, b=buf: e.tensor_scalar(p0xn[:], b, t_mean, t_rstd,
                   ALU.subtract, ALU.mult), waits=tok_r + tok_mn + ([prev_st] if prev_st else []),
                   incs=[("dve", 1)])
        prev_xn = tok_xn[0]
        if os.environ.get("P0DBG") == "1":
            tok_sd = D("vector", lambda e, i=i: e.tensor_copy(p0std[:, i * 8:i * 8 + 8], p0st[:]),
                       waits=tok_xn, incs=[("dve", 1)])
            prev_xn = tok_sd[0]
            xn_dst = A["xn_d"].rearrange("(a p) d -> a p d", p=128)
            tok_xd = D("sync", lambda e, i=i: e.dma_start(out=xn_dst[i], in_=p0xn[:]),
                       waits=tok_xn, incs=[("D_dbg", 16)])
            prev_xn = tok_xd[0]
        evs = []
        for kk in range(16):
            c, pl = kk // 2, kk % 2
            src = p0xn[:, 256 * c + pl:256 * c + 256:2]
            gi = kk // 4
            pt_ = [p0psA, p0psB][gi % 2]
            pslot = pt_[:, (kk % 4) * 128:(kk % 4) * 128 + 128]
            w = tok_xn + k.consts
            if kk % 4 == 0 and gi >= 2:
                w = w + [evs[gi - 2]]
            elif kk % 4 == 0 and gi < 2 and prev_ev is not None:
                w = w + [prev_ev]
            tok_tr = D("tensor", lambda e, d_=pslot, s=src: e.transpose(d_, s, k.eye_b[:]),
                       waits=w, incs=[("pe", 1)])
            if kk % 4 == 3:
                tok_ev = D("scalar", lambda e, gi=gi, p_=pt_: e.copy(
                    p0tr[:, gi * 512:gi * 512 + 512], p_[:, 0:512]),
                    waits=tok_tr, incs=[("act", 1)])
                evs.append(tok_ev[0])
        prev_ev = evs[-1]
        tok_st = D("sync", lambda e, i=i: e.dma_start(
            out=xnt_dst[:, :, 128 * i:128 * i + 128].rearrange("c p t -> p c t"),
            in_=p0tr[:].rearrange("p (c t) -> p c t", c=16)),
            waits=[evs[-1]], incs=[("D_p0st", 16)])
        prev_st = tok_st[0]
    if os.environ.get("P0DBG") == "1":
        D("sync", lambda e: e.dma_start(out=A["st_d"][:], in_=p0std[:]),
          waits=[("dve", P.counts["dve"])], incs=[("D_dbg", 16)])
    tok = P.barrier()
    st.close()
    return tok


def phase_wconv(k, prev):
    from contextlib import ExitStack
    P, A, D = k.P, k.A, k.P.do
    st = ExitStack()
    p1f = k.T("p1_f", [128, DIP], F32, st)
    p1b = k.T("p1_b", [128, DIP], BF16, st)
    prevtok = None
    for srcT, dstT, wd, n in [(A["winT"], A["winT_bf"], DIP, 16), (A["woutT"], A["woutT_bf"], DM, 32)]:
        s_t = srcT.rearrange("(a p) f -> a p f", p=128)
        d_t = dstT.rearrange("(a p) f -> a p f", p=128)
        for i in range(n):
            tok_ld = D("sync", lambda e, i=i, s=s_t, wd=wd: e.dma_start(out=p1f[:, 0:wd], in_=s[i]),
                       waits=prev + ([prevtok] if prevtok else []), incs=[("D_p1l", 16)])
            tok_cv = D("scalar", lambda e, wd=wd: e.copy(p1b[:, 0:wd], p1f[:, 0:wd]),
                       waits=tok_ld, incs=[("act", 1)])
            tok_st = D("sync", lambda e, i=i, d=d_t, wd=wd: e.dma_start(out=d[i], in_=p1b[:, 0:wd]),
                       waits=tok_cv, incs=[("D_p1s", 16)])
            prevtok = tok_st[0]
    tok = P.barrier()
    st.close()
    return tok


def phase_mamba(k, prev):
    from contextlib import ExitStack
    P, A, D = k.P, k.A, k.P.do
    st = ExitStack()
    T = lambda n, s, d: k.T(n, s, d, st)
    PT = lambda n, s, d: k.PT(n, s, d, st)
    w, setw = k.w, k.setw

    hs = T("m_hs", [128, DIN], F32)
    hsb = T("m_hsb", [128, DIN], BF16)
    xbcs = T("m_xbcs", [128, 34 * SC], BF16)
    zst = T("m_zst", [128, 32 * SC], BF16)
    xnsl = T("m_xnsl", [128, 16 * SC], BF16)
    wsl = T("m_wsl", [128, 2 * 4096], BF16)
    flg = T("m_flg", [1, 2 * 1024], F32)
    dtp = T("m_dtp", [64, SC], F32)
    dts = T("m_dts", [64, SC], F32)
    dca = T("m_dca", [64, 2 * (128 + CHUNK)], F32)
    dcb = T("m_dcb", [64, 2 * (128 + CHUNK)], F32)
    mht = T("m_mht", [128, 2 * 1024], BF16)
    segs = T("m_segs", [128, 1024], F32)
    xch = T("m_xch", [128, DIN], BF16)
    ut = T("m_ut", [128, DIN], BF16)
    upt = T("m_upt", [128, DIN], BF16)
    yb = T("m_yb", [128, DIN], F32)
    ynb = T("m_ynb", [128, DIN], BF16)
    yntb = T("m_yntb", [128, 32 * SC], BF16)
    cbtm = T("m_cbtm", [128, 128], BF16)
    blt = T("m_blt", [128, 128], BF16)
    zs = T("m_zs", [128, DIN], BF16)
    sm = T("m_sm", [128, 512], F32)
    nwcol = T("m_nwcol", [128, 32], F32)
    cvs = T("m_cvs", [128, SC + 4], F32)
    cvs2 = T("m_cvs2", [128, SC], F32)
    halo = T("m_halo", [128, 34 * 3], F32)
    colw = T("m_colw", [128, 34 * 4], F32)
    colb = T("m_colb", [128, 34], F32)
    acol = T("m_acol", [64, 4], F32)
    accst = T("m_accst", [128, SC], F32)
    xts = T("m_xts", [128, 2 * SC], F32)
    rowsc = T("m_rowsc", [1, 128], F32)
    scr5 = T("m_scr5", [128, 512], F32)
    p_yA = PT("p_yA", [128, 512], F32)
    p_yB = PT("p_yB", [128, 512], F32)
    p_sgA = PT("p_sgA", [128, 1024], F32)
    p_sgB = PT("p_sgB", [128, 1024], F32)
    p_trA = PT("p_trA", [128, 1024], BF16)
    p_trB = PT("p_trB", [128, 1024], BF16)
    p_tr = [p_trA, p_trB]
    dAcsT = sm[:, 0:64]
    sdoT = sm[:, 64:128]
    decayT = sm[:, 128:192]
    dtT = sm[:, 192:256]
    cdb = sm[:, 256:320]
    Db = sm[:, 320:384]
    rms = sm[:, 384:385]
    rstd = sm[:, 385:386]

    D("sync", lambda e: e.dma_start(out=colw[:], in_=A["wconv"][:]), waits=prev, incs=[("D_minit", 16)])
    D("sync", lambda e: e.dma_start(out=colb[:], in_=A["bconv"][:]), incs=[("D_minit", 16)])
    D("sync", lambda e: e.dma_start(out=acol[:, 0:1], in_=A["alog"][:]), incs=[("D_minit", 16)])
    D("sync", lambda e: e.dma_start(out=acol[:, 1:2], in_=A["dtbias"][:]), incs=[("D_minit", 16)])
    D("sync", lambda e: e.dma_start(out=rowsc[:, 0:64], in_=A["dvec"][:]), incs=[("D_minit", 16)])
    D("sync", lambda e: e.dma_start(out=nwcol[:], in_=A["nwc"][:]), incs=[("D_minit", 16)])
    cdma = [("D_minit", P.counts["D_minit"])]
    D("vector", lambda e: e.memset(hs[:], 0.0), waits=prev, incs=[("dve", 1)])
    D("vector", lambda e: e.memset(halo[:], 0.0), incs=[("dve", 1)])
    D("vector", lambda e: e.memset(dca[:], 0.0), incs=[("dve", 1)])
    D("vector", lambda e: e.memset(dcb[:], 0.0), incs=[("dve", 1)])
    tok_a = D("scalar", lambda e: e.activation(acol[:, 2:3], acol[:, 0:1], AF.Exp), waits=cdma, incs=[("act", 1)])
    D("vector", lambda e: e.tensor_scalar(acol[:, 2:3], acol[:, 2:3], -1.0, None, ALU.mult),
      waits=tok_a, incs=[("dve", 1)])
    tok_mm = D("tensor", lambda e: e.matmul(p_yA[:, 0:64], k.onesf[:], rowsc[:, 0:64], start=True, stop=True),
               waits=cdma + k.consts, incs=[("pe", 1)])
    D("vector", lambda e: e.tensor_copy(Db, p_yA[:, 0:64]), waits=tok_mm, incs=[("dve", 1)])
    bar = P.barrier()

    xnt_src = A["xnt_d"].rearrange("(c p) t -> c p t", p=128)
    win_src = A["winT_bf"].rearrange("(c p) f -> c p f", p=128)
    wo_src = A["woutT_bf"].rearrange("(c p) d -> c p d", p=128)
    xT_src = A["xT"].rearrange("(d p) t -> d p t", p=128)
    acc_dst = A["acc_d"].rearrange("(d p) t -> d p t", p=128)

    ptr_key = [("ptr", 0), ("ptr", 1)]

    def transpose_seq(pairs, first_waits):
        evs = []
        n = len(pairs)
        for i in range(n):
            gi = i // 4
            pt = p_tr[gi % 2]
            slot = pt[:, (i % 4) * 128:(i % 4) * 128 + 128]
            wt = []
            if i == 0:
                wt = list(first_waits) + w(ptr_key[0]) + w(ptr_key[1])
            if i % 4 == 0 and gi >= 2:
                wt = wt + [evs[gi - 2]]
            src = pairs[i][0]
            tok_tr = D("tensor", lambda e, d_=slot, s=src: e.transpose(d_, s, k.eye_b[:]),
                       waits=wt, incs=[("pe", 1)])
            if i % 4 == 3 or i == n - 1:
                fn = pairs[i][1]
                tok_ev = fn(pt, gi, tok_tr)
                evs.append(tok_ev[0])
        if len(evs) >= 2:
            setw(ptr_key[(len(evs) - 2) % 2], [evs[-2]])
        setw(ptr_key[(len(evs) - 1) % 2], [evs[-1]])
        return evs

    for sc in range(NSC):
        t0 = sc * SC
        start_tok = bar if sc == 0 else []
        tok_xn = D("sync", lambda e, t0=t0: e.dma_start(
            out=xnsl[:].rearrange("p (c t) -> p c t", c=16),
            in_=xnt_src[:, :, t0:t0 + SC].rearrange("c p t -> p c t")),
            waits=start_tok + w("xnsl"), incs=[("D_xnsl", 16)])

        dtraw_tok = None
        tokm = []
        for fg in range(34):
            f0 = 256 * fg
            wdt = 256 if fg < 33 else 64
            sb = wsl[:, (fg % 2) * 4096:(fg % 2) * 4096 + 16 * wdt]
            tok_w = D("sync", lambda e, f0=f0, wdt=wdt, sb=sb: e.dma_start(
                out=sb.rearrange("p (c f) -> p c f", c=16),
                in_=win_src[:, :, f0:f0 + wdt].rearrange("c p f -> p c f")),
                waits=start_tok + w(("wsl", fg % 2)), incs=[(f"D_wsl{fg % 2}", 16)])
            for ft in range(2 if wdt == 256 else 1):
                fr = f0 + 128 * ft
                rows = min(128, DIP - fr)
                par = (fg * 2 + ft) % 2
                pt_ = [p_yA, p_yB][par]
                ps = pt_[0:rows, 0:SC]
                pkey = ("p_inp", par)
                for kk in range(16):
                    tokm = D("tensor", lambda e, ps=ps, sb=sb, kk=kk, ft=ft, wdt=wdt, rows=rows: e.matmul(
                        ps, sb[:, kk * wdt + 128 * ft: kk * wdt + 128 * ft + rows],
                        xnsl[:, kk * SC:(kk + 1) * SC],
                        start=(kk == 0), stop=(kk == 15)),
                        waits=(tok_xn + tok_w + w(pkey) + w(("p_y", par)) if kk == 0 else []),
                        incs=([("pe", 1)] if kk == 15 else []))
                ftile = fr // 128
                if fr + rows <= DIN:
                    tok_e = D("scalar", lambda e, ps=ps, ftile=ftile: e.activation(
                        zst[:, ftile * SC:(ftile + 1) * SC], ps, AF.Silu),
                        waits=tokm + w("zst_cons"), incs=[("act", 1)])
                    setw(pkey, tok_e)
                elif fr < DIN + CONVD:
                    ct = ftile - 32
                    tok_h = D("vector", lambda e, ct=ct: e.tensor_copy(cvs[:, 0:3], halo[:, ct * 3:ct * 3 + 3]),
                              waits=tokm + w("cvs") + w(("halo", ct)), incs=[("dve", 1)])
                    tok_c = D("vector", lambda e, ps=ps: e.tensor_copy(cvs[:, 3:3 + SC], ps), incs=[("dve", 1)])
                    setw(pkey, tok_c)
                    tok_h2 = D("vector", lambda e, ct=ct: e.tensor_copy(halo[:, ct * 3:ct * 3 + 3],
                               cvs[:, SC:SC + 3]), waits=tok_c, incs=[("dve", 1)])
                    setw(("halo", ct), tok_h2)
                    tok_t = D("vector", lambda e, ct=ct: e.tensor_scalar(
                        cvs2[:], cvs[:, 0:SC], colw[:, ct * 4:ct * 4 + 1], None, ALU.mult),
                        waits=w("cvs2") + tok_h + tok_h2)
                    for j in (1, 2, 3):
                        tok_t = D("vector", lambda e, ct=ct, j=j: e.scalar_tensor_tensor(
                            cvs2[:], cvs[:, j:j + SC], colw[:, ct * 4 + j:ct * 4 + j + 1],
                            cvs2[:], ALU.mult, ALU.add),
                            incs=([("dve", 1)] if j == 3 else []))
                    setw("cvs", tok_t)
                    tok_s = D("scalar", lambda e, ct=ct: e.activation(
                        xbcs[:, ct * SC:(ct + 1) * SC], cvs2[:], AF.Silu,
                        bias=colb[:, ct:ct + 1]), waits=tok_t, incs=[("act", 1)])
                    setw("cvs2", tok_s)
                else:
                    tok_c = D("vector", lambda e, ps=ps: e.tensor_copy(dtp[:], ps),
                              waits=tokm, incs=[("dve", 1)])
                    setw(pkey, tok_c)
                    dtraw_tok = tok_c
            setw(("wsl", fg % 2), tokm)
        inproj_act = [("act", P.counts["act"])]

        tok_b1 = D("vector", lambda e: e.tensor_scalar(dtp[:], dtp[:], acol[:, 1:2], None, ALU.add),
                   waits=dtraw_tok, incs=[("dve", 1)])
        tok = D("scalar", lambda e: e.activation(dts[:], dtp[:], AF.Abs), waits=tok_b1, incs=[("act", 1)])
        tok = D("scalar", lambda e: e.activation(dts[:], dts[:], AF.Exp, scale=-1.0), waits=tok,
                incs=[("act", 1)])
        tok = D("scalar", lambda e: e.activation(dts[:], dts[:], AF.Ln, bias=k.one[0:64, :]), waits=tok,
                incs=[("act", 1)])
        tok_mx = D("vector", lambda e: e.tensor_scalar(dtp[:], dtp[:], 0.0, None, ALU.max), waits=tok,
                   incs=[("dve", 1)])
        tok3 = D("vector", lambda e: e.tensor_tensor(dtp[:], dtp[:], dts[:], ALU.add), waits=tok_mx,
                 incs=[("dve", 1)])
        tok_da = tok3
        for ch in range(CPS):
            tok_da = D("vector", lambda e, ch=ch: e.tensor_scalar(
                dca[:, 256 * ch + 128:256 * ch + 256], dtp[:, ch * CHUNK:(ch + 1) * CHUNK],
                acol[:, 2:3], None, ALU.mult),
                waits=tok_da + (w("dacs_cons") + w("dacsb_done") if ch == 0 else []), incs=[("dve", 1)])
        tok_cs = tok_da
        for shp in range(7):
            s_ = 1 << shp
            sb_, db_ = (dca, dcb) if shp % 2 == 0 else (dcb, dca)
            for ch in range(CPS):
                b = 256 * ch + 128
                tok_cs = D("vector", lambda e, b=b, s_=s_, sb_=sb_, db_=db_: e.tensor_tensor(
                    db_[:, b:b + CHUNK], sb_[:, b:b + CHUNK], sb_[:, b - s_:b + CHUNK - s_], ALU.add),
                    waits=tok_cs, incs=[("dve", 1)])
        dacs_buf = dcb
        dacs_v = lambda ch: dacs_buf[:, 256 * ch + 128:256 * ch + 256]
        tok_bd = D("gpsimd", lambda e: e.dma_start(
            out=A["dacs_d"][:].rearrange("h (c l) -> h c l", c=2),
            in_=dacs_buf[:].rearrange("h (c q) -> h c q", c=2)[:, :, 128:256]),
            waits=tok_cs + w(("flg", 0)) + w(("flg", 1)), incs=[("D_dacsb", 16)])
        setw("dacsb_done", tok_bd)

        for ch in range(CPS):
            cb = ch * CHUNK
            t1 = D("tensor", lambda e, ch=ch: e.transpose(p_yA[:, 0:64], dacs_v(ch), k.eye_f[0:64, 0:64]),
                   waits=tok_cs + w(("p_y", 0)), incs=[("pe", 1)])
            t1e = D("vector", lambda e: e.tensor_copy(dAcsT, p_yA[:, 0:64]), waits=t1, incs=[("dve", 1)])
            t2 = D("tensor", lambda e, cb=cb: e.transpose(p_yB[:, 0:64], dtp[:, cb:cb + CHUNK], k.eye_f[0:64, 0:64]),
                   waits=tok3 + w(("p_y", 1)), incs=[("pe", 1)])
            t2e = D("vector", lambda e: e.tensor_copy(dtT, p_yB[:, 0:64]), waits=t2, incs=[("dve", 1)])
            tok_sdo = D("scalar", lambda e: e.activation(sdoT, dAcsT, AF.Exp), waits=t1e, incs=[("act", 1)])
            t3_ = D("tensor", lambda e, ch=ch: e.transpose(p_yA[0:1, 64:128],
                    dacs_v(ch)[:, CHUNK - 1:CHUNK], k.eye_f[0:64, 0:64]), waits=t1e, incs=[("pe", 1)])
            t3e = D("vector", lambda e: e.tensor_copy(rowsc[:, 64:128], p_yA[0:1, 64:128]),
                    waits=t3_, incs=[("dve", 1)])
            tok_cd = D("scalar", lambda e: e.activation(rowsc[:, 0:64], rowsc[:, 64:128], AF.Exp),
                       waits=t3e, incs=[("act", 1)])
            t4 = D("tensor", lambda e: e.matmul(p_yA[:, 64:128], k.onesf[:], rowsc[:, 0:64],
                   start=True, stop=True), waits=tok_cd, incs=[("pe", 1)])
            t4e = D("vector", lambda e: e.tensor_copy(cdb, p_yA[:, 64:128]), waits=t4, incs=[("dve", 1)])
            t5 = D("tensor", lambda e: e.matmul(p_yB[:, 64:128], k.onesf[:], rowsc[:, 64:128],
                   start=True, stop=True), waits=t3e + t2e, incs=[("pe", 1)])
            t5e = D("vector", lambda e: e.tensor_tensor(decayT, p_yB[:, 64:128], dAcsT, ALU.subtract),
                    waits=t5 + t1e, incs=[("dve", 1)])
            tok_dce = D("scalar", lambda e: e.activation(decayT, decayT, AF.Exp), waits=t5e, incs=[("act", 1)])
            t6 = D("tensor", lambda e, cb=cb: e.matmul(p_yA[:, 128:256],
                   xbcs[:, 32 * SC + cb:32 * SC + cb + CHUNK],
                   xbcs[:, 33 * SC + cb:33 * SC + cb + CHUNK], start=True, stop=True),
                   waits=inproj_act + t4e, incs=[("pe", 1)])
            t6e = D("vector", lambda e: e.tensor_copy(cbtm[:], p_yA[:, 128:256]), waits=t6, incs=[("dve", 1)])
            smalls = t6e + t4e + [t5e[0], tok_dce[0], t2e[0], tok_sdo[0]]
            setw(("p_y", 0), t6e)
            setw(("p_y", 1), t5e)

            def mkev_x(dst):
                def f(ptx, gi, tok_tr):
                    return D("scalar", lambda e, ptx=ptx, gi=gi: e.copy(dst[:, gi * 512:gi * 512 + 512],
                             ptx[:, 0:512]), waits=tok_tr, incs=[("act", 1)])
                return f
            prx = [(xbcs[:, i * SC + cb:i * SC + cb + CHUNK], mkev_x(xch)) for i in range(32)]
            evx = transpose_seq(prx, inproj_act + w("xch_cons"))
            prz = [(zst[:, i * SC + cb:i * SC + cb + CHUNK], mkev_x(zs)) for i in range(32)]
            evz = transpose_seq(prz, inproj_act + w("zs_cons"))
            wt = w(ptr_key[0]) + w(ptr_key[1])
            tok_bl = D("tensor", lambda e, cb=cb: e.transpose(p_trA[:, 0:128],
                       xbcs[:, 32 * SC + cb:32 * SC + cb + CHUNK], k.eye_b[:]), waits=wt, incs=[("pe", 1)])
            tok_ble = D("vector", lambda e: e.tensor_copy(blt[:], p_trA[:, 0:128]), waits=tok_bl, incs=[("dve", 1)])
            setw(ptr_key[0], tok_ble)

            tok_u = D("vector", lambda e: e.tensor_tensor(
                ut[:].rearrange("p (h q) -> p h q", h=64), xch[:].rearrange("p (h q) -> p h q", h=64),
                dtT.unsqueeze(2).broadcast_to([128, 64, 64]), ALU.mult),
                waits=[evx[-1]] + t2e + w("ut_cons"), incs=[("dve", 1)])
            tok_up = D("vector", lambda e: e.tensor_tensor(
                upt[:].rearrange("p (h q) -> p h q", h=64), ut[:].rearrange("p (h q) -> p h q", h=64),
                decayT.unsqueeze(2).broadcast_to([128, 64, 64]), ALU.mult),
                waits=tok_u + tok_dce + w("upt_cons"), incs=[("dve", 1)])
            tok_hsb = D("vector", lambda e: e.tensor_copy(hsb[:], hs[:]),
                        waits=w("hs_upd") + w("hsb_cons"), incs=[("dve", 1)])

            y_toks = []
            sg_tok = []
            for g in range(8):
                pg = [p_sgA, p_sgB][g % 2]
                py = [p_yA, p_yB][g % 2]
                sgkey = ("p_sg", g % 2)
                pykey = ("p_y", g % 2)
                fslot = flg[:, (g % 2) * 1024:(g % 2) * 1024 + 1024]
                tok_fl = D("gpsimd", lambda e, g=g, cb=cb, fs=fslot: e.dma_start(
                    out=fs.rearrange("a (h l) -> a h l", h=8),
                    in_=A["dacs_d"][g * 8:g * 8 + 8, cb:cb + CHUNK].unsqueeze(0)),
                    waits=tok_bd + w(("flg", g % 2)), incs=[(f"D_flg{g % 2}", 16)])
                for jj in range(2):
                    sg_tok = D("tensor", lambda e, pg=pg, jj=jj, fs=fslot: e.matmul(
                        pg[:, jj * 512:jj * 512 + 512], k.onesf[:],
                        fs[:, jj * 512:jj * 512 + 512],
                        start=True, stop=True),
                        waits=(tok_fl + w(sgkey) if jj == 0 else []),
                        incs=([("pe", 1)] if jj == 1 else []))
                tok_sm = D("vector", lambda e, pg=pg, g=g: e.tensor_tensor(
                    segs[:].rearrange("p (h l) -> p h l", h=8),
                    pg[:].rearrange("p (h l) -> p h l", h=8),
                    dAcsT[:, g * 8:g * 8 + 8].unsqueeze(2).broadcast_to([128, 8, 128]),
                    ALU.subtract), waits=sg_tok + t1e + w("segs"), incs=[("dve", 1)])
                tok_sm = D("vector", lambda e: e.tensor_tensor(
                    segs[:].rearrange("p (h l) -> p h l", h=8),
                    segs[:].rearrange("p (h l) -> p h l", h=8),
                    k.mask[:].unsqueeze(1).broadcast_to([128, 8, 128]),
                    ALU.add), incs=[("dve", 1)])
                setw(sgkey, tok_sm)
                setw(("flg", g % 2), sg_tok)
                mh = mht[:, (g % 2) * 1024:(g % 2) * 1024 + 1024]
                tok_ex = D("scalar", lambda e, mh=mh: e.activation(mh, segs[:], AF.Exp),
                           waits=tok_sm + w(("mht", g % 2)), incs=[("act", 1)])
                setw("segs", tok_ex)
                tok_mh = D("vector", lambda e, mh=mh: e.tensor_tensor(
                    mh.rearrange("p (h l) -> p h l", h=8), mh.rearrange("p (h l) -> p h l", h=8),
                    cbtm[:].unsqueeze(1).broadcast_to([128, 8, 128]), ALU.mult),
                    waits=tok_ex + t6e, incs=[("dve", 1)])
                tok_yo = D("tensor", lambda e, py=py, g=g, cb=cb: e.matmul(
                    py[:], xbcs[:, 33 * SC + cb:33 * SC + cb + CHUNK],
                    hsb[:, g * 512:g * 512 + 512], start=True, stop=True),
                    waits=tok_hsb + w(pykey), incs=[("pe", 1)])
                tok_sc = D("vector", lambda e, py=py, g=g: e.tensor_tensor(
                    py[:].rearrange("p (h q) -> p h q", h=8), py[:].rearrange("p (h q) -> p h q", h=8),
                    sdoT[:, g * 8:g * 8 + 8].unsqueeze(2).broadcast_to([128, 8, 64]), ALU.mult),
                    waits=tok_yo + tok_sdo, incs=[("dve", 1)])
                tok_yd = []
                for hh in range(8):
                    tok_yd = D("tensor", lambda e, py=py, mh=mh, g=g, hh=hh: e.matmul(
                        py[:, hh * 64:hh * 64 + 64], mh[:, hh * 128:hh * 128 + 128],
                        ut[:, g * 512 + hh * 64:g * 512 + hh * 64 + 64],
                        start=False, stop=(hh == 7)),
                        waits=(tok_sc + tok_mh + tok_u if hh == 0 else []),
                        incs=([("pe", 1)] if hh == 7 else []))
                setw(("mht", g % 2), tok_yd)
                tok_d2 = D("vector", lambda e, g=g: e.tensor_tensor(
                    scr5[:].rearrange("p (h q) -> p h q", h=8),
                    xch[:, g * 512:g * 512 + 512].rearrange("p (h q) -> p h q", h=8),
                    Db[:, g * 8:g * 8 + 8].unsqueeze(2).broadcast_to([128, 8, 64]), ALU.mult),
                    waits=[evx[-1]] + w("scr5"))
                tok_ye = D("vector", lambda e, py=py, g=g: e.tensor_tensor(
                    yb[:, g * 512:g * 512 + 512], py[:], scr5[:], ALU.add),
                    waits=tok_yd + w("yb_cons"), incs=[("dve", 1)])
                setw("scr5", tok_ye)
                setw(pykey, tok_ye)
                y_toks.append(tok_ye[0])
            setw("ut_cons", tok_yd)
            setw("xch_cons", [y_toks[-1]])

            st_toks = []
            for g in range(8):
                py = [p_yA, p_yB][g % 2]
                pykey = ("p_y", g % 2)
                tok_s1 = D("tensor", lambda e, py=py, g=g: e.matmul(
                    py[:], blt[:], upt[:, g * 512:g * 512 + 512], start=True, stop=True),
                    waits=tok_ble + tok_up + w(pykey), incs=[("pe", 1)])
                D("vector", lambda e, g=g: e.tensor_tensor(
                    hs[:, g * 512:g * 512 + 512].rearrange("p (h q) -> p h q", h=8),
                    hs[:, g * 512:g * 512 + 512].rearrange("p (h q) -> p h q", h=8),
                    cdb[:, g * 8:g * 8 + 8].unsqueeze(2).broadcast_to([128, 8, 64]), ALU.mult),
                    waits=t4e + tok_hsb)
                tok_h2 = D("vector", lambda e, py=py, g=g: e.tensor_tensor(
                    hs[:, g * 512:g * 512 + 512], hs[:, g * 512:g * 512 + 512], py[:], ALU.add),
                    waits=tok_s1, incs=[("dve", 1)])
                setw(pykey, tok_h2)
                st_toks.append(tok_h2[0])
            setw("hs_upd", [st_toks[-1]])
            setw("upt_cons", [st_toks[-1]])
            setw("hsb_cons", [tok_yo[0] if False else y_toks[-1]])

            if os.environ.get("P2DBG") == "1":
                yd_dst = A["y_d"].rearrange("(a p) f -> a p f", p=128)
                tok_yD = D("sync", lambda e, sc=sc, ch=ch: e.dma_start(
                    out=yd_dst[sc * CPS + ch], in_=yb[:]), waits=[y_toks[-1]], incs=[("D_dbg", 16)])
            tok_yg = D("vector", lambda e: e.tensor_tensor(yb[:], yb[:], zs[:], ALU.mult),
                       waits=[y_toks[-1], evz[-1]], incs=[("dve", 1)])
            setw("zs_cons", tok_yg)
            tok_ss = D("scalar", lambda e: e.activation(ynb[:], yb[:], AF.Square, accum_out=rms),
                       waits=tok_yg, incs=[("act", 1)])
            tok_l2 = D("scalar", lambda e: e.activation(rstd, rms, AF.Ln, scale=1.0 / DIN, bias=k.eps[:]),
              waits=tok_ss, incs=[("act", 1)])
            tok_r2 = D("scalar", lambda e: e.activation(rstd, rstd, AF.Exp, scale=-0.5), waits=tok_l2,
                       incs=[("act", 1)])
            tok_yn = D("vector", lambda e: e.tensor_scalar(ynb[:], yb[:], rstd, None, ALU.mult),
                       waits=tok_r2, incs=[("dve", 1)])
            setw("yb_cons", tok_yn)

            def mkev_y(cb):
                def f(ptx, gi, tok_tr):
                    toks = []
                    for q in range(4):
                        i = gi * 4 + q
                        toks = D("vector", lambda e, ptx=ptx, i=i, q=q, cb=cb: e.tensor_scalar(
                            yntb[:, i * SC + cb:i * SC + cb + CHUNK], ptx[:, q * 128:q * 128 + 128],
                            nwcol[:, i:i + 1], None, ALU.mult),
                            waits=(tok_tr + w("yntb_cons") if q == 0 else []),
                            incs=([("dve", 1)] if q == 3 else []))
                    return toks
                return f
            pry = [(ynb[:, i * 128:i * 128 + 128], mkev_y(cb)) for i in range(32)]
            evy = transpose_seq(pry, tok_yn)
            setw("zst_cons", [evz[-1]])
            setw("xbcs_cons", [y_toks[-1]])
            setw("dacs_cons", t3_)
            setw("dtp_cons", t2)
            last_ev = evy[-1]

        op_toks = []
        for d_ in range(16):
            sb = wsl[:, (d_ % 2) * 4096:(d_ % 2) * 4096 + 4096]
            tok_w = D("sync", lambda e, d_=d_, sb=sb: e.dma_start(
                out=sb.rearrange("p (c f) -> p c f", c=32),
                in_=wo_src[:, :, d_ * 128:d_ * 128 + 128].rearrange("c p f -> p c f")),
                waits=bar + w(("wsl", d_ % 2)), incs=[(f"D_wsl{d_ % 2}", 16)])
            xtb = xts[:, (d_ % 2) * SC:(d_ % 2) * SC + SC]
            tok_x = D("sync", lambda e, d_=d_, xtb=xtb, t0=t0: e.dma_start(
                out=xtb, in_=xT_src[d_][:, t0:t0 + SC]),
                waits=w(("xts", d_ % 2)), incs=[(f"D_xts{d_ % 2}", 16)])
            py = [p_yA, p_yB][d_ % 2]
            pykey = ("p_y", d_ % 2)
            for c2 in range(32):
                tokm = D("tensor", lambda e, py=py, sb=sb, c2=c2: e.matmul(
                    py[:, 0:SC], sb[:, c2 * 128:c2 * 128 + 128], yntb[:, c2 * SC:c2 * SC + SC],
                    start=(c2 == 0), stop=(c2 == 31)),
                    waits=(tok_w + [last_ev] + w(pykey) if c2 == 0 else []),
                    incs=([("pe", 1)] if c2 == 31 else []))
            setw(("wsl", d_ % 2), tokm)
            tok_ac = D("vector", lambda e, py=py, xtb=xtb: e.tensor_tensor(accst[:], py[:, 0:SC], xtb, ALU.add),
                       waits=tokm + tok_x + w("accst"), incs=[("dve", 1)])
            setw(pykey, tok_ac)
            setw(("xts", d_ % 2), tok_ac)
            tok_ad = D("sync", lambda e, d_=d_, t0=t0: e.dma_start(
                out=acc_dst[d_][:, t0:t0 + SC], in_=accst[:]), waits=tok_ac, incs=[("D_accst", 16)])
            setw("accst", tok_ad)
            op_toks.append(tok_ad[0])
        setw("yntb_cons", [op_toks[-1]])

    bar = P.barrier()
    st.close()
    return bar


def phase_ffn(k, prev):
    from contextlib import ExitStack
    P, A, D = k.P, k.A, k.P.do
    st = ExitStack()
    T = lambda n, s, d: k.T(n, s, d, st)
    PT = lambda n, s, d: k.PT(n, s, d, st)
    w, setw = k.w, k.setw

    xh = T("f_xh", [128, 16 * TH], F16)
    accb = T("f_accb", [128, 16 * TH], F32)
    intT = T("f_intT", [128, 8 * TH], F16)
    bufG = [T(f"f_bufG{i}", [128, 16 * 256], F16) for i in range(2)]
    bufU = [T(f"f_bufU{i}", [128, 16 * 256], F16) for i in range(2)]
    bufD = [T(f"f_bufD{i}", [128, 2 * 2048], F16) for i in range(4)]
    pw = T("f_pw", [128, 2048], I32)
    psc = T("f_psc", [128, 2048], F32)
    tA = T("f_tA", [128, 512], I32)
    tB = T("f_tB", [128, 512], I32)
    tC = T("f_tC", [128, 512], I32)
    br4 = T("f_br4", [128, 512], BF16)
    br6 = T("f_br6", [128, 512], BF16)
    bg = T("f_bg", [128, 512], BF16)
    bv = T("f_bv", [128, 512], BF16)
    gel = T("f_gel", [128, TH], F16)
    p_g = PT("f_pg", [128, 1024], F32)
    p_u = PT("f_pu", [128, 1024], F32)
    p_dA = PT("f_pdA", [128, 512], F32)
    p_dB = PT("f_pdB", [128, 512], F32)

    gwp_s = A["gwp"].rearrange("(c p) n -> c p n", p=128)
    uwp_s = A["uwp"].rearrange("(c p) n -> c p n", p=128)
    gsc_s = A["gsc"].rearrange("(c p) n -> c p n", p=128)
    usc_s = A["usc"].rearrange("(c p) n -> c p n", p=128)
    dwp_s = A["dwp"].rearrange("(j p) h -> j p h", p=128)
    dsc_s = A["dsc"].rearrange("(j p) h -> j p h", p=128)
    gdec_s = A["gdec"].rearrange("(c p) n -> c p n", p=128)
    udec_s = A["udec"].rearrange("(c p) n -> c p n", p=128)
    ddec_s = A["ddec"].rearrange("(j p) h -> j p h", p=128)
    xnt_src = A["xnt_d"].rearrange("(c p) t -> c p t", p=128)
    acc_src = A["acc_d"].rearrange("(d p) t -> d p t", p=128)
    out_dst = A["out"].rearrange("(d p) t -> d p t", p=128)

    def decode_unit(n0, wsrc, ssrc, dst, dstkey, kind, extraw):
        if kind == "gu":
            tok_w = D("sync", lambda e, n0=n0, ws=wsrc: e.dma_start(
                out=pw[:].rearrange("p (c n) -> p c n", c=8),
                in_=ws[:, :, n0:n0 + 256].rearrange("c p n -> p c n")),
                waits=extraw + w("pw"), incs=[("D_pw", 16)])
            tok_s = D("sync", lambda e, n0=n0, ss=ssrc: e.dma_start(
                out=psc[:].rearrange("p (c n) -> p c n", c=8),
                in_=ss[:, :, n0:n0 + 256].rearrange("c p n -> p c n")),
                waits=w("psc"), incs=[("D_psc", 16)])
        else:
            ju = n0 // 256
            tok_w = D("sync", lambda e, ju=ju, ws=wsrc: e.dma_start(out=pw[:], in_=ws[ju]),
                      waits=extraw + w("pw"), incs=[("D_pw", 16)])
            tok_s = D("sync", lambda e, ju=ju, ss=ssrc: e.dma_start(out=psc[:], in_=ss[ju]),
                      waits=w("psc"), incs=[("D_psc", 16)])
        t8 = []
        prev_act = k.last.get("dec_act")
        for ci in range(4):
            soff, wdt = ci * 512, 512
            srcp = pw[:, soff:soff + wdt]
            scl = psc[:, soff:soff + wdt]
            sclv = scl.rearrange("p (c n) -> p c n", n=256)
            a = tA[:]; b2 = tB[:]; c3 = tC[:]; bh = tB[:]
            r4_ = br4[:]; r6_ = br6[:]; g_ = bg[:]; v_ = bv[:]
            vv = v_.rearrange("p (c n) -> p c n", n=256)
            if kind == "gu":
                dall = dst.rearrange("p (c two n) -> p c two n", two=2, n=256)
                dl = dall[:, 2 * ci:2 * ci + 2, 0, :]
                dh = dall[:, 2 * ci:2 * ci + 2, 1, :]
            else:
                dl = dst[:, ci * 512:ci * 512 + 512].rearrange("p (c n) -> p c n", n=256)
                dh = dst[:, 2048 + ci * 512:2048 + ci * 512 + 512].rearrange("p (c n) -> p c n", n=256)
            for pl in range(2):
                wt = (tok_w + w("dtmp") if (ci == 0 and pl == 0) else [])
                if prev_act is not None:
                    wt = wt + [prev_act]
                    prev_act = None
                if pl == 0:
                    D("vector", lambda e, srcp=srcp, a=a: e.tensor_scalar(a, srcp, 7, None, ALU.bitwise_and),
                      waits=wt)
                    t2 = D("vector", lambda e, srcp=srcp, b2=b2: e.tensor_scalar(b2, srcp, 8, None,
                           ALU.bitwise_and), incs=[("dve", 1)])
                else:
                    D("vector", lambda e, srcp=srcp, c3=c3: e.tensor_scalar(c3, srcp, 4, None,
                      ALU.logical_shift_right), waits=wt)
                    D("vector", lambda e, c3=c3, bh=bh: e.tensor_scalar(bh, c3, 8, None, ALU.bitwise_and))
                    t2 = D("vector", lambda e, c3=c3: e.tensor_scalar(c3, c3, 7, None, ALU.bitwise_and),
                           incs=[("dve", 1)])
                am = a if pl == 0 else c3
                asn = b2 if pl == 0 else bh
                t4a = D("scalar", lambda e, am=am, r4_=r4_: e.activation(r4_, am, AF.Relu, bias=k.negtwo[:],
                        scale=0.5), waits=t2, incs=[("act", 1)])
                t4b = D("scalar", lambda e, am=am, r6_=r6_: e.activation(r6_, am, AF.Relu, bias=k.negsix[:]),
                        incs=[("act", 1)])
                tc_ = D("scalar", lambda e, asn=asn, g_=g_: e.activation(g_, asn, AF.Identity, bias=k.one[:],
                        scale=-0.25), incs=[("act", 1)])
                prev_act = tc_[0]
                D("vector", lambda e, am=am, r4_=r4_, v_=v_: e.scalar_tensor_tensor(
                    v_, am, 0.5, r4_, ALU.mult, ALU.add), waits=t4a + t4b + tc_)
                D("vector", lambda e, v_=v_, r6_=r6_: e.tensor_tensor(v_, v_, r6_, ALU.add))
                D("vector", lambda e, v_=v_, g_=g_: e.tensor_tensor(v_, v_, g_, ALU.mult))
                ds = dl if pl == 0 else dh
                t8 = D("vector", lambda e, vv=vv, sclv=sclv, ds=ds: e.tensor_tensor(ds, vv, sclv, ALU.mult),
                       waits=(tok_s + w(dstkey) if (ci == 0 and pl == 0) else []),
                       incs=[("dve", 1)])
        k.last["dec_act"] = prev_act
        setw("pw", t8)
        setw("psc", t8)
        setw("dtmp", t8)
        return t8

    for H in range(2):
        t0 = H * TH
        tok_xh = D("sync", lambda e, t0=t0: e.dma_start(
            out=xh[:].bitcast(BF16).rearrange("p (c t) -> p c t", c=16),
            in_=xnt_src[:, :, t0:t0 + TH].rearrange("c p t -> p c t")),
            waits=prev, incs=[("D_xh", 16)])
        tok_xh = D("scalar", lambda e: e.copy(xh[:], xh[:].bitcast(BF16)),
                   waits=tok_xh, incs=[("act", 1)])
        tok_ab = D("sync", lambda e, t0=t0: e.dma_start(
            out=accb[:].rearrange("p (d t) -> p d t", d=16),
            in_=acc_src[:, :, t0:t0 + TH].rearrange("d p t -> p d t")),
            waits=prev, incs=[("D_accb", 16)])
        tD_ = []
        decode_both = os.environ.get("FFN_DECODE_BOTH") == "1"
        for sup in range(16):
            int_toks = []
            for u4 in range(4):
                n0 = 1024 * sup + 256 * u4
                ug = sup * 4 + u4
                par = ug % 2
                bG, bU = bufG[par], bufU[par]
                if H == 0 or decode_both:
                    tG = decode_unit(n0, gwp_s, gsc_s, bG, ("bufG", par), "gu",
                                     (prev if ug == 0 else []) + w(("bufG_st", par)))
                    tGs = D("sync", lambda e, n0=n0, bG=bG: e.dma_start(
                        out=gdec_s[:, :, n0:n0 + 256].rearrange("c p n -> p c n"),
                        in_=bG[:].rearrange("p (c n) -> p c n", c=16)), waits=tG, incs=[("D_gst", 16)])
                    setw(("bufG_st", par), tGs)
                    tU = decode_unit(n0, uwp_s, usc_s, bU, ("bufU", par), "gu", w(("bufU_st", par)))
                    tUs = D("sync", lambda e, n0=n0, bU=bU: e.dma_start(
                        out=udec_s[:, :, n0:n0 + 256].rearrange("c p n -> p c n"),
                        in_=bU[:].rearrange("p (c n) -> p c n", c=16)), waits=tU, incs=[("D_ust", 16)])
                    setw(("bufU_st", par), tUs)
                    tD_ = decode_unit(n0, dwp_s, dsc_s, bufD[u4], ("bufD", u4), "d",
                                      w("down_cons") + w(("bufD_st", u4)))
                    tDs = D("sync", lambda e, n0=n0, u4=u4: e.dma_start(
                        out=ddec_s[n0 // 128:n0 // 128 + 2].rearrange("j p h -> p j h"),
                        in_=bufD[u4][:].rearrange("p (j h) -> p j h", j=2)), waits=tD_, incs=[("D_dst", 16)])
                    setw(("bufD_st", u4), tDs)
                else:
                    tG = D("sync", lambda e, n0=n0, bG=bG: e.dma_start(
                        out=bG[:].rearrange("p (c n) -> p c n", c=16),
                        in_=gdec_s[:, :, n0:n0 + 256].rearrange("c p n -> p c n")),
                        waits=(prev if ug == 0 else []) + w(("bufG", par)), incs=[("D_gld", 16)])
                    tU = D("sync", lambda e, n0=n0, bU=bU: e.dma_start(
                        out=bU[:].rearrange("p (c n) -> p c n", c=16),
                        in_=udec_s[:, :, n0:n0 + 256].rearrange("c p n -> p c n")),
                        waits=w(("bufU", par)), incs=[("D_uld", 16)])
                    tD_ = D("sync", lambda e, n0=n0, u4=u4: e.dma_start(
                        out=bufD[u4][:].rearrange("p (j h) -> p j h", j=2),
                        in_=ddec_s[n0 // 128:n0 // 128 + 2].rearrange("j p h -> p j h")),
                        waits=w("down_cons") + w(("bufD_st", u4)), incs=[("D_dld", 16)])
                    setw(("bufD_st", u4), tD_)
                for jb in range(2):
                    tg = []
                    tu = []
                    for half in range(2):
                        for kk in range(16):
                            tg = D("tensor", lambda e, kk=kk, jb=jb, hf=half, bG=bG: e.matmul(
                                p_g[:, hf * 512:hf * 512 + 512],
                                bG[:, kk * 256 + jb * 128:kk * 256 + jb * 128 + 128],
                                xh[:, kk * TH + hf * 512:kk * TH + hf * 512 + 512],
                                start=(kk == 0), stop=(kk == 15)),
                                waits=(tG + tok_xh + w("p_g") if (kk == 0 and half == 0) else []),
                                incs=([("pe", 1)] if (kk == 15 and half == 1) else []))
                    for half in range(2):
                        for kk in range(16):
                            tu = D("tensor", lambda e, kk=kk, jb=jb, hf=half, bU=bU: e.matmul(
                                p_u[:, hf * 512:hf * 512 + 512],
                                bU[:, kk * 256 + jb * 128:kk * 256 + jb * 128 + 128],
                                xh[:, kk * TH + hf * 512:kk * TH + hf * 512 + 512],
                                start=(kk == 0), stop=(kk == 15)),
                                waits=(tU + w("p_u") if (kk == 0 and half == 0) else []),
                                incs=([("pe", 1)] if (kk == 15 and half == 1) else []))
                    if jb == 1:
                        setw(("bufG", par), tg)
                        setw(("bufU", par), tu)
                    tok_gl = D("scalar", lambda e: e.activation(gel[:], p_g[:], AF.Gelu_apprx_tanh),
                               waits=tg + w("gel"), incs=[("act", 1)])
                    setw("p_g", tok_gl)
                    jg = 2 * u4 + jb
                    tok_it = D("vector", lambda e, jg=jg: e.tensor_tensor(
                        intT[:, jg * TH:jg * TH + TH], gel[:], p_u[:], ALU.mult),
                        waits=tok_gl + tu + w(("intT", jg)), incs=[("dve", 1)])
                    setw("gel", tok_it)
                    setw("p_u", tok_it)
                    int_toks.append(tok_it[0])
            dn_toks = []
            for ht in range(16):
                for tp in range(2):
                    par2 = (ht * 2 + tp) % 2
                    pd = [p_dA, p_dB][par2]
                    pdkey = ("p_d", par2)
                    tdn = []
                    for jp in range(8):
                        u4p, pl = jp // 2, jp % 2
                        tdn = D("tensor", lambda e, pd=pd, u4p=u4p, pl=pl, ht=ht, tp=tp, jp=jp: e.matmul(
                            pd[:], bufD[u4p][:, pl * 2048 + ht * 128:pl * 2048 + ht * 128 + 128],
                            intT[:, (2 * u4p + pl) * TH + tp * 512:(2 * u4p + pl) * TH + tp * 512 + 512],
                            start=(jp == 0), stop=(jp == 7)),
                            waits=(int_toks + w(pdkey) + tD_ if jp == 0 else []),
                            incs=([("pe", 1)] if jp == 7 else []))
                    tok_ae = D("vector", lambda e, pd=pd, ht=ht, tp=tp: e.tensor_tensor(
                        accb[:, ht * TH + tp * 512:ht * TH + tp * 512 + 512],
                        accb[:, ht * TH + tp * 512:ht * TH + tp * 512 + 512], pd[:], ALU.add),
                        waits=tdn + tok_ab, incs=[("dve", 1)])
                    setw(pdkey, tok_ae)
                    dn_toks.append(tok_ae[0])
            setw("down_cons", [dn_toks[-1]])
            for jg in range(8):
                setw(("intT", jg), [dn_toks[-1]])
        D("sync", lambda e, t0=t0: e.dma_start(
            out=out_dst[:, :, t0:t0 + TH].rearrange("d p t -> p d t"),
            in_=accb[:].rearrange("p (d t) -> p d t", d=16)),
            waits=[dn_toks[-1]], incs=[("D_outs", 16)])
        prev = P.barrier()
    D("sync", lambda e: None, waits=P.dma_flush_waits())
    st.close()


_NC_CACHE = None


def kernel(**inputs):
    global _NC_CACHE
    x = np.asarray(inputs["L0_attn_hidden_states"], np.float32)
    shared = host_prep(inputs)
    if _NC_CACHE is None:
        _NC_CACHE = build()
    nc = _NC_CACHE
    in_maps = []
    for b in range(8):
        m = dict(shared)
        m["x"] = np.ascontiguousarray(x[b])
        m["xT"] = np.ascontiguousarray(x[b].T)
        in_maps.append(m)
    res = run_bass_kernel_spmd(nc, in_maps, core_ids=list(range(8)))
    out = np.empty((8, SEQ, DM), np.float32)
    for b in range(8):
        out[b] = np.asarray(res.results[b]["out"]).T
    return out


# version 06
# speedup vs baseline: 1.0361x; 1.0361x over previous
import os
import numpy as np
import ml_dtypes
import concourse.bass as bass
import concourse.mybir as mybir
from concourse.bass_utils import run_bass_kernel_spmd

F32 = mybir.dt.float32
F16 = mybir.dt.float16
BF16 = mybir.dt.bfloat16
I32 = mybir.dt.int32
ALU = mybir.AluOpType
AF = mybir.ActivationFunctionType
AXX = mybir.AxisListType.X
bfdt = ml_dtypes.bfloat16

DM, SEQ, DIN, DST, HEADS, PD, CHUNK = 2048, 2048, 4096, 128, 64, 64, 128
DIP, CONVD, MLP = 8512, 4352, 16384
SC, NSC = 256, 8
CPS = SC // CHUNK
TH = 1024
NEG = -1.0e30

DBG = []


class Prog:
    ENGS = ("sync", "scalar", "vector", "tensor", "gpsimd")

    def __init__(self, nc):
        self.nc = nc
        self.items = {e: [] for e in self.ENGS}
        self.sems = {}
        self.counts = {}
        for s in ("dma", "dmg", "pe", "dve", "act", "ph"):
            self.sems[s] = nc._stack.enter_context(nc.semaphore(s))
            self.counts[s] = 0

    def ensure(self, s):
        if s not in self.sems:
            self.sems[s] = self.nc._stack.enter_context(self.nc.semaphore(s))
            self.counts[s] = 0

    def do(self, eng, fn, waits=(), incs=()):
        waits = tuple(w for w in waits if w is not None)
        for s, v in waits:
            self.ensure(s)
            assert v <= self.counts[s], f"wait {s}>={v} not yet produced ({self.counts[s]})"
        toks = []
        for s, a in incs:
            self.ensure(s)
            self.counts[s] += a
            toks.append((s, self.counts[s]))
        self.items[eng].append((fn, waits, tuple(incs)))
        return toks

    def dma_flush_waits(self):
        return [(s, self.counts[s]) for s in self.sems
                if s.startswith("D_") and self.counts[s] > 0]

    def barrier(self):
        self.do("sync", None, waits=self.dma_flush_waits())
        base = self.counts["ph"]
        for e in self.ENGS:
            self.do(e, None, incs=[("ph", 1)])
        tgt = base + len(self.ENGS)
        for e in self.ENGS:
            self.do(e, None, waits=[("ph", tgt)])
        return [("ph", tgt)]

    def emit(self, block):
        for e in self.ENGS:
            items = self.items[e]
            if not items:
                continue

            def mk(items):
                def body(proxy):
                    for fn, waits, incs in items:
                        for s, v in waits:
                            proxy.wait_ge(self.sems[s], v)
                        inst = fn(proxy) if fn is not None else None
                        if inst is None:
                            for s, a in incs:
                                proxy.sem_inc(self.sems[s], a)
                        else:
                            for s, a in incs:
                                inst = inst.then_inc(self.sems[s], a)
                return body
            getattr(block, e)(mk(items))


def _perm(n):
    return np.concatenate([np.concatenate([np.arange(256 * i, 256 * i + 256, 2),
                                           np.arange(256 * i + 1, 256 * i + 256, 2)])
                           for i in range(n // 256)])


def host_prep(inp):
    f32 = np.float32
    d = {}
    dperm = _perm(DM)
    nperm = _perm(MLP)
    d["winT"] = np.ascontiguousarray(np.asarray(inp["L0_attn_W_in"], f32).T[dperm, :])
    d["woutT"] = np.ascontiguousarray(np.asarray(inp["L0_attn_W_out"], f32).T)
    wconv = np.asarray(inp["L0_attn_W_conv"], f32).reshape(CONVD, 4)
    d["wconv"] = np.ascontiguousarray(wconv.reshape(34, 128, 4).transpose(1, 0, 2)).reshape(128, 34 * 4)
    d["bconv"] = np.ascontiguousarray(np.asarray(inp["L0_attn_b_conv"], f32).reshape(34, 128).T)
    d["alog"] = np.ascontiguousarray(np.asarray(inp["L0_attn_A_log"], f32).reshape(HEADS, 1))
    d["dvec"] = np.ascontiguousarray(np.asarray(inp["L0_attn_D"], f32).reshape(1, HEADS))
    d["dtbias"] = np.ascontiguousarray(np.asarray(inp["L0_attn_dt_bias"], f32).reshape(HEADS, 1))
    d["nwc"] = np.ascontiguousarray(np.asarray(inp["L0_attn_norm_weight"], f32).reshape(32, 128).T)
    d["gwp"] = np.ascontiguousarray(np.asarray(inp["L0_ffn_gate_W_packed"])[nperm, :].T.astype(np.int32))
    d["uwp"] = np.ascontiguousarray(np.asarray(inp["L0_ffn_up_W_packed"])[nperm, :].T.astype(np.int32))
    d["gsc"] = np.ascontiguousarray(np.repeat(np.asarray(inp["L0_ffn_gate_scales"], f32)[nperm, :].T, 16, axis=0))
    d["usc"] = np.ascontiguousarray(np.repeat(np.asarray(inp["L0_ffn_up_scales"], f32)[nperm, :].T, 16, axis=0))
    d["dwp"] = np.ascontiguousarray(np.asarray(inp["L0_ffn_down_W_packed"]).T.astype(np.int32))
    d["dsc"] = np.ascontiguousarray(np.repeat(np.asarray(inp["L0_ffn_down_scales"], f32).T, 16, axis=0))
    d["eyeb"] = np.eye(128, dtype=bfdt)
    d["eyef"] = np.eye(128, dtype=np.float32)
    s, l = np.meshgrid(np.arange(128), np.arange(128), indexing="ij")
    d["maskb"] = np.where(s <= l, 0.0, NEG).astype(f32)
    return d


IN_SPECS = [
    ("x", [SEQ, DM], F32), ("xT", [DM, SEQ], F32),
    ("winT", [DM, DIP], F32), ("woutT", [DIN, DM], F32),
    ("wconv", [128, 34 * 4], F32), ("bconv", [128, 34], F32),
    ("alog", [HEADS, 1], F32), ("dvec", [1, HEADS], F32), ("dtbias", [HEADS, 1], F32),
    ("nwc", [128, 32], F32),
    ("gwp", [1024, MLP], I32), ("uwp", [1024, MLP], I32),
    ("gsc", [1024, MLP], F32), ("usc", [1024, MLP], F32),
    ("dwp", [MLP // 2, DM], I32), ("dsc", [MLP // 2, DM], F32),
    ("eyeb", [128, 128], BF16), ("eyef", [128, 128], F32),
    ("maskb", [128, 128], F32),
]

SCRATCH_SPECS = [
    ("xnt_d", [DM, SEQ], BF16), ("winT_bf", [DM, DIP], BF16),
    ("woutT_bf", [DIN, DM], BF16), ("acc_d", [DM, SEQ], F32),
    ("dacs_d", [HEADS, SC], F32),
    ("gdec", [DM, MLP], F16), ("udec", [DM, MLP], F16), ("ddec", [MLP, DM], F16),
    ("xn_d", [SEQ, DM], BF16), ("st_d", [128, 128], F32), ("y_d", [SEQ, DIN], F32),
]


class K:
    def __init__(self):
        from contextlib import ExitStack
        nc = bass.Bass("TRN2", target_bir_lowering=False)
        nc._stack = ExitStack()
        self.nc = nc
        self.A = {}
        for name, shape, dt in IN_SPECS:
            self.A[name] = nc.dram_tensor(name, shape, dt, kind="ExternalInput").ap()
        self.A["out"] = nc.dram_tensor("out", [DM, SEQ], F32, kind="ExternalOutput").ap()
        for name, shape, dt in SCRATCH_SPECS:
            kind = "ExternalOutput" if name in DBG else "Internal"
            self.A[name] = nc.dram_tensor(name, shape, dt, kind=kind).ap()
        self.P = Prog(nc)
        self.last = {}

    def T(self, n, s, d, stack=None):
        return (stack or self.nc._stack).enter_context(self.nc.sbuf_tensor(n, s, d))

    def PT(self, n, s, d, stack=None):
        return (stack or self.nc._stack).enter_context(self.nc.psum_tensor(n, s, d))

    def w(self, key):
        t = self.last.get(key)
        return [t] if t is not None else []

    def setw(self, key, toks):
        if toks:
            self.last[key] = toks[-1]


def build():
    k = K()
    nc, P, A, D = k.nc, k.P, k.A, k.P.do

    k.eye_b = k.T("c_eye_b", [128, 128], BF16)
    k.eye_f = k.T("c_eye_f", [128, 128], F32)
    k.mask = k.T("c_mask", [128, 128], F32)
    k.eps = k.T("c_eps", [128, 1], F32)
    k.one = k.T("c_one", [128, 1], F32)
    k.negtwo = k.T("c_negtwo", [128, 1], F32)
    k.negsix = k.T("c_negsix", [128, 1], F32)
    k.onesf = k.T("c_onesf", [1, 128], F32)
    for tile, src in [(k.eye_b, "eyeb"), (k.eye_f, "eyef"), (k.mask, "maskb")]:
        D("sync", lambda e, t=tile, s=src: e.dma_start(out=t[:], in_=A[s][:]), incs=[("D_cst", 16)])
    for tile, val in [(k.eps, 1.0e-5), (k.one, 1.0), (k.negtwo, -2.0), (k.negsix, -6.0)]:
        D("vector", lambda e, t=tile, v=val: e.memset(t[:], v), incs=[("dve", 1)])
    D("vector", lambda e: e.memset(k.onesf[:], 1.0), incs=[("dve", 1)])
    k.consts = [("D_cst", P.counts["D_cst"]), ("dve", P.counts["dve"])]

    lim = int(os.environ.get("PH_LIMIT", "3"))
    tok = phase_ln(k)
    if lim >= 1:
        tok = phase_wconv(k, tok)
    if lim >= 2:
        tok = phase_mamba(k, tok)
    if lim >= 3:
        phase_ffn(k, tok)
    else:
        phase_stub_out(k, tok)

    with nc.Block() as block:
        P.emit(block)
    return nc


def phase_stub_out(k, prev):
    from contextlib import ExitStack
    P, A, D = k.P, k.A, k.P.do
    st = ExitStack()
    z = k.T("stub_z", [128, SEQ], F32, st)
    tok = D("vector", lambda e: e.memset(z[:], 0.0), waits=prev, incs=[("dve", 1)])
    out_dst = A["out"].rearrange("(d p) t -> d p t", p=128)
    for i in range(16):
        tok = D("sync", lambda e, i=i: e.dma_start(out=out_dst[i], in_=z[:]),
                waits=tok, incs=[("D_outs", 16)])
    D("sync", lambda e: None, waits=P.dma_flush_waits())
    st.close()


def phase_ln(k):
    from contextlib import ExitStack
    P, A, D = k.P, k.A, k.P.do
    st = ExitStack()
    p0x = k.T("p0_x", [128, 2 * DM], F32, st)
    p0sq = k.T("p0_sq", [128, DM], F32, st)
    p0xn = k.T("p0_xn", [128, DM], BF16, st)
    p0st = k.T("p0_st", [128, 8], F32, st)
    p0tr = k.T("p0_tr", [128, 16 * 128], BF16, st)
    p0std = k.T("p0_std", [128, 128], F32, st)
    p0psA = k.PT("p0_psA", [128, 1024], BF16, st)
    p0psB = k.PT("p0_psB", [128, 1024], BF16, st)
    x_t = A["x"].rearrange("(a p) d -> a p d", p=128)
    xnt_dst = A["xnt_d"].rearrange("(c p) t -> c p t", p=128)
    prev_st = None
    prev_xn = None
    prev_ev = None
    for i in range(16):
        buf = p0x[:, (i % 2) * DM:(i % 2) * DM + DM]
        tok_ld = D("sync", lambda e, b=buf, i=i: e.dma_start(out=b, in_=x_t[i]),
                   waits=([prev_xn] if i >= 2 else []), incs=[(f"D_p0x{i % 2}", 16)])
        t_sum, t_sq, t_mean, t_var, t_rstd, t_m2 = (p0st[:, j:j + 1] for j in range(6))
        tok_su = D("vector", lambda e, b=buf: e.reduce_sum(t_sum, b, axis=AXX), waits=tok_ld,
                   incs=[("dve", 1)])
        tok_sq = D("scalar", lambda e, b=buf: e.activation(p0sq[:], b, AF.Square, accum_out=t_sq),
                   waits=tok_ld, incs=[("act", 1)])
        tok_mn = D("vector", lambda e: e.tensor_scalar(t_mean, t_sum, 1.0 / DM, None, ALU.mult),
                   waits=tok_sq + tok_su, incs=[("dve", 1)])
        tok_m2 = D("vector", lambda e: e.tensor_tensor(t_m2, t_mean, t_mean, ALU.mult),
                   waits=tok_mn, incs=[("dve", 1)])
        tok_va = D("vector", lambda e: e.tensor_scalar(t_var, t_sq, 1.0 / DM, None, ALU.mult),
                   incs=[("dve", 1)])
        tok_v = D("vector", lambda e: e.tensor_tensor(t_var, t_var, t_m2, ALU.subtract),
                  waits=tok_m2 + tok_va, incs=[("dve", 1)])
        tok_ln = D("scalar", lambda e: e.activation(t_rstd, t_var, AF.Ln, bias=k.eps[:]),
                   waits=tok_v + k.consts, incs=[("act", 1)])
        tok_r = D("scalar", lambda e: e.activation(t_rstd, t_rstd, AF.Exp, scale=-0.5),
                  waits=tok_ln, incs=[("act", 1)])
        tok_xn = D("vector", lambda e, b=buf: e.tensor_scalar(p0xn[:], b, t_mean, t_rstd,
                   ALU.subtract, ALU.mult), waits=tok_r + tok_mn + ([prev_st] if prev_st else []),
                   incs=[("dve", 1)])
        prev_xn = tok_xn[0]
        if os.environ.get("P0DBG") == "1":
            tok_sd = D("vector", lambda e, i=i: e.tensor_copy(p0std[:, i * 8:i * 8 + 8], p0st[:]),
                       waits=tok_xn, incs=[("dve", 1)])
            prev_xn = tok_sd[0]
            xn_dst = A["xn_d"].rearrange("(a p) d -> a p d", p=128)
            tok_xd = D("sync", lambda e, i=i: e.dma_start(out=xn_dst[i], in_=p0xn[:]),
                       waits=tok_xn, incs=[("D_dbg", 16)])
            prev_xn = tok_xd[0]
        evs = []
        for kk in range(16):
            c, pl = kk // 2, kk % 2
            src = p0xn[:, 256 * c + pl:256 * c + 256:2]
            gi = kk // 4
            pt_ = [p0psA, p0psB][gi % 2]
            pslot = pt_[:, (kk % 4) * 128:(kk % 4) * 128 + 128]
            w = tok_xn + k.consts
            if kk % 4 == 0 and gi >= 2:
                w = w + [evs[gi - 2]]
            elif kk % 4 == 0 and gi < 2 and prev_ev is not None:
                w = w + [prev_ev]
            tok_tr = D("tensor", lambda e, d_=pslot, s=src: e.transpose(d_, s, k.eye_b[:]),
                       waits=w, incs=[("pe", 1)])
            if kk % 4 == 3:
                tok_ev = D("scalar", lambda e, gi=gi, p_=pt_: e.copy(
                    p0tr[:, gi * 512:gi * 512 + 512], p_[:, 0:512]),
                    waits=tok_tr, incs=[("act", 1)])
                evs.append(tok_ev[0])
        prev_ev = evs[-1]
        tok_st = D("sync", lambda e, i=i: e.dma_start(
            out=xnt_dst[:, :, 128 * i:128 * i + 128].rearrange("c p t -> p c t"),
            in_=p0tr[:].rearrange("p (c t) -> p c t", c=16)),
            waits=[evs[-1]], incs=[("D_p0st", 16)])
        prev_st = tok_st[0]
    if os.environ.get("P0DBG") == "1":
        D("sync", lambda e: e.dma_start(out=A["st_d"][:], in_=p0std[:]),
          waits=[("dve", P.counts["dve"])], incs=[("D_dbg", 16)])
    tok = P.barrier()
    st.close()
    return tok


def phase_wconv(k, prev):
    from contextlib import ExitStack
    P, A, D = k.P, k.A, k.P.do
    st = ExitStack()
    p1f = k.T("p1_f", [128, DIP], F32, st)
    p1b = k.T("p1_b", [128, DIP], BF16, st)
    prevtok = None
    for srcT, dstT, wd, n in [(A["winT"], A["winT_bf"], DIP, 16), (A["woutT"], A["woutT_bf"], DM, 32)]:
        s_t = srcT.rearrange("(a p) f -> a p f", p=128)
        d_t = dstT.rearrange("(a p) f -> a p f", p=128)
        for i in range(n):
            tok_ld = D("sync", lambda e, i=i, s=s_t, wd=wd: e.dma_start(out=p1f[:, 0:wd], in_=s[i]),
                       waits=prev + ([prevtok] if prevtok else []), incs=[("D_p1l", 16)])
            tok_cv = D("scalar", lambda e, wd=wd: e.copy(p1b[:, 0:wd], p1f[:, 0:wd]),
                       waits=tok_ld, incs=[("act", 1)])
            tok_st = D("sync", lambda e, i=i, d=d_t, wd=wd: e.dma_start(out=d[i], in_=p1b[:, 0:wd]),
                       waits=tok_cv, incs=[("D_p1s", 16)])
            prevtok = tok_st[0]
    tok = P.barrier()
    st.close()
    return tok


def phase_mamba(k, prev):
    from contextlib import ExitStack
    P, A, D = k.P, k.A, k.P.do
    st = ExitStack()
    T = lambda n, s, d: k.T(n, s, d, st)
    PT = lambda n, s, d: k.PT(n, s, d, st)
    w, setw = k.w, k.setw

    hs = T("m_hs", [128, DIN], F32)
    hsb = T("m_hsb", [128, DIN], BF16)
    xbcs = T("m_xbcs", [128, 34 * SC], BF16)
    zst = T("m_zst", [128, 32 * SC], BF16)
    xnsl = T("m_xnsl", [128, 16 * SC], BF16)
    wsl = T("m_wsl", [128, 2 * 4096], BF16)
    flg = T("m_flg", [1, 2 * 1024], F32)
    dtp = T("m_dtp", [64, SC], F32)
    dts = T("m_dts", [64, SC], F32)
    dca = T("m_dca", [64, 2 * (128 + CHUNK)], F32)
    dcb = T("m_dcb", [64, 2 * (128 + CHUNK)], F32)
    mht = T("m_mht", [128, 2 * 1024], BF16)
    segs = T("m_segs", [128, 1024], F32)
    xch = T("m_xch", [128, DIN], BF16)
    ut = T("m_ut", [128, DIN], BF16)
    upt = T("m_upt", [128, DIN], BF16)
    yb = T("m_yb", [128, DIN], F32)
    ynb = T("m_ynb", [128, DIN], BF16)
    yntb = T("m_yntb", [128, 32 * SC], BF16)
    cbtm = T("m_cbtm", [128, 128], BF16)
    blt = T("m_blt", [128, 128], BF16)
    zs = T("m_zs", [128, DIN], BF16)
    sm = T("m_sm", [128, 512], F32)
    nwcol = T("m_nwcol", [128, 32], F32)
    cvs = T("m_cvs", [128, SC + 4], F32)
    cvs2 = T("m_cvs2", [128, SC], F32)
    halo = T("m_halo", [128, 34 * 3], F32)
    colw = T("m_colw", [128, 34 * 4], F32)
    colb = T("m_colb", [128, 34], F32)
    acol = T("m_acol", [64, 4], F32)
    accst = T("m_accst", [128, SC], F32)
    xts = T("m_xts", [128, 2 * SC], F32)
    rowsc = T("m_rowsc", [1, 128], F32)
    scr5 = T("m_scr5", [128, 512], F32)
    p_yA = PT("p_yA", [128, 512], F32)
    p_yB = PT("p_yB", [128, 512], F32)
    p_sgA = PT("p_sgA", [128, 1024], F32)
    p_sgB = PT("p_sgB", [128, 1024], F32)
    p_trA = PT("p_trA", [128, 1024], BF16)
    p_trB = PT("p_trB", [128, 1024], BF16)
    p_tr = [p_trA, p_trB]
    dAcsT = sm[:, 0:64]
    sdoT = sm[:, 64:128]
    decayT = sm[:, 128:192]
    dtT = sm[:, 192:256]
    cdb = sm[:, 256:320]
    Db = sm[:, 320:384]
    rms = sm[:, 384:385]
    rstd = sm[:, 385:386]

    D("sync", lambda e: e.dma_start(out=colw[:], in_=A["wconv"][:]), waits=prev, incs=[("D_minit", 16)])
    D("sync", lambda e: e.dma_start(out=colb[:], in_=A["bconv"][:]), incs=[("D_minit", 16)])
    D("sync", lambda e: e.dma_start(out=acol[:, 0:1], in_=A["alog"][:]), incs=[("D_minit", 16)])
    D("sync", lambda e: e.dma_start(out=acol[:, 1:2], in_=A["dtbias"][:]), incs=[("D_minit", 16)])
    D("sync", lambda e: e.dma_start(out=rowsc[:, 0:64], in_=A["dvec"][:]), incs=[("D_minit", 16)])
    D("sync", lambda e: e.dma_start(out=nwcol[:], in_=A["nwc"][:]), incs=[("D_minit", 16)])
    cdma = [("D_minit", P.counts["D_minit"])]
    D("vector", lambda e: e.memset(hs[:], 0.0), waits=prev, incs=[("dve", 1)])
    D("vector", lambda e: e.memset(halo[:], 0.0), incs=[("dve", 1)])
    D("vector", lambda e: e.memset(dca[:], 0.0), incs=[("dve", 1)])
    D("vector", lambda e: e.memset(dcb[:], 0.0), incs=[("dve", 1)])
    tok_a = D("scalar", lambda e: e.activation(acol[:, 2:3], acol[:, 0:1], AF.Exp), waits=cdma, incs=[("act", 1)])
    D("vector", lambda e: e.tensor_scalar(acol[:, 2:3], acol[:, 2:3], -1.0, None, ALU.mult),
      waits=tok_a, incs=[("dve", 1)])
    tok_mm = D("tensor", lambda e: e.matmul(p_yA[:, 0:64], k.onesf[:], rowsc[:, 0:64], start=True, stop=True),
               waits=cdma + k.consts, incs=[("pe", 1)])
    D("vector", lambda e: e.tensor_copy(Db, p_yA[:, 0:64]), waits=tok_mm, incs=[("dve", 1)])
    bar = P.barrier()

    xnt_src = A["xnt_d"].rearrange("(c p) t -> c p t", p=128)
    win_src = A["winT_bf"].rearrange("(c p) f -> c p f", p=128)
    wo_src = A["woutT_bf"].rearrange("(c p) d -> c p d", p=128)
    xT_src = A["xT"].rearrange("(d p) t -> d p t", p=128)
    acc_dst = A["acc_d"].rearrange("(d p) t -> d p t", p=128)

    ptr_key = [("ptr", 0), ("ptr", 1)]

    def transpose_seq(pairs, first_waits):
        evs = []
        n = len(pairs)
        for i in range(n):
            gi = i // 4
            pt = p_tr[gi % 2]
            slot = pt[:, (i % 4) * 128:(i % 4) * 128 + 128]
            wt = []
            if i == 0:
                wt = list(first_waits) + w(ptr_key[0]) + w(ptr_key[1])
            if i % 4 == 0 and gi >= 2:
                wt = wt + [evs[gi - 2]]
            src = pairs[i][0]
            tok_tr = D("tensor", lambda e, d_=slot, s=src: e.transpose(d_, s, k.eye_b[:]),
                       waits=wt, incs=[("pe", 1)])
            if i % 4 == 3 or i == n - 1:
                fn = pairs[i][1]
                tok_ev = fn(pt, gi, tok_tr)
                evs.append(tok_ev[0])
        if len(evs) >= 2:
            setw(ptr_key[(len(evs) - 2) % 2], [evs[-2]])
        setw(ptr_key[(len(evs) - 1) % 2], [evs[-1]])
        return evs

    for sc in range(NSC):
        t0 = sc * SC
        start_tok = bar if sc == 0 else []
        tok_xn = D("sync", lambda e, t0=t0: e.dma_start(
            out=xnsl[:].rearrange("p (c t) -> p c t", c=16),
            in_=xnt_src[:, :, t0:t0 + SC].rearrange("c p t -> p c t")),
            waits=start_tok + w("xnsl"), incs=[("D_xnsl", 16)])

        dtraw_tok = None
        tokm = []
        for fg in range(34):
            f0 = 256 * fg
            wdt = 256 if fg < 33 else 64
            sb = wsl[:, (fg % 2) * 4096:(fg % 2) * 4096 + 16 * wdt]
            tok_w = D("sync", lambda e, f0=f0, wdt=wdt, sb=sb: e.dma_start(
                out=sb.rearrange("p (c f) -> p c f", c=16),
                in_=win_src[:, :, f0:f0 + wdt].rearrange("c p f -> p c f")),
                waits=start_tok + w(("wsl", fg % 2)), incs=[(f"D_wsl{fg % 2}", 16)])
            for ft in range(2 if wdt == 256 else 1):
                fr = f0 + 128 * ft
                rows = min(128, DIP - fr)
                par = (fg * 2 + ft) % 2
                pt_ = [p_yA, p_yB][par]
                ps = pt_[0:rows, 0:SC]
                pkey = ("p_inp", par)
                for kk in range(16):
                    tokm = D("tensor", lambda e, ps=ps, sb=sb, kk=kk, ft=ft, wdt=wdt, rows=rows: e.matmul(
                        ps, sb[:, kk * wdt + 128 * ft: kk * wdt + 128 * ft + rows],
                        xnsl[:, kk * SC:(kk + 1) * SC],
                        start=(kk == 0), stop=(kk == 15)),
                        waits=(tok_xn + tok_w + w(pkey) + w(("p_y", par)) if kk == 0 else []),
                        incs=([("pe", 1)] if kk == 15 else []))
                ftile = fr // 128
                if fr + rows <= DIN:
                    tok_e = D("scalar", lambda e, ps=ps, ftile=ftile: e.activation(
                        zst[:, ftile * SC:(ftile + 1) * SC], ps, AF.Silu),
                        waits=tokm + w("zst_cons"), incs=[("act", 1)])
                    setw(pkey, tok_e)
                elif fr < DIN + CONVD:
                    ct = ftile - 32
                    tok_h = D("vector", lambda e, ct=ct: e.tensor_copy(cvs[:, 0:3], halo[:, ct * 3:ct * 3 + 3]),
                              waits=tokm + w("cvs") + w(("halo", ct)), incs=[("dve", 1)])
                    tok_c = D("vector", lambda e, ps=ps: e.tensor_copy(cvs[:, 3:3 + SC], ps), incs=[("dve", 1)])
                    setw(pkey, tok_c)
                    tok_h2 = D("vector", lambda e, ct=ct: e.tensor_copy(halo[:, ct * 3:ct * 3 + 3],
                               cvs[:, SC:SC + 3]), waits=tok_c, incs=[("dve", 1)])
                    setw(("halo", ct), tok_h2)
                    tok_t = D("vector", lambda e, ct=ct: e.tensor_scalar(
                        cvs2[:], cvs[:, 0:SC], colw[:, ct * 4:ct * 4 + 1], None, ALU.mult),
                        waits=w("cvs2") + tok_h + tok_h2)
                    for j in (1, 2, 3):
                        tok_t = D("vector", lambda e, ct=ct, j=j: e.scalar_tensor_tensor(
                            cvs2[:], cvs[:, j:j + SC], colw[:, ct * 4 + j:ct * 4 + j + 1],
                            cvs2[:], ALU.mult, ALU.add),
                            incs=([("dve", 1)] if j == 3 else []))
                    setw("cvs", tok_t)
                    tok_s = D("scalar", lambda e, ct=ct: e.activation(
                        xbcs[:, ct * SC:(ct + 1) * SC], cvs2[:], AF.Silu,
                        bias=colb[:, ct:ct + 1]), waits=tok_t, incs=[("act", 1)])
                    setw("cvs2", tok_s)
                else:
                    tok_c = D("vector", lambda e, ps=ps: e.tensor_copy(dtp[:], ps),
                              waits=tokm, incs=[("dve", 1)])
                    setw(pkey, tok_c)
                    dtraw_tok = tok_c
            setw(("wsl", fg % 2), tokm)
        inproj_act = [("act", P.counts["act"])]

        tok_b1 = D("vector", lambda e: e.tensor_scalar(dtp[:], dtp[:], acol[:, 1:2], None, ALU.add),
                   waits=dtraw_tok, incs=[("dve", 1)])
        tok = D("scalar", lambda e: e.activation(dts[:], dtp[:], AF.Abs), waits=tok_b1, incs=[("act", 1)])
        tok = D("scalar", lambda e: e.activation(dts[:], dts[:], AF.Exp, scale=-1.0), waits=tok,
                incs=[("act", 1)])
        tok = D("scalar", lambda e: e.activation(dts[:], dts[:], AF.Ln, bias=k.one[0:64, :]), waits=tok,
                incs=[("act", 1)])
        tok_mx = D("vector", lambda e: e.tensor_scalar(dtp[:], dtp[:], 0.0, None, ALU.max), waits=tok,
                   incs=[("dve", 1)])
        tok3 = D("vector", lambda e: e.tensor_tensor(dtp[:], dtp[:], dts[:], ALU.add), waits=tok_mx,
                 incs=[("dve", 1)])
        tok_da = tok3
        for ch in range(CPS):
            tok_da = D("vector", lambda e, ch=ch: e.tensor_scalar(
                dca[:, 256 * ch + 128:256 * ch + 256], dtp[:, ch * CHUNK:(ch + 1) * CHUNK],
                acol[:, 2:3], None, ALU.mult),
                waits=tok_da + (w("dacs_cons") + w("dacsb_done") if ch == 0 else []), incs=[("dve", 1)])
        tok_cs = tok_da
        for shp in range(7):
            s_ = 1 << shp
            sb_, db_ = (dca, dcb) if shp % 2 == 0 else (dcb, dca)
            for ch in range(CPS):
                b = 256 * ch + 128
                tok_cs = D("vector", lambda e, b=b, s_=s_, sb_=sb_, db_=db_: e.tensor_tensor(
                    db_[:, b:b + CHUNK], sb_[:, b:b + CHUNK], sb_[:, b - s_:b + CHUNK - s_], ALU.add),
                    waits=tok_cs, incs=[("dve", 1)])
        dacs_buf = dcb
        dacs_v = lambda ch: dacs_buf[:, 256 * ch + 128:256 * ch + 256]
        tok_bd = D("gpsimd", lambda e: e.dma_start(
            out=A["dacs_d"][:].rearrange("h (c l) -> h c l", c=2),
            in_=dacs_buf[:].rearrange("h (c q) -> h c q", c=2)[:, :, 128:256]),
            waits=tok_cs + w(("flg", 0)) + w(("flg", 1)), incs=[("D_dacsb", 16)])
        setw("dacsb_done", tok_bd)

        for ch in range(CPS):
            cb = ch * CHUNK
            t1 = D("tensor", lambda e, ch=ch: e.transpose(p_yA[:, 0:64], dacs_v(ch), k.eye_f[0:64, 0:64]),
                   waits=tok_cs + w(("p_y", 0)), incs=[("pe", 1)])
            t1e = D("vector", lambda e: e.tensor_copy(dAcsT, p_yA[:, 0:64]), waits=t1, incs=[("dve", 1)])
            t2 = D("tensor", lambda e, cb=cb: e.transpose(p_yB[:, 0:64], dtp[:, cb:cb + CHUNK], k.eye_f[0:64, 0:64]),
                   waits=tok3 + w(("p_y", 1)), incs=[("pe", 1)])
            t2e = D("vector", lambda e: e.tensor_copy(dtT, p_yB[:, 0:64]), waits=t2, incs=[("dve", 1)])
            tok_sdo = D("scalar", lambda e: e.activation(sdoT, dAcsT, AF.Exp), waits=t1e, incs=[("act", 1)])
            t3_ = D("tensor", lambda e, ch=ch: e.transpose(p_yA[0:1, 64:128],
                    dacs_v(ch)[:, CHUNK - 1:CHUNK], k.eye_f[0:64, 0:64]), waits=t1e, incs=[("pe", 1)])
            t3e = D("vector", lambda e: e.tensor_copy(rowsc[:, 64:128], p_yA[0:1, 64:128]),
                    waits=t3_, incs=[("dve", 1)])
            tok_cd = D("scalar", lambda e: e.activation(rowsc[:, 0:64], rowsc[:, 64:128], AF.Exp),
                       waits=t3e, incs=[("act", 1)])
            t4 = D("tensor", lambda e: e.matmul(p_yA[:, 64:128], k.onesf[:], rowsc[:, 0:64],
                   start=True, stop=True), waits=tok_cd, incs=[("pe", 1)])
            t4e = D("vector", lambda e: e.tensor_copy(cdb, p_yA[:, 64:128]), waits=t4, incs=[("dve", 1)])
            t5 = D("tensor", lambda e: e.matmul(p_yB[:, 64:128], k.onesf[:], rowsc[:, 64:128],
                   start=True, stop=True), waits=t3e + t2e, incs=[("pe", 1)])
            t5e = D("vector", lambda e: e.tensor_tensor(decayT, p_yB[:, 64:128], dAcsT, ALU.subtract),
                    waits=t5 + t1e, incs=[("dve", 1)])
            tok_dce = D("scalar", lambda e: e.activation(decayT, decayT, AF.Exp), waits=t5e, incs=[("act", 1)])
            t6 = D("tensor", lambda e, cb=cb: e.matmul(p_yA[:, 128:256],
                   xbcs[:, 32 * SC + cb:32 * SC + cb + CHUNK],
                   xbcs[:, 33 * SC + cb:33 * SC + cb + CHUNK], start=True, stop=True),
                   waits=inproj_act + t4e, incs=[("pe", 1)])
            t6e = D("vector", lambda e: e.tensor_copy(cbtm[:], p_yA[:, 128:256]), waits=t6, incs=[("dve", 1)])
            smalls = t6e + t4e + [t5e[0], tok_dce[0], t2e[0], tok_sdo[0]]
            setw(("p_y", 0), t6e)
            setw(("p_y", 1), t5e)

            def mkev_x(dst):
                def f(ptx, gi, tok_tr):
                    return D("scalar", lambda e, ptx=ptx, gi=gi: e.copy(dst[:, gi * 512:gi * 512 + 512],
                             ptx[:, 0:512]), waits=tok_tr, incs=[("act", 1)])
                return f
            prx = [(xbcs[:, i * SC + cb:i * SC + cb + CHUNK], mkev_x(xch)) for i in range(32)]
            evx = transpose_seq(prx, inproj_act + w("xch_cons"))
            prz = [(zst[:, i * SC + cb:i * SC + cb + CHUNK], mkev_x(zs)) for i in range(32)]
            evz = transpose_seq(prz, inproj_act + w("zs_cons"))
            wt = w(ptr_key[0]) + w(ptr_key[1])
            tok_bl = D("tensor", lambda e, cb=cb: e.transpose(p_trA[:, 0:128],
                       xbcs[:, 32 * SC + cb:32 * SC + cb + CHUNK], k.eye_b[:]), waits=wt, incs=[("pe", 1)])
            tok_ble = D("vector", lambda e: e.tensor_copy(blt[:], p_trA[:, 0:128]), waits=tok_bl, incs=[("dve", 1)])
            setw(ptr_key[0], tok_ble)

            tok_u = D("vector", lambda e: e.tensor_tensor(
                ut[:].rearrange("p (h q) -> p h q", h=64), xch[:].rearrange("p (h q) -> p h q", h=64),
                dtT.unsqueeze(2).broadcast_to([128, 64, 64]), ALU.mult),
                waits=[evx[-1]] + t2e + w("ut_cons"), incs=[("dve", 1)])
            tok_up = D("vector", lambda e: e.tensor_tensor(
                upt[:].rearrange("p (h q) -> p h q", h=64), ut[:].rearrange("p (h q) -> p h q", h=64),
                decayT.unsqueeze(2).broadcast_to([128, 64, 64]), ALU.mult),
                waits=tok_u + tok_dce + w("upt_cons"), incs=[("dve", 1)])
            tok_hsb = D("vector", lambda e: e.tensor_copy(hsb[:], hs[:]),
                        waits=w("hs_upd") + w("hsb_cons"), incs=[("dve", 1)])

            y_toks = []
            sg_tok = []
            for g in range(8):
                pg = [p_sgA, p_sgB][g % 2]
                py = [p_yA, p_yB][g % 2]
                sgkey = ("p_sg", g % 2)
                pykey = ("p_y", g % 2)
                fslot = flg[:, (g % 2) * 1024:(g % 2) * 1024 + 1024]
                tok_fl = D("gpsimd", lambda e, g=g, cb=cb, fs=fslot: e.dma_start(
                    out=fs.rearrange("a (h l) -> a h l", h=8),
                    in_=A["dacs_d"][g * 8:g * 8 + 8, cb:cb + CHUNK].unsqueeze(0)),
                    waits=tok_bd + w(("flg", g % 2)), incs=[(f"D_flg{g % 2}", 16)])
                for jj in range(2):
                    sg_tok = D("tensor", lambda e, pg=pg, jj=jj, fs=fslot: e.matmul(
                        pg[:, jj * 512:jj * 512 + 512], k.onesf[:],
                        fs[:, jj * 512:jj * 512 + 512],
                        start=True, stop=True),
                        waits=(tok_fl + w(sgkey) if jj == 0 else []),
                        incs=([("pe", 1)] if jj == 1 else []))
                tok_sm = D("vector", lambda e, pg=pg, g=g: e.tensor_tensor(
                    segs[:].rearrange("p (h l) -> p h l", h=8),
                    pg[:].rearrange("p (h l) -> p h l", h=8),
                    dAcsT[:, g * 8:g * 8 + 8].unsqueeze(2).broadcast_to([128, 8, 128]),
                    ALU.subtract), waits=sg_tok + t1e + w("segs"), incs=[("dve", 1)])
                tok_sm = D("vector", lambda e: e.tensor_tensor(
                    segs[:].rearrange("p (h l) -> p h l", h=8),
                    segs[:].rearrange("p (h l) -> p h l", h=8),
                    k.mask[:].unsqueeze(1).broadcast_to([128, 8, 128]),
                    ALU.add), incs=[("dve", 1)])
                setw(sgkey, tok_sm)
                setw(("flg", g % 2), sg_tok)
                mh = mht[:, (g % 2) * 1024:(g % 2) * 1024 + 1024]
                tok_ex = D("scalar", lambda e, mh=mh: e.activation(mh, segs[:], AF.Exp),
                           waits=tok_sm + w(("mht", g % 2)), incs=[("act", 1)])
                setw("segs", tok_ex)
                tok_mh = D("vector", lambda e, mh=mh: e.tensor_tensor(
                    mh.rearrange("p (h l) -> p h l", h=8), mh.rearrange("p (h l) -> p h l", h=8),
                    cbtm[:].unsqueeze(1).broadcast_to([128, 8, 128]), ALU.mult),
                    waits=tok_ex + t6e, incs=[("dve", 1)])
                tok_yo = D("tensor", lambda e, py=py, g=g, cb=cb: e.matmul(
                    py[:], xbcs[:, 33 * SC + cb:33 * SC + cb + CHUNK],
                    hsb[:, g * 512:g * 512 + 512], start=True, stop=True),
                    waits=tok_hsb + w(pykey), incs=[("pe", 1)])
                tok_sc = D("vector", lambda e, py=py, g=g: e.tensor_tensor(
                    py[:].rearrange("p (h q) -> p h q", h=8), py[:].rearrange("p (h q) -> p h q", h=8),
                    sdoT[:, g * 8:g * 8 + 8].unsqueeze(2).broadcast_to([128, 8, 64]), ALU.mult),
                    waits=tok_yo + tok_sdo, incs=[("dve", 1)])
                tok_yd = []
                for hh in range(8):
                    tok_yd = D("tensor", lambda e, py=py, mh=mh, g=g, hh=hh: e.matmul(
                        py[:, hh * 64:hh * 64 + 64], mh[:, hh * 128:hh * 128 + 128],
                        ut[:, g * 512 + hh * 64:g * 512 + hh * 64 + 64],
                        start=False, stop=(hh == 7)),
                        waits=(tok_sc + tok_mh + tok_u if hh == 0 else []),
                        incs=([("pe", 1)] if hh == 7 else []))
                setw(("mht", g % 2), tok_yd)
                tok_d2 = D("vector", lambda e, g=g: e.tensor_tensor(
                    scr5[:].rearrange("p (h q) -> p h q", h=8),
                    xch[:, g * 512:g * 512 + 512].rearrange("p (h q) -> p h q", h=8),
                    Db[:, g * 8:g * 8 + 8].unsqueeze(2).broadcast_to([128, 8, 64]), ALU.mult),
                    waits=[evx[-1]] + w("scr5"))
                tok_ye = D("vector", lambda e, py=py, g=g: e.tensor_tensor(
                    yb[:, g * 512:g * 512 + 512], py[:], scr5[:], ALU.add),
                    waits=tok_yd + w("yb_cons"), incs=[("dve", 1)])
                setw("scr5", tok_ye)
                setw(pykey, tok_ye)
                y_toks.append(tok_ye[0])
            setw("ut_cons", tok_yd)
            setw("xch_cons", [y_toks[-1]])

            st_toks = []
            for g in range(8):
                py = [p_yA, p_yB][g % 2]
                pykey = ("p_y", g % 2)
                tok_s1 = D("tensor", lambda e, py=py, g=g: e.matmul(
                    py[:], blt[:], upt[:, g * 512:g * 512 + 512], start=True, stop=True),
                    waits=tok_ble + tok_up + w(pykey), incs=[("pe", 1)])
                D("vector", lambda e, g=g: e.tensor_tensor(
                    hs[:, g * 512:g * 512 + 512].rearrange("p (h q) -> p h q", h=8),
                    hs[:, g * 512:g * 512 + 512].rearrange("p (h q) -> p h q", h=8),
                    cdb[:, g * 8:g * 8 + 8].unsqueeze(2).broadcast_to([128, 8, 64]), ALU.mult),
                    waits=t4e + tok_hsb)
                tok_h2 = D("vector", lambda e, py=py, g=g: e.tensor_tensor(
                    hs[:, g * 512:g * 512 + 512], hs[:, g * 512:g * 512 + 512], py[:], ALU.add),
                    waits=tok_s1, incs=[("dve", 1)])
                setw(pykey, tok_h2)
                st_toks.append(tok_h2[0])
            setw("hs_upd", [st_toks[-1]])
            setw("upt_cons", [st_toks[-1]])
            setw("hsb_cons", [tok_yo[0] if False else y_toks[-1]])

            if os.environ.get("P2DBG") == "1":
                yd_dst = A["y_d"].rearrange("(a p) f -> a p f", p=128)
                tok_yD = D("sync", lambda e, sc=sc, ch=ch: e.dma_start(
                    out=yd_dst[sc * CPS + ch], in_=yb[:]), waits=[y_toks[-1]], incs=[("D_dbg", 16)])
            tok_yg = D("vector", lambda e: e.tensor_tensor(yb[:], yb[:], zs[:], ALU.mult),
                       waits=[y_toks[-1], evz[-1]], incs=[("dve", 1)])
            setw("zs_cons", tok_yg)
            tok_ss = D("scalar", lambda e: e.activation(ynb[:], yb[:], AF.Square, accum_out=rms),
                       waits=tok_yg, incs=[("act", 1)])
            tok_l2 = D("scalar", lambda e: e.activation(rstd, rms, AF.Ln, scale=1.0 / DIN, bias=k.eps[:]),
              waits=tok_ss, incs=[("act", 1)])
            tok_r2 = D("scalar", lambda e: e.activation(rstd, rstd, AF.Exp, scale=-0.5), waits=tok_l2,
                       incs=[("act", 1)])
            tok_yn = D("vector", lambda e: e.tensor_scalar(ynb[:], yb[:], rstd, None, ALU.mult),
                       waits=tok_r2, incs=[("dve", 1)])
            setw("yb_cons", tok_yn)

            def mkev_y(cb):
                def f(ptx, gi, tok_tr):
                    toks = []
                    for q in range(4):
                        i = gi * 4 + q
                        toks = D("vector", lambda e, ptx=ptx, i=i, q=q, cb=cb: e.tensor_scalar(
                            yntb[:, i * SC + cb:i * SC + cb + CHUNK], ptx[:, q * 128:q * 128 + 128],
                            nwcol[:, i:i + 1], None, ALU.mult),
                            waits=(tok_tr + w("yntb_cons") if q == 0 else []),
                            incs=([("dve", 1)] if q == 3 else []))
                    return toks
                return f
            pry = [(ynb[:, i * 128:i * 128 + 128], mkev_y(cb)) for i in range(32)]
            evy = transpose_seq(pry, tok_yn)
            setw("zst_cons", [evz[-1]])
            setw("xbcs_cons", [y_toks[-1]])
            setw("dacs_cons", t3_)
            setw("dtp_cons", t2)
            last_ev = evy[-1]

        op_toks = []
        for d_ in range(16):
            sb = wsl[:, (d_ % 2) * 4096:(d_ % 2) * 4096 + 4096]
            tok_w = D("sync", lambda e, d_=d_, sb=sb: e.dma_start(
                out=sb.rearrange("p (c f) -> p c f", c=32),
                in_=wo_src[:, :, d_ * 128:d_ * 128 + 128].rearrange("c p f -> p c f")),
                waits=bar + w(("wsl", d_ % 2)), incs=[(f"D_wsl{d_ % 2}", 16)])
            xtb = xts[:, (d_ % 2) * SC:(d_ % 2) * SC + SC]
            tok_x = D("sync", lambda e, d_=d_, xtb=xtb, t0=t0: e.dma_start(
                out=xtb, in_=xT_src[d_][:, t0:t0 + SC]),
                waits=w(("xts", d_ % 2)), incs=[(f"D_xts{d_ % 2}", 16)])
            py = [p_yA, p_yB][d_ % 2]
            pykey = ("p_y", d_ % 2)
            for c2 in range(32):
                tokm = D("tensor", lambda e, py=py, sb=sb, c2=c2: e.matmul(
                    py[:, 0:SC], sb[:, c2 * 128:c2 * 128 + 128], yntb[:, c2 * SC:c2 * SC + SC],
                    start=(c2 == 0), stop=(c2 == 31)),
                    waits=(tok_w + [last_ev] + w(pykey) if c2 == 0 else []),
                    incs=([("pe", 1)] if c2 == 31 else []))
            setw(("wsl", d_ % 2), tokm)
            tok_ac = D("vector", lambda e, py=py, xtb=xtb: e.tensor_tensor(accst[:], py[:, 0:SC], xtb, ALU.add),
                       waits=tokm + tok_x + w("accst"), incs=[("dve", 1)])
            setw(pykey, tok_ac)
            setw(("xts", d_ % 2), tok_ac)
            tok_ad = D("sync", lambda e, d_=d_, t0=t0: e.dma_start(
                out=acc_dst[d_][:, t0:t0 + SC], in_=accst[:]), waits=tok_ac, incs=[("D_accst", 16)])
            setw("accst", tok_ad)
            op_toks.append(tok_ad[0])
        setw("yntb_cons", [op_toks[-1]])

    bar = P.barrier()
    st.close()
    return bar


def phase_ffn(k, prev):
    from contextlib import ExitStack
    P, A, D = k.P, k.A, k.P.do
    st = ExitStack()
    T = lambda n, s, d: k.T(n, s, d, st)
    PT = lambda n, s, d: k.PT(n, s, d, st)
    w, setw = k.w, k.setw

    xh = T("f_xh", [128, 16 * TH], F16)
    accb = T("f_accb", [128, 16 * TH], F32)
    intT = T("f_intT", [128, 8 * TH], F16)
    bufG = [T(f"f_bufG{i}", [128, 16 * 256], F16) for i in range(2)]
    bufU = [T(f"f_bufU{i}", [128, 16 * 256], F16) for i in range(2)]
    bufD = [T(f"f_bufD{i}", [128, 2 * 2048], F16) for i in range(4)]
    pw = T("f_pw", [128, 2048], I32)
    psc = T("f_psc", [128, 2048], F32)
    tA = T("f_tA", [128, 512], I32)
    tB = T("f_tB", [128, 512], I32)
    tC = T("f_tC", [128, 512], I32)
    br4 = T("f_br4", [128, 512], BF16)
    br6 = T("f_br6", [128, 512], BF16)
    bg = T("f_bg", [128, 512], BF16)
    bv = T("f_bv", [128, 512], BF16)
    gel = T("f_gel", [128, TH], F16)
    p_g = PT("f_pg", [128, 1024], F32)
    p_u = PT("f_pu", [128, 1024], F32)
    p_dA = PT("f_pdA", [128, 512], F32)
    p_dB = PT("f_pdB", [128, 512], F32)

    gwp_s = A["gwp"].rearrange("(c p) n -> c p n", p=128)
    uwp_s = A["uwp"].rearrange("(c p) n -> c p n", p=128)
    gsc_s = A["gsc"].rearrange("(c p) n -> c p n", p=128)
    usc_s = A["usc"].rearrange("(c p) n -> c p n", p=128)
    dwp_s = A["dwp"].rearrange("(j p) h -> j p h", p=128)
    dsc_s = A["dsc"].rearrange("(j p) h -> j p h", p=128)
    gdec_s = A["gdec"].rearrange("(c p) n -> c p n", p=128)
    udec_s = A["udec"].rearrange("(c p) n -> c p n", p=128)
    ddec_s = A["ddec"].rearrange("(j p) h -> j p h", p=128)
    xnt_src = A["xnt_d"].rearrange("(c p) t -> c p t", p=128)
    acc_src = A["acc_d"].rearrange("(d p) t -> d p t", p=128)
    out_dst = A["out"].rearrange("(d p) t -> d p t", p=128)

    def decode_unit(n0, wsrc, ssrc, dst, dstkey, kind, extraw):
        if kind == "gu":
            tok_w = D("sync", lambda e, n0=n0, ws=wsrc: e.dma_start(
                out=pw[:].rearrange("p (c n) -> p c n", c=8),
                in_=ws[:, :, n0:n0 + 256].rearrange("c p n -> p c n")),
                waits=extraw + w("pw"), incs=[("D_pw", 16)])
            tok_s = D("sync", lambda e, n0=n0, ss=ssrc: e.dma_start(
                out=psc[:].rearrange("p (c n) -> p c n", c=8),
                in_=ss[:, :, n0:n0 + 256].rearrange("c p n -> p c n")),
                waits=w("psc"), incs=[("D_psc", 16)])
        else:
            ju = n0 // 256
            tok_w = D("sync", lambda e, ju=ju, ws=wsrc: e.dma_start(out=pw[:], in_=ws[ju]),
                      waits=extraw + w("pw"), incs=[("D_pw", 16)])
            tok_s = D("sync", lambda e, ju=ju, ss=ssrc: e.dma_start(out=psc[:], in_=ss[ju]),
                      waits=w("psc"), incs=[("D_psc", 16)])
        t8 = []
        prev_act = k.last.get("dec_act")
        for ci in range(4):
            soff, wdt = ci * 512, 512
            srcp = pw[:, soff:soff + wdt]
            scl = psc[:, soff:soff + wdt]
            sclv = scl.rearrange("p (c n) -> p c n", n=256)
            a = tA[:]; b2 = tB[:]; c3 = tC[:]; bh = tB[:]
            r4_ = br4[:]; r6_ = br6[:]; g_ = bg[:]; v_ = bv[:]
            vv = v_.rearrange("p (c n) -> p c n", n=256)
            if kind == "gu":
                dall = dst.rearrange("p (c two n) -> p c two n", two=2, n=256)
                dl = dall[:, 2 * ci:2 * ci + 2, 0, :]
                dh = dall[:, 2 * ci:2 * ci + 2, 1, :]
            else:
                dl = dst[:, ci * 512:ci * 512 + 512].rearrange("p (c n) -> p c n", n=256)
                dh = dst[:, 2048 + ci * 512:2048 + ci * 512 + 512].rearrange("p (c n) -> p c n", n=256)
            for pl in range(2):
                wt = (tok_w + w("dtmp") if (ci == 0 and pl == 0) else [])
                if prev_act is not None:
                    wt = wt + [prev_act]
                    prev_act = None
                if pl == 0:
                    D("vector", lambda e, srcp=srcp, a=a: e.tensor_scalar(a, srcp, 7, None, ALU.bitwise_and),
                      waits=wt)
                    t2 = D("vector", lambda e, srcp=srcp, b2=b2: e.tensor_scalar(b2, srcp, 8, None,
                           ALU.bitwise_and), incs=[("dve", 1)])
                else:
                    tsh = D("vector", lambda e, srcp=srcp, c3=c3: e.tensor_scalar(c3, srcp, 4, None,
                            ALU.logical_shift_right), waits=wt,
                            incs=([("dve", 1)] if ci == 3 else []))
                    if ci == 3:
                        setw("pw", tsh)
                    D("vector", lambda e, c3=c3, bh=bh: e.tensor_scalar(bh, c3, 8, None, ALU.bitwise_and))
                    t2 = D("vector", lambda e, c3=c3: e.tensor_scalar(c3, c3, 7, None, ALU.bitwise_and),
                           incs=[("dve", 1)])
                am = a if pl == 0 else c3
                asn = b2 if pl == 0 else bh
                t4a = D("scalar", lambda e, am=am, r4_=r4_: e.activation(r4_, am, AF.Relu, bias=k.negtwo[:],
                        scale=0.5), waits=t2, incs=[("act", 1)])
                t4b = D("scalar", lambda e, am=am, r6_=r6_: e.activation(r6_, am, AF.Relu, bias=k.negsix[:]),
                        incs=[("act", 1)])
                prev_act = t4b[0]
                D("vector", lambda e, asn=asn, g_=g_: e.tensor_scalar(g_, asn, -0.25, 1.0, ALU.mult, ALU.add))
                D("vector", lambda e, am=am, r4_=r4_, v_=v_: e.scalar_tensor_tensor(
                    v_, am, 0.5, r4_, ALU.mult, ALU.add), waits=t4a + t4b)
                D("vector", lambda e, v_=v_, r6_=r6_: e.tensor_tensor(v_, v_, r6_, ALU.add))
                D("vector", lambda e, v_=v_, g_=g_: e.tensor_tensor(v_, v_, g_, ALU.mult))
                ds = dl if pl == 0 else dh
                t8 = D("vector", lambda e, vv=vv, sclv=sclv, ds=ds: e.tensor_tensor(ds, vv, sclv, ALU.mult),
                       waits=(tok_s + w(dstkey) if (ci == 0 and pl == 0) else []),
                       incs=[("dve", 1)])
        k.last["dec_act"] = prev_act
        setw("psc", t8)
        setw("dtmp", t8)
        return t8

    for H in range(2):
        t0 = H * TH
        tok_xh = D("sync", lambda e, t0=t0: e.dma_start(
            out=xh[:].bitcast(BF16).rearrange("p (c t) -> p c t", c=16),
            in_=xnt_src[:, :, t0:t0 + TH].rearrange("c p t -> p c t")),
            waits=prev, incs=[("D_xh", 16)])
        tok_xh = D("scalar", lambda e: e.copy(xh[:], xh[:].bitcast(BF16)),
                   waits=tok_xh, incs=[("act", 1)])
        tok_ab = D("sync", lambda e, t0=t0: e.dma_start(
            out=accb[:].rearrange("p (d t) -> p d t", d=16),
            in_=acc_src[:, :, t0:t0 + TH].rearrange("d p t -> p d t")),
            waits=prev, incs=[("D_accb", 16)])
        tD_ = []
        decode_both = os.environ.get("FFN_DECODE_BOTH") == "1"
        for sup in range(16):
            int_toks = []
            for u4 in range(4):
                n0 = 1024 * sup + 256 * u4
                ug = sup * 4 + u4
                par = ug % 2
                bG, bU = bufG[par], bufU[par]
                if H == 0 or decode_both:
                    tG = decode_unit(n0, gwp_s, gsc_s, bG, ("bufG", par), "gu",
                                     (prev if ug == 0 else []) + w(("bufG_st", par)))
                    tGs = D("sync", lambda e, n0=n0, bG=bG: e.dma_start(
                        out=gdec_s[:, :, n0:n0 + 256].rearrange("c p n -> p c n"),
                        in_=bG[:].rearrange("p (c n) -> p c n", c=16)), waits=tG, incs=[("D_gst", 16)])
                    setw(("bufG_st", par), tGs)
                    tU = decode_unit(n0, uwp_s, usc_s, bU, ("bufU", par), "gu", w(("bufU_st", par)))
                    tUs = D("sync", lambda e, n0=n0, bU=bU: e.dma_start(
                        out=udec_s[:, :, n0:n0 + 256].rearrange("c p n -> p c n"),
                        in_=bU[:].rearrange("p (c n) -> p c n", c=16)), waits=tU, incs=[("D_ust", 16)])
                    setw(("bufU_st", par), tUs)
                    tD_ = decode_unit(n0, dwp_s, dsc_s, bufD[u4], ("bufD", u4), "d",
                                      w("down_cons") + w(("bufD_st", u4)))
                    tDs = D("sync", lambda e, n0=n0, u4=u4: e.dma_start(
                        out=ddec_s[n0 // 128:n0 // 128 + 2].rearrange("j p h -> p j h"),
                        in_=bufD[u4][:].rearrange("p (j h) -> p j h", j=2)), waits=tD_, incs=[("D_dst", 16)])
                    setw(("bufD_st", u4), tDs)
                else:
                    tG = D("sync", lambda e, n0=n0, bG=bG: e.dma_start(
                        out=bG[:].rearrange("p (c n) -> p c n", c=16),
                        in_=gdec_s[:, :, n0:n0 + 256].rearrange("c p n -> p c n")),
                        waits=(prev if ug == 0 else []) + w(("bufG", par)), incs=[("D_gld", 16)])
                    tU = D("sync", lambda e, n0=n0, bU=bU: e.dma_start(
                        out=bU[:].rearrange("p (c n) -> p c n", c=16),
                        in_=udec_s[:, :, n0:n0 + 256].rearrange("c p n -> p c n")),
                        waits=w(("bufU", par)), incs=[("D_uld", 16)])
                    tD_ = D("sync", lambda e, n0=n0, u4=u4: e.dma_start(
                        out=bufD[u4][:].rearrange("p (j h) -> p j h", j=2),
                        in_=ddec_s[n0 // 128:n0 // 128 + 2].rearrange("j p h -> p j h")),
                        waits=w("down_cons") + w(("bufD_st", u4)), incs=[("D_dld", 16)])
                    setw(("bufD_st", u4), tD_)
                for jb in range(2):
                    tg = []
                    tu = []
                    for half in range(2):
                        for kk in range(16):
                            tg = D("tensor", lambda e, kk=kk, jb=jb, hf=half, bG=bG: e.matmul(
                                p_g[:, hf * 512:hf * 512 + 512],
                                bG[:, kk * 256 + jb * 128:kk * 256 + jb * 128 + 128],
                                xh[:, kk * TH + hf * 512:kk * TH + hf * 512 + 512],
                                start=(kk == 0), stop=(kk == 15)),
                                waits=(tG + tok_xh + w("p_g") if (kk == 0 and half == 0) else []),
                                incs=([("pe", 1)] if (kk == 15 and half == 1) else []))
                    for half in range(2):
                        for kk in range(16):
                            tu = D("tensor", lambda e, kk=kk, jb=jb, hf=half, bU=bU: e.matmul(
                                p_u[:, hf * 512:hf * 512 + 512],
                                bU[:, kk * 256 + jb * 128:kk * 256 + jb * 128 + 128],
                                xh[:, kk * TH + hf * 512:kk * TH + hf * 512 + 512],
                                start=(kk == 0), stop=(kk == 15)),
                                waits=(tU + w("p_u") if (kk == 0 and half == 0) else []),
                                incs=([("pe", 1)] if (kk == 15 and half == 1) else []))
                    if jb == 1:
                        setw(("bufG", par), tg)
                        setw(("bufU", par), tu)
                    tok_gl = D("scalar", lambda e: e.activation(gel[:], p_g[:], AF.Gelu_apprx_tanh),
                               waits=tg + w("gel"), incs=[("act", 1)])
                    setw("p_g", tok_gl)
                    jg = 2 * u4 + jb
                    tok_it = D("vector", lambda e, jg=jg: e.tensor_tensor(
                        intT[:, jg * TH:jg * TH + TH], gel[:], p_u[:], ALU.mult),
                        waits=tok_gl + tu + w(("intT", jg)), incs=[("dve", 1)])
                    setw("gel", tok_it)
                    setw("p_u", tok_it)
                    int_toks.append(tok_it[0])
            dn_toks = []
            for ht in range(16):
                for tp in range(2):
                    par2 = (ht * 2 + tp) % 2
                    pd = [p_dA, p_dB][par2]
                    pdkey = ("p_d", par2)
                    tdn = []
                    for jp in range(8):
                        u4p, pl = jp // 2, jp % 2
                        tdn = D("tensor", lambda e, pd=pd, u4p=u4p, pl=pl, ht=ht, tp=tp, jp=jp: e.matmul(
                            pd[:], bufD[u4p][:, pl * 2048 + ht * 128:pl * 2048 + ht * 128 + 128],
                            intT[:, (2 * u4p + pl) * TH + tp * 512:(2 * u4p + pl) * TH + tp * 512 + 512],
                            start=(jp == 0), stop=(jp == 7)),
                            waits=(int_toks + w(pdkey) + tD_ if jp == 0 else []),
                            incs=([("pe", 1)] if jp == 7 else []))
                    tok_ae = D("vector", lambda e, pd=pd, ht=ht, tp=tp: e.tensor_tensor(
                        accb[:, ht * TH + tp * 512:ht * TH + tp * 512 + 512],
                        accb[:, ht * TH + tp * 512:ht * TH + tp * 512 + 512], pd[:], ALU.add),
                        waits=tdn + tok_ab, incs=[("dve", 1)])
                    setw(pdkey, tok_ae)
                    dn_toks.append(tok_ae[0])
            setw("down_cons", [dn_toks[-1]])
            for jg in range(8):
                setw(("intT", jg), [dn_toks[-1]])
        D("sync", lambda e, t0=t0: e.dma_start(
            out=out_dst[:, :, t0:t0 + TH].rearrange("d p t -> p d t"),
            in_=accb[:].rearrange("p (d t) -> p d t", d=16)),
            waits=[dn_toks[-1]], incs=[("D_outs", 16)])
        prev = P.barrier()
    D("sync", lambda e: None, waits=P.dma_flush_waits())
    st.close()


_NC_CACHE = None


def kernel(**inputs):
    global _NC_CACHE
    x = np.asarray(inputs["L0_attn_hidden_states"], np.float32)
    shared = host_prep(inputs)
    if _NC_CACHE is None:
        _NC_CACHE = build()
    nc = _NC_CACHE
    in_maps = []
    for b in range(8):
        m = dict(shared)
        m["x"] = np.ascontiguousarray(x[b])
        m["xT"] = np.ascontiguousarray(x[b].T)
        in_maps.append(m)
    res = run_bass_kernel_spmd(nc, in_maps, core_ids=list(range(8)))
    out = np.empty((8, SEQ, DM), np.float32)
    for b in range(8):
        out[b] = np.asarray(res.results[b]["out"]).T
    return out


# version 08
# speedup vs baseline: 1.0578x; 1.0210x over previous
import os
import numpy as np
import ml_dtypes
import concourse.bass as bass
import concourse.mybir as mybir
from concourse.bass_utils import run_bass_kernel_spmd

F32 = mybir.dt.float32
F16 = mybir.dt.float16
BF16 = mybir.dt.bfloat16
I32 = mybir.dt.int32
ALU = mybir.AluOpType
AF = mybir.ActivationFunctionType
AXX = mybir.AxisListType.X
bfdt = ml_dtypes.bfloat16

DM, SEQ, DIN, DST, HEADS, PD, CHUNK = 2048, 2048, 4096, 128, 64, 64, 128
DIP, CONVD, MLP = 8512, 4352, 16384
SC, NSC = 256, 8
CPS = SC // CHUNK
TH = 1024
NEG = -1.0e30

DBG = []


class Prog:
    ENGS = ("sync", "scalar", "vector", "tensor", "gpsimd")

    def __init__(self, nc):
        self.nc = nc
        self.items = {e: [] for e in self.ENGS}
        self.sems = {}
        self.counts = {}
        for s in ("dma", "dmg", "pe", "dve", "act", "ph"):
            self.sems[s] = nc._stack.enter_context(nc.semaphore(s))
            self.counts[s] = 0

    def ensure(self, s):
        if s not in self.sems:
            self.sems[s] = self.nc._stack.enter_context(self.nc.semaphore(s))
            self.counts[s] = 0

    def do(self, eng, fn, waits=(), incs=()):
        waits = tuple(w for w in waits if w is not None)
        for s, v in waits:
            self.ensure(s)
            assert v <= self.counts[s], f"wait {s}>={v} not yet produced ({self.counts[s]})"
        toks = []
        for s, a in incs:
            self.ensure(s)
            self.counts[s] += a
            toks.append((s, self.counts[s]))
        self.items[eng].append((fn, waits, tuple(incs)))
        return toks

    def dma_flush_waits(self):
        return [(s, self.counts[s]) for s in self.sems
                if s.startswith("D_") and self.counts[s] > 0]

    def barrier(self):
        self.do("sync", None, waits=self.dma_flush_waits())
        base = self.counts["ph"]
        for e in self.ENGS:
            self.do(e, None, incs=[("ph", 1)])
        tgt = base + len(self.ENGS)
        for e in self.ENGS:
            self.do(e, None, waits=[("ph", tgt)])
        return [("ph", tgt)]

    def emit(self, block):
        for e in self.ENGS:
            items = self.items[e]
            if not items:
                continue

            def mk(items):
                def body(proxy):
                    for fn, waits, incs in items:
                        for s, v in waits:
                            proxy.wait_ge(self.sems[s], v)
                        inst = fn(proxy) if fn is not None else None
                        if inst is None:
                            for s, a in incs:
                                proxy.sem_inc(self.sems[s], a)
                        else:
                            for s, a in incs:
                                inst = inst.then_inc(self.sems[s], a)
                return body
            getattr(block, e)(mk(items))


def _perm(n):
    return np.concatenate([np.concatenate([np.arange(256 * i, 256 * i + 256, 2),
                                           np.arange(256 * i + 1, 256 * i + 256, 2)])
                           for i in range(n // 256)])


def host_prep(inp):
    f32 = np.float32
    d = {}
    dperm = _perm(DM)
    nperm = _perm(MLP)
    d["winT"] = np.ascontiguousarray(np.asarray(inp["L0_attn_W_in"], f32).T[dperm, :])
    d["woutT"] = np.ascontiguousarray(np.asarray(inp["L0_attn_W_out"], f32).T)
    wconv = np.asarray(inp["L0_attn_W_conv"], f32).reshape(CONVD, 4)
    d["wconv"] = np.ascontiguousarray(wconv.reshape(34, 128, 4).transpose(1, 0, 2)).reshape(128, 34 * 4)
    d["bconv"] = np.ascontiguousarray(np.asarray(inp["L0_attn_b_conv"], f32).reshape(34, 128).T)
    d["alog"] = np.ascontiguousarray(np.asarray(inp["L0_attn_A_log"], f32).reshape(HEADS, 1))
    d["dvec"] = np.ascontiguousarray(np.asarray(inp["L0_attn_D"], f32).reshape(1, HEADS))
    d["dtbias"] = np.ascontiguousarray(np.asarray(inp["L0_attn_dt_bias"], f32).reshape(HEADS, 1))
    d["nwc"] = np.ascontiguousarray(np.asarray(inp["L0_attn_norm_weight"], f32).reshape(32, 128).T)
    d["gwp"] = np.ascontiguousarray(np.asarray(inp["L0_ffn_gate_W_packed"])[nperm, :].T.astype(np.int32))
    d["uwp"] = np.ascontiguousarray(np.asarray(inp["L0_ffn_up_W_packed"])[nperm, :].T.astype(np.int32))
    d["gsc"] = np.ascontiguousarray(np.repeat(np.asarray(inp["L0_ffn_gate_scales"], f32)[nperm, :].T, 16, axis=0))
    d["usc"] = np.ascontiguousarray(np.repeat(np.asarray(inp["L0_ffn_up_scales"], f32)[nperm, :].T, 16, axis=0))
    d["dwp"] = np.ascontiguousarray(np.asarray(inp["L0_ffn_down_W_packed"]).T.astype(np.int32))
    d["dsc"] = np.ascontiguousarray(np.repeat(np.asarray(inp["L0_ffn_down_scales"], f32).T, 16, axis=0))
    d["eyeb"] = np.eye(128, dtype=bfdt)
    d["eyef"] = np.eye(128, dtype=np.float32)
    s, l = np.meshgrid(np.arange(128), np.arange(128), indexing="ij")
    d["maskb"] = np.where(s <= l, 0.0, NEG).astype(f32)
    return d


IN_SPECS = [
    ("x", [SEQ, DM], F32), ("xT", [DM, SEQ], F32),
    ("winT", [DM, DIP], F32), ("woutT", [DIN, DM], F32),
    ("wconv", [128, 34 * 4], F32), ("bconv", [128, 34], F32),
    ("alog", [HEADS, 1], F32), ("dvec", [1, HEADS], F32), ("dtbias", [HEADS, 1], F32),
    ("nwc", [128, 32], F32),
    ("gwp", [1024, MLP], I32), ("uwp", [1024, MLP], I32),
    ("gsc", [1024, MLP], F32), ("usc", [1024, MLP], F32),
    ("dwp", [MLP // 2, DM], I32), ("dsc", [MLP // 2, DM], F32),
    ("eyeb", [128, 128], BF16), ("eyef", [128, 128], F32),
    ("maskb", [128, 128], F32),
]

SCRATCH_SPECS = [
    ("xnt_d", [DM, SEQ], BF16), ("winT_bf", [DM, DIP], BF16),
    ("woutT_bf", [DIN, DM], BF16), ("acc_d", [DM, SEQ], F32),
    ("dacs_d", [HEADS, SC], F32),
    ("gdec", [DM, MLP], F16), ("udec", [DM, MLP], F16), ("ddec", [MLP, DM], F16),
    ("xn_d", [SEQ, DM], BF16), ("st_d", [128, 128], F32), ("y_d", [SEQ, DIN], F32),
]


class K:
    def __init__(self):
        from contextlib import ExitStack
        nc = bass.Bass("TRN2", target_bir_lowering=False)
        nc._stack = ExitStack()
        self.nc = nc
        self.A = {}
        for name, shape, dt in IN_SPECS:
            self.A[name] = nc.dram_tensor(name, shape, dt, kind="ExternalInput").ap()
        self.A["out"] = nc.dram_tensor("out", [DM, SEQ], F32, kind="ExternalOutput").ap()
        for name, shape, dt in SCRATCH_SPECS:
            kind = "ExternalOutput" if name in DBG else "Internal"
            self.A[name] = nc.dram_tensor(name, shape, dt, kind=kind).ap()
        self.P = Prog(nc)
        self.last = {}

    def T(self, n, s, d, stack=None):
        return (stack or self.nc._stack).enter_context(self.nc.sbuf_tensor(n, s, d))

    def PT(self, n, s, d, stack=None):
        return (stack or self.nc._stack).enter_context(self.nc.psum_tensor(n, s, d))

    def w(self, key):
        t = self.last.get(key)
        return [t] if t is not None else []

    def setw(self, key, toks):
        if toks:
            self.last[key] = toks[-1]


def build():
    k = K()
    nc, P, A, D = k.nc, k.P, k.A, k.P.do

    k.eye_b = k.T("c_eye_b", [128, 128], BF16)
    k.eye_f = k.T("c_eye_f", [128, 128], F32)
    k.mask = k.T("c_mask", [128, 128], F32)
    k.eps = k.T("c_eps", [128, 1], F32)
    k.one = k.T("c_one", [128, 1], F32)
    k.negtwo = k.T("c_negtwo", [128, 1], F32)
    k.negsix = k.T("c_negsix", [128, 1], F32)
    k.negfour = k.T("c_negfour", [128, 1], F32)
    k.onesf = k.T("c_onesf", [1, 128], F32)
    for tile, src in [(k.eye_b, "eyeb"), (k.eye_f, "eyef"), (k.mask, "maskb")]:
        D("sync", lambda e, t=tile, s=src: e.dma_start(out=t[:], in_=A[s][:]), incs=[("D_cst", 16)])
    for tile, val in [(k.eps, 1.0e-5), (k.one, 1.0), (k.negtwo, -2.0), (k.negsix, -6.0),
                      (k.negfour, -4.0)]:
        D("vector", lambda e, t=tile, v=val: e.memset(t[:], v), incs=[("dve", 1)])
    D("vector", lambda e: e.memset(k.onesf[:], 1.0), incs=[("dve", 1)])
    k.consts = [("D_cst", P.counts["D_cst"]), ("dve", P.counts["dve"])]

    lim = int(os.environ.get("PH_LIMIT", "3"))
    tok = phase_ln(k)
    if lim >= 1:
        tok = phase_wconv(k, tok)
    if lim >= 2:
        tok = phase_mamba(k, tok)
    if lim >= 3:
        phase_ffn(k, tok)
    else:
        phase_stub_out(k, tok)

    with nc.Block() as block:
        P.emit(block)
    return nc


def phase_stub_out(k, prev):
    from contextlib import ExitStack
    P, A, D = k.P, k.A, k.P.do
    st = ExitStack()
    z = k.T("stub_z", [128, SEQ], F32, st)
    tok = D("vector", lambda e: e.memset(z[:], 0.0), waits=prev, incs=[("dve", 1)])
    out_dst = A["out"].rearrange("(d p) t -> d p t", p=128)
    for i in range(16):
        tok = D("sync", lambda e, i=i: e.dma_start(out=out_dst[i], in_=z[:]),
                waits=tok, incs=[("D_outs", 16)])
    D("sync", lambda e: None, waits=P.dma_flush_waits())
    st.close()


def phase_ln(k):
    from contextlib import ExitStack
    P, A, D = k.P, k.A, k.P.do
    st = ExitStack()
    p0x = k.T("p0_x", [128, 2 * DM], F32, st)
    p0sq = k.T("p0_sq", [128, DM], F32, st)
    p0xn = k.T("p0_xn", [128, DM], BF16, st)
    p0st = k.T("p0_st", [128, 8], F32, st)
    p0tr = k.T("p0_tr", [128, 16 * 128], BF16, st)
    p0std = k.T("p0_std", [128, 128], F32, st)
    p0psA = k.PT("p0_psA", [128, 1024], BF16, st)
    p0psB = k.PT("p0_psB", [128, 1024], BF16, st)
    x_t = A["x"].rearrange("(a p) d -> a p d", p=128)
    xnt_dst = A["xnt_d"].rearrange("(c p) t -> c p t", p=128)
    prev_st = None
    prev_xn = None
    prev_ev = None
    for i in range(16):
        buf = p0x[:, (i % 2) * DM:(i % 2) * DM + DM]
        tok_ld = D("sync", lambda e, b=buf, i=i: e.dma_start(out=b, in_=x_t[i]),
                   waits=([prev_xn] if i >= 2 else []), incs=[(f"D_p0x{i % 2}", 16)])
        t_sum, t_sq, t_mean, t_var, t_rstd, t_m2 = (p0st[:, j:j + 1] for j in range(6))
        tok_su = D("vector", lambda e, b=buf: e.reduce_sum(t_sum, b, axis=AXX), waits=tok_ld,
                   incs=[("dve", 1)])
        tok_sq = D("scalar", lambda e, b=buf: e.activation(p0sq[:], b, AF.Square, accum_out=t_sq),
                   waits=tok_ld, incs=[("act", 1)])
        tok_mn = D("vector", lambda e: e.tensor_scalar(t_mean, t_sum, 1.0 / DM, None, ALU.mult),
                   waits=tok_sq + tok_su, incs=[("dve", 1)])
        tok_m2 = D("vector", lambda e: e.tensor_tensor(t_m2, t_mean, t_mean, ALU.mult),
                   waits=tok_mn, incs=[("dve", 1)])
        tok_va = D("vector", lambda e: e.tensor_scalar(t_var, t_sq, 1.0 / DM, None, ALU.mult),
                   incs=[("dve", 1)])
        tok_v = D("vector", lambda e: e.tensor_tensor(t_var, t_var, t_m2, ALU.subtract),
                  waits=tok_m2 + tok_va, incs=[("dve", 1)])
        tok_ln = D("scalar", lambda e: e.activation(t_rstd, t_var, AF.Ln, bias=k.eps[:]),
                   waits=tok_v + k.consts, incs=[("act", 1)])
        tok_r = D("scalar", lambda e: e.activation(t_rstd, t_rstd, AF.Exp, scale=-0.5),
                  waits=tok_ln, incs=[("act", 1)])
        tok_xn = D("vector", lambda e, b=buf: e.tensor_scalar(p0xn[:], b, t_mean, t_rstd,
                   ALU.subtract, ALU.mult), waits=tok_r + tok_mn + ([prev_st] if prev_st else []),
                   incs=[("dve", 1)])
        prev_xn = tok_xn[0]
        if os.environ.get("P0DBG") == "1":
            tok_sd = D("vector", lambda e, i=i: e.tensor_copy(p0std[:, i * 8:i * 8 + 8], p0st[:]),
                       waits=tok_xn, incs=[("dve", 1)])
            prev_xn = tok_sd[0]
            xn_dst = A["xn_d"].rearrange("(a p) d -> a p d", p=128)
            tok_xd = D("sync", lambda e, i=i: e.dma_start(out=xn_dst[i], in_=p0xn[:]),
                       waits=tok_xn, incs=[("D_dbg", 16)])
            prev_xn = tok_xd[0]
        evs = []
        for kk in range(16):
            c, pl = kk // 2, kk % 2
            src = p0xn[:, 256 * c + pl:256 * c + 256:2]
            gi = kk // 4
            pt_ = [p0psA, p0psB][gi % 2]
            pslot = pt_[:, (kk % 4) * 128:(kk % 4) * 128 + 128]
            w = tok_xn + k.consts
            if kk % 4 == 0 and gi >= 2:
                w = w + [evs[gi - 2]]
            elif kk % 4 == 0 and gi < 2 and prev_ev is not None:
                w = w + [prev_ev]
            tok_tr = D("tensor", lambda e, d_=pslot, s=src: e.transpose(d_, s, k.eye_b[:]),
                       waits=w, incs=[("pe", 1)])
            if kk % 4 == 3:
                tok_ev = D("scalar", lambda e, gi=gi, p_=pt_: e.copy(
                    p0tr[:, gi * 512:gi * 512 + 512], p_[:, 0:512]),
                    waits=tok_tr, incs=[("act", 1)])
                evs.append(tok_ev[0])
        prev_ev = evs[-1]
        tok_st = D("sync", lambda e, i=i: e.dma_start(
            out=xnt_dst[:, :, 128 * i:128 * i + 128].rearrange("c p t -> p c t"),
            in_=p0tr[:].rearrange("p (c t) -> p c t", c=16)),
            waits=[evs[-1]], incs=[("D_p0st", 16)])
        prev_st = tok_st[0]
    if os.environ.get("P0DBG") == "1":
        D("sync", lambda e: e.dma_start(out=A["st_d"][:], in_=p0std[:]),
          waits=[("dve", P.counts["dve"])], incs=[("D_dbg", 16)])
    tok = P.barrier()
    st.close()
    return tok


def phase_wconv(k, prev):
    from contextlib import ExitStack
    P, A, D = k.P, k.A, k.P.do
    st = ExitStack()
    p1f = k.T("p1_f", [128, DIP], F32, st)
    p1b = k.T("p1_b", [128, DIP], BF16, st)
    prevtok = None
    for srcT, dstT, wd, n in [(A["winT"], A["winT_bf"], DIP, 16), (A["woutT"], A["woutT_bf"], DM, 32)]:
        s_t = srcT.rearrange("(a p) f -> a p f", p=128)
        d_t = dstT.rearrange("(a p) f -> a p f", p=128)
        for i in range(n):
            tok_ld = D("sync", lambda e, i=i, s=s_t, wd=wd: e.dma_start(out=p1f[:, 0:wd], in_=s[i]),
                       waits=prev + ([prevtok] if prevtok else []), incs=[("D_p1l", 16)])
            tok_cv = D("scalar", lambda e, wd=wd: e.copy(p1b[:, 0:wd], p1f[:, 0:wd]),
                       waits=tok_ld, incs=[("act", 1)])
            tok_st = D("sync", lambda e, i=i, d=d_t, wd=wd: e.dma_start(out=d[i], in_=p1b[:, 0:wd]),
                       waits=tok_cv, incs=[("D_p1s", 16)])
            prevtok = tok_st[0]
    tok = P.barrier()
    st.close()
    return tok


def phase_mamba(k, prev):
    from contextlib import ExitStack
    P, A, D = k.P, k.A, k.P.do
    st = ExitStack()
    T = lambda n, s, d: k.T(n, s, d, st)
    PT = lambda n, s, d: k.PT(n, s, d, st)
    w, setw = k.w, k.setw

    hs = T("m_hs", [128, DIN], F32)
    hsb = T("m_hsb", [128, DIN], BF16)
    xbcs = T("m_xbcs", [128, 34 * SC], BF16)
    zst = T("m_zst", [128, 32 * SC], BF16)
    xnsl = T("m_xnsl", [128, 16 * SC], BF16)
    wsl = T("m_wsl", [128, 2 * 4096], BF16)
    flg = T("m_flg", [1, 2 * 1024], F32)
    dtp = T("m_dtp", [64, SC], F32)
    dts = T("m_dts", [64, SC], F32)
    dca = T("m_dca", [64, 2 * (128 + CHUNK)], F32)
    dcb = T("m_dcb", [64, 2 * (128 + CHUNK)], F32)
    mht = T("m_mht", [128, 2 * 1024], BF16)
    segs = T("m_segs", [128, 1024], F32)
    xch = T("m_xch", [128, DIN], BF16)
    ut = T("m_ut", [128, DIN], BF16)
    upt = T("m_upt", [128, DIN], BF16)
    yb = T("m_yb", [128, DIN], F32)
    ynb = T("m_ynb", [128, DIN], BF16)
    yntb = T("m_yntb", [128, 32 * SC], BF16)
    cbtm = T("m_cbtm", [128, 128], BF16)
    blt = T("m_blt", [128, 128], BF16)
    zs = T("m_zs", [128, DIN], BF16)
    sm = T("m_sm", [128, 512], F32)
    nwcol = T("m_nwcol", [128, 32], F32)
    cvs = T("m_cvs", [128, SC + 4], F32)
    cvs2 = T("m_cvs2", [128, SC], F32)
    halo = T("m_halo", [128, 34 * 3], F32)
    colw = T("m_colw", [128, 34 * 4], F32)
    colb = T("m_colb", [128, 34], F32)
    acol = T("m_acol", [64, 4], F32)
    accst = T("m_accst", [128, SC], F32)
    xts = T("m_xts", [128, 2 * SC], F32)
    rowsc = T("m_rowsc", [1, 128], F32)
    scr5 = T("m_scr5", [128, 512], F32)
    p_yA = PT("p_yA", [128, 512], F32)
    p_yB = PT("p_yB", [128, 512], F32)
    p_sgA = PT("p_sgA", [128, 1024], F32)
    p_sgB = PT("p_sgB", [128, 1024], F32)
    p_trA = PT("p_trA", [128, 1024], BF16)
    p_trB = PT("p_trB", [128, 1024], BF16)
    p_tr = [p_trA, p_trB]
    dAcsT = sm[:, 0:64]
    sdoT = sm[:, 64:128]
    decayT = sm[:, 128:192]
    dtT = sm[:, 192:256]
    cdb = sm[:, 256:320]
    Db = sm[:, 320:384]
    rms = sm[:, 384:385]
    rstd = sm[:, 385:386]

    D("sync", lambda e: e.dma_start(out=colw[:], in_=A["wconv"][:]), waits=prev, incs=[("D_minit", 16)])
    D("sync", lambda e: e.dma_start(out=colb[:], in_=A["bconv"][:]), incs=[("D_minit", 16)])
    D("sync", lambda e: e.dma_start(out=acol[:, 0:1], in_=A["alog"][:]), incs=[("D_minit", 16)])
    D("sync", lambda e: e.dma_start(out=acol[:, 1:2], in_=A["dtbias"][:]), incs=[("D_minit", 16)])
    D("sync", lambda e: e.dma_start(out=rowsc[:, 0:64], in_=A["dvec"][:]), incs=[("D_minit", 16)])
    D("sync", lambda e: e.dma_start(out=nwcol[:], in_=A["nwc"][:]), incs=[("D_minit", 16)])
    cdma = [("D_minit", P.counts["D_minit"])]
    D("vector", lambda e: e.memset(hs[:], 0.0), waits=prev, incs=[("dve", 1)])
    D("vector", lambda e: e.memset(halo[:], 0.0), incs=[("dve", 1)])
    D("vector", lambda e: e.memset(dca[:], 0.0), incs=[("dve", 1)])
    D("vector", lambda e: e.memset(dcb[:], 0.0), incs=[("dve", 1)])
    tok_a = D("scalar", lambda e: e.activation(acol[:, 2:3], acol[:, 0:1], AF.Exp), waits=cdma, incs=[("act", 1)])
    D("vector", lambda e: e.tensor_scalar(acol[:, 2:3], acol[:, 2:3], -1.0, None, ALU.mult),
      waits=tok_a, incs=[("dve", 1)])
    tok_mm = D("tensor", lambda e: e.matmul(p_yA[:, 0:64], k.onesf[:], rowsc[:, 0:64], start=True, stop=True),
               waits=cdma + k.consts, incs=[("pe", 1)])
    D("vector", lambda e: e.tensor_copy(Db, p_yA[:, 0:64]), waits=tok_mm, incs=[("dve", 1)])
    bar = P.barrier()

    xnt_src = A["xnt_d"].rearrange("(c p) t -> c p t", p=128)
    win_src = A["winT_bf"].rearrange("(c p) f -> c p f", p=128)
    wo_src = A["woutT_bf"].rearrange("(c p) d -> c p d", p=128)
    xT_src = A["xT"].rearrange("(d p) t -> d p t", p=128)
    acc_dst = A["acc_d"].rearrange("(d p) t -> d p t", p=128)

    ptr_key = [("ptr", 0), ("ptr", 1)]

    def transpose_seq(pairs, first_waits):
        evs = []
        n = len(pairs)
        for i in range(n):
            gi = i // 4
            pt = p_tr[gi % 2]
            slot = pt[:, (i % 4) * 128:(i % 4) * 128 + 128]
            wt = []
            if i == 0:
                wt = list(first_waits) + w(ptr_key[0]) + w(ptr_key[1])
            if i % 4 == 0 and gi >= 2:
                wt = wt + [evs[gi - 2]]
            src = pairs[i][0]
            tok_tr = D("tensor", lambda e, d_=slot, s=src: e.transpose(d_, s, k.eye_b[:]),
                       waits=wt, incs=[("pe", 1)])
            if i % 4 == 3 or i == n - 1:
                fn = pairs[i][1]
                tok_ev = fn(pt, gi, tok_tr)
                evs.append(tok_ev[0])
        if len(evs) >= 2:
            setw(ptr_key[(len(evs) - 2) % 2], [evs[-2]])
        setw(ptr_key[(len(evs) - 1) % 2], [evs[-1]])
        return evs

    for sc in range(NSC):
        t0 = sc * SC
        start_tok = bar if sc == 0 else []
        tok_xn = D("sync", lambda e, t0=t0: e.dma_start(
            out=xnsl[:].rearrange("p (c t) -> p c t", c=16),
            in_=xnt_src[:, :, t0:t0 + SC].rearrange("c p t -> p c t")),
            waits=start_tok + w("xnsl"), incs=[("D_xnsl", 16)])

        dtraw_tok = None
        tokm = []
        for fg in range(34):
            f0 = 256 * fg
            wdt = 256 if fg < 33 else 64
            sb = wsl[:, (fg % 2) * 4096:(fg % 2) * 4096 + 16 * wdt]
            tok_w = D("sync", lambda e, f0=f0, wdt=wdt, sb=sb: e.dma_start(
                out=sb.rearrange("p (c f) -> p c f", c=16),
                in_=win_src[:, :, f0:f0 + wdt].rearrange("c p f -> p c f")),
                waits=start_tok + w(("wsl", fg % 2)), incs=[(f"D_wsl{fg % 2}", 16)])
            for ft in range(2 if wdt == 256 else 1):
                fr = f0 + 128 * ft
                rows = min(128, DIP - fr)
                par = (fg * 2 + ft) % 2
                pt_ = [p_yA, p_yB][par]
                ps = pt_[0:rows, 0:SC]
                pkey = ("p_inp", par)
                for kk in range(16):
                    tokm = D("tensor", lambda e, ps=ps, sb=sb, kk=kk, ft=ft, wdt=wdt, rows=rows: e.matmul(
                        ps, sb[:, kk * wdt + 128 * ft: kk * wdt + 128 * ft + rows],
                        xnsl[:, kk * SC:(kk + 1) * SC],
                        start=(kk == 0), stop=(kk == 15)),
                        waits=(tok_xn + tok_w + w(pkey) + w(("p_y", par)) if kk == 0 else []),
                        incs=([("pe", 1)] if kk == 15 else []))
                ftile = fr // 128
                if fr + rows <= DIN:
                    tok_e = D("scalar", lambda e, ps=ps, ftile=ftile: e.activation(
                        zst[:, ftile * SC:(ftile + 1) * SC], ps, AF.Silu),
                        waits=tokm + w("zst_cons"), incs=[("act", 1)])
                    setw(pkey, tok_e)
                elif fr < DIN + CONVD:
                    ct = ftile - 32
                    tok_h = D("vector", lambda e, ct=ct: e.tensor_copy(cvs[:, 0:3], halo[:, ct * 3:ct * 3 + 3]),
                              waits=tokm + w("cvs") + w(("halo", ct)), incs=[("dve", 1)])
                    tok_c = D("vector", lambda e, ps=ps: e.tensor_copy(cvs[:, 3:3 + SC], ps), incs=[("dve", 1)])
                    setw(pkey, tok_c)
                    tok_h2 = D("vector", lambda e, ct=ct: e.tensor_copy(halo[:, ct * 3:ct * 3 + 3],
                               cvs[:, SC:SC + 3]), waits=tok_c, incs=[("dve", 1)])
                    setw(("halo", ct), tok_h2)
                    tok_t = D("vector", lambda e, ct=ct: e.tensor_scalar(
                        cvs2[:], cvs[:, 0:SC], colw[:, ct * 4:ct * 4 + 1], None, ALU.mult),
                        waits=w("cvs2") + tok_h + tok_h2)
                    for j in (1, 2, 3):
                        tok_t = D("vector", lambda e, ct=ct, j=j: e.scalar_tensor_tensor(
                            cvs2[:], cvs[:, j:j + SC], colw[:, ct * 4 + j:ct * 4 + j + 1],
                            cvs2[:], ALU.mult, ALU.add),
                            incs=([("dve", 1)] if j == 3 else []))
                    setw("cvs", tok_t)
                    tok_s = D("scalar", lambda e, ct=ct: e.activation(
                        xbcs[:, ct * SC:(ct + 1) * SC], cvs2[:], AF.Silu,
                        bias=colb[:, ct:ct + 1]), waits=tok_t, incs=[("act", 1)])
                    setw("cvs2", tok_s)
                else:
                    tok_c = D("vector", lambda e, ps=ps: e.tensor_copy(dtp[:], ps),
                              waits=tokm, incs=[("dve", 1)])
                    setw(pkey, tok_c)
                    dtraw_tok = tok_c
            setw(("wsl", fg % 2), tokm)
        inproj_act = [("act", P.counts["act"])]

        tok_b1 = D("vector", lambda e: e.tensor_scalar(dtp[:], dtp[:], acol[:, 1:2], None, ALU.add),
                   waits=dtraw_tok, incs=[("dve", 1)])
        tok = D("scalar", lambda e: e.activation(dts[:], dtp[:], AF.Abs), waits=tok_b1, incs=[("act", 1)])
        tok = D("scalar", lambda e: e.activation(dts[:], dts[:], AF.Exp, scale=-1.0), waits=tok,
                incs=[("act", 1)])
        tok = D("scalar", lambda e: e.activation(dts[:], dts[:], AF.Ln, bias=k.one[0:64, :]), waits=tok,
                incs=[("act", 1)])
        tok_mx = D("vector", lambda e: e.tensor_scalar(dtp[:], dtp[:], 0.0, None, ALU.max), waits=tok,
                   incs=[("dve", 1)])
        tok3 = D("vector", lambda e: e.tensor_tensor(dtp[:], dtp[:], dts[:], ALU.add), waits=tok_mx,
                 incs=[("dve", 1)])
        tok_da = tok3
        for ch in range(CPS):
            tok_da = D("vector", lambda e, ch=ch: e.tensor_scalar(
                dca[:, 256 * ch + 128:256 * ch + 256], dtp[:, ch * CHUNK:(ch + 1) * CHUNK],
                acol[:, 2:3], None, ALU.mult),
                waits=tok_da + (w("dacs_cons") + w("dacsb_done") if ch == 0 else []), incs=[("dve", 1)])
        tok_cs = tok_da
        for shp in range(7):
            s_ = 1 << shp
            sb_, db_ = (dca, dcb) if shp % 2 == 0 else (dcb, dca)
            for ch in range(CPS):
                b = 256 * ch + 128
                tok_cs = D("vector", lambda e, b=b, s_=s_, sb_=sb_, db_=db_: e.tensor_tensor(
                    db_[:, b:b + CHUNK], sb_[:, b:b + CHUNK], sb_[:, b - s_:b + CHUNK - s_], ALU.add),
                    waits=tok_cs, incs=[("dve", 1)])
        dacs_buf = dcb
        dacs_v = lambda ch: dacs_buf[:, 256 * ch + 128:256 * ch + 256]
        tok_bd = D("gpsimd", lambda e: e.dma_start(
            out=A["dacs_d"][:].rearrange("h (c l) -> h c l", c=2),
            in_=dacs_buf[:].rearrange("h (c q) -> h c q", c=2)[:, :, 128:256]),
            waits=tok_cs + w(("flg", 0)) + w(("flg", 1)), incs=[("D_dacsb", 16)])
        setw("dacsb_done", tok_bd)

        for ch in range(CPS):
            cb = ch * CHUNK
            t1 = D("tensor", lambda e, ch=ch: e.transpose(p_yA[:, 0:64], dacs_v(ch), k.eye_f[0:64, 0:64]),
                   waits=tok_cs + w(("p_y", 0)), incs=[("pe", 1)])
            t1e = D("vector", lambda e: e.tensor_copy(dAcsT, p_yA[:, 0:64]), waits=t1, incs=[("dve", 1)])
            t2 = D("tensor", lambda e, cb=cb: e.transpose(p_yB[:, 0:64], dtp[:, cb:cb + CHUNK], k.eye_f[0:64, 0:64]),
                   waits=tok3 + w(("p_y", 1)), incs=[("pe", 1)])
            t2e = D("vector", lambda e: e.tensor_copy(dtT, p_yB[:, 0:64]), waits=t2, incs=[("dve", 1)])
            tok_sdo = D("scalar", lambda e: e.activation(sdoT, dAcsT, AF.Exp), waits=t1e, incs=[("act", 1)])
            t3_ = D("tensor", lambda e, ch=ch: e.transpose(p_yA[0:1, 64:128],
                    dacs_v(ch)[:, CHUNK - 1:CHUNK], k.eye_f[0:64, 0:64]), waits=t1e, incs=[("pe", 1)])
            t3e = D("vector", lambda e: e.tensor_copy(rowsc[:, 64:128], p_yA[0:1, 64:128]),
                    waits=t3_, incs=[("dve", 1)])
            tok_cd = D("scalar", lambda e: e.activation(rowsc[:, 0:64], rowsc[:, 64:128], AF.Exp),
                       waits=t3e, incs=[("act", 1)])
            t4 = D("tensor", lambda e: e.matmul(p_yA[:, 64:128], k.onesf[:], rowsc[:, 0:64],
                   start=True, stop=True), waits=tok_cd, incs=[("pe", 1)])
            t4e = D("vector", lambda e: e.tensor_copy(cdb, p_yA[:, 64:128]), waits=t4, incs=[("dve", 1)])
            t5 = D("tensor", lambda e: e.matmul(p_yB[:, 64:128], k.onesf[:], rowsc[:, 64:128],
                   start=True, stop=True), waits=t3e + t2e, incs=[("pe", 1)])
            t5e = D("vector", lambda e: e.tensor_tensor(decayT, p_yB[:, 64:128], dAcsT, ALU.subtract),
                    waits=t5 + t1e, incs=[("dve", 1)])
            tok_dce = D("scalar", lambda e: e.activation(decayT, decayT, AF.Exp), waits=t5e, incs=[("act", 1)])
            t6 = D("tensor", lambda e, cb=cb: e.matmul(p_yA[:, 128:256],
                   xbcs[:, 32 * SC + cb:32 * SC + cb + CHUNK],
                   xbcs[:, 33 * SC + cb:33 * SC + cb + CHUNK], start=True, stop=True),
                   waits=inproj_act + t4e, incs=[("pe", 1)])
            t6e = D("vector", lambda e: e.tensor_copy(cbtm[:], p_yA[:, 128:256]), waits=t6, incs=[("dve", 1)])
            smalls = t6e + t4e + [t5e[0], tok_dce[0], t2e[0], tok_sdo[0]]
            setw(("p_y", 0), t6e)
            setw(("p_y", 1), t5e)

            def mkev_x(dst):
                def f(ptx, gi, tok_tr):
                    return D("scalar", lambda e, ptx=ptx, gi=gi: e.copy(dst[:, gi * 512:gi * 512 + 512],
                             ptx[:, 0:512]), waits=tok_tr, incs=[("act", 1)])
                return f
            prx = [(xbcs[:, i * SC + cb:i * SC + cb + CHUNK], mkev_x(xch)) for i in range(32)]
            evx = transpose_seq(prx, inproj_act + w("xch_cons"))
            prz = [(zst[:, i * SC + cb:i * SC + cb + CHUNK], mkev_x(zs)) for i in range(32)]
            evz = transpose_seq(prz, inproj_act + w("zs_cons"))
            wt = w(ptr_key[0]) + w(ptr_key[1])
            tok_bl = D("tensor", lambda e, cb=cb: e.transpose(p_trA[:, 0:128],
                       xbcs[:, 32 * SC + cb:32 * SC + cb + CHUNK], k.eye_b[:]), waits=wt, incs=[("pe", 1)])
            tok_ble = D("vector", lambda e: e.tensor_copy(blt[:], p_trA[:, 0:128]), waits=tok_bl, incs=[("dve", 1)])
            setw(ptr_key[0], tok_ble)

            tok_u = D("vector", lambda e: e.tensor_tensor(
                ut[:].rearrange("p (h q) -> p h q", h=64), xch[:].rearrange("p (h q) -> p h q", h=64),
                dtT.unsqueeze(2).broadcast_to([128, 64, 64]), ALU.mult),
                waits=[evx[-1]] + t2e + w("ut_cons"), incs=[("dve", 1)])
            tok_up = D("vector", lambda e: e.tensor_tensor(
                upt[:].rearrange("p (h q) -> p h q", h=64), ut[:].rearrange("p (h q) -> p h q", h=64),
                decayT.unsqueeze(2).broadcast_to([128, 64, 64]), ALU.mult),
                waits=tok_u + tok_dce + w("upt_cons"), incs=[("dve", 1)])
            tok_hsb = D("vector", lambda e: e.tensor_copy(hsb[:], hs[:]),
                        waits=w("hs_upd") + w("hsb_cons"), incs=[("dve", 1)])

            y_toks = []
            sg_tok = []
            for g in range(8):
                pg = [p_sgA, p_sgB][g % 2]
                py = [p_yA, p_yB][g % 2]
                sgkey = ("p_sg", g % 2)
                pykey = ("p_y", g % 2)
                fslot = flg[:, (g % 2) * 1024:(g % 2) * 1024 + 1024]
                tok_fl = D("gpsimd", lambda e, g=g, cb=cb, fs=fslot: e.dma_start(
                    out=fs.rearrange("a (h l) -> a h l", h=8),
                    in_=A["dacs_d"][g * 8:g * 8 + 8, cb:cb + CHUNK].unsqueeze(0)),
                    waits=tok_bd + w(("flg", g % 2)), incs=[(f"D_flg{g % 2}", 16)])
                for jj in range(2):
                    sg_tok = D("tensor", lambda e, pg=pg, jj=jj, fs=fslot: e.matmul(
                        pg[:, jj * 512:jj * 512 + 512], k.onesf[:],
                        fs[:, jj * 512:jj * 512 + 512],
                        start=True, stop=True),
                        waits=(tok_fl + w(sgkey) if jj == 0 else []),
                        incs=([("pe", 1)] if jj == 1 else []))
                tok_sm = D("vector", lambda e, pg=pg, g=g: e.tensor_tensor(
                    segs[:].rearrange("p (h l) -> p h l", h=8),
                    pg[:].rearrange("p (h l) -> p h l", h=8),
                    dAcsT[:, g * 8:g * 8 + 8].unsqueeze(2).broadcast_to([128, 8, 128]),
                    ALU.subtract), waits=sg_tok + t1e + w("segs"), incs=[("dve", 1)])
                tok_sm = D("vector", lambda e: e.tensor_tensor(
                    segs[:].rearrange("p (h l) -> p h l", h=8),
                    segs[:].rearrange("p (h l) -> p h l", h=8),
                    k.mask[:].unsqueeze(1).broadcast_to([128, 8, 128]),
                    ALU.add), incs=[("dve", 1)])
                setw(sgkey, tok_sm)
                setw(("flg", g % 2), sg_tok)
                mh = mht[:, (g % 2) * 1024:(g % 2) * 1024 + 1024]
                tok_ex = D("scalar", lambda e, mh=mh: e.activation(mh, segs[:], AF.Exp),
                           waits=tok_sm + w(("mht", g % 2)), incs=[("act", 1)])
                setw("segs", tok_ex)
                tok_mh = D("vector", lambda e, mh=mh: e.tensor_tensor(
                    mh.rearrange("p (h l) -> p h l", h=8), mh.rearrange("p (h l) -> p h l", h=8),
                    cbtm[:].unsqueeze(1).broadcast_to([128, 8, 128]), ALU.mult),
                    waits=tok_ex + t6e, incs=[("dve", 1)])
                tok_yo = D("tensor", lambda e, py=py, g=g, cb=cb: e.matmul(
                    py[:], xbcs[:, 33 * SC + cb:33 * SC + cb + CHUNK],
                    hsb[:, g * 512:g * 512 + 512], start=True, stop=True),
                    waits=tok_hsb + w(pykey), incs=[("pe", 1)])
                tok_sc = D("vector", lambda e, py=py, g=g: e.tensor_tensor(
                    py[:].rearrange("p (h q) -> p h q", h=8), py[:].rearrange("p (h q) -> p h q", h=8),
                    sdoT[:, g * 8:g * 8 + 8].unsqueeze(2).broadcast_to([128, 8, 64]), ALU.mult),
                    waits=tok_yo + tok_sdo, incs=[("dve", 1)])
                tok_yd = []
                for hh in range(8):
                    tok_yd = D("tensor", lambda e, py=py, mh=mh, g=g, hh=hh: e.matmul(
                        py[:, hh * 64:hh * 64 + 64], mh[:, hh * 128:hh * 128 + 128],
                        ut[:, g * 512 + hh * 64:g * 512 + hh * 64 + 64],
                        start=False, stop=(hh == 7)),
                        waits=(tok_sc + tok_mh + tok_u if hh == 0 else []),
                        incs=([("pe", 1)] if hh == 7 else []))
                setw(("mht", g % 2), tok_yd)
                tok_d2 = D("vector", lambda e, g=g: e.tensor_tensor(
                    scr5[:].rearrange("p (h q) -> p h q", h=8),
                    xch[:, g * 512:g * 512 + 512].rearrange("p (h q) -> p h q", h=8),
                    Db[:, g * 8:g * 8 + 8].unsqueeze(2).broadcast_to([128, 8, 64]), ALU.mult),
                    waits=[evx[-1]] + w("scr5"))
                tok_ye = D("vector", lambda e, py=py, g=g: e.tensor_tensor(
                    yb[:, g * 512:g * 512 + 512], py[:], scr5[:], ALU.add),
                    waits=tok_yd + w("yb_cons"), incs=[("dve", 1)])
                setw("scr5", tok_ye)
                setw(pykey, tok_ye)
                y_toks.append(tok_ye[0])
            setw("ut_cons", tok_yd)
            setw("xch_cons", [y_toks[-1]])

            st_toks = []
            for g in range(8):
                py = [p_yA, p_yB][g % 2]
                pykey = ("p_y", g % 2)
                tok_s1 = D("tensor", lambda e, py=py, g=g: e.matmul(
                    py[:], blt[:], upt[:, g * 512:g * 512 + 512], start=True, stop=True),
                    waits=tok_ble + tok_up + w(pykey), incs=[("pe", 1)])
                D("vector", lambda e, g=g: e.tensor_tensor(
                    hs[:, g * 512:g * 512 + 512].rearrange("p (h q) -> p h q", h=8),
                    hs[:, g * 512:g * 512 + 512].rearrange("p (h q) -> p h q", h=8),
                    cdb[:, g * 8:g * 8 + 8].unsqueeze(2).broadcast_to([128, 8, 64]), ALU.mult),
                    waits=t4e + tok_hsb)
                tok_h2 = D("vector", lambda e, py=py, g=g: e.tensor_tensor(
                    hs[:, g * 512:g * 512 + 512], hs[:, g * 512:g * 512 + 512], py[:], ALU.add),
                    waits=tok_s1, incs=[("dve", 1)])
                setw(pykey, tok_h2)
                st_toks.append(tok_h2[0])
            setw("hs_upd", [st_toks[-1]])
            setw("upt_cons", [st_toks[-1]])
            setw("hsb_cons", [tok_yo[0] if False else y_toks[-1]])

            if os.environ.get("P2DBG") == "1":
                yd_dst = A["y_d"].rearrange("(a p) f -> a p f", p=128)
                tok_yD = D("sync", lambda e, sc=sc, ch=ch: e.dma_start(
                    out=yd_dst[sc * CPS + ch], in_=yb[:]), waits=[y_toks[-1]], incs=[("D_dbg", 16)])
            tok_yg = D("vector", lambda e: e.tensor_tensor(yb[:], yb[:], zs[:], ALU.mult),
                       waits=[y_toks[-1], evz[-1]], incs=[("dve", 1)])
            setw("zs_cons", tok_yg)
            tok_ss = D("scalar", lambda e: e.activation(ynb[:], yb[:], AF.Square, accum_out=rms),
                       waits=tok_yg, incs=[("act", 1)])
            tok_l2 = D("scalar", lambda e: e.activation(rstd, rms, AF.Ln, scale=1.0 / DIN, bias=k.eps[:]),
              waits=tok_ss, incs=[("act", 1)])
            tok_r2 = D("scalar", lambda e: e.activation(rstd, rstd, AF.Exp, scale=-0.5), waits=tok_l2,
                       incs=[("act", 1)])
            tok_yn = D("vector", lambda e: e.tensor_scalar(ynb[:], yb[:], rstd, None, ALU.mult),
                       waits=tok_r2, incs=[("dve", 1)])
            setw("yb_cons", tok_yn)

            def mkev_y(cb):
                def f(ptx, gi, tok_tr):
                    toks = []
                    for q in range(4):
                        i = gi * 4 + q
                        toks = D("vector", lambda e, ptx=ptx, i=i, q=q, cb=cb: e.tensor_scalar(
                            yntb[:, i * SC + cb:i * SC + cb + CHUNK], ptx[:, q * 128:q * 128 + 128],
                            nwcol[:, i:i + 1], None, ALU.mult),
                            waits=(tok_tr + w("yntb_cons") if q == 0 else []),
                            incs=([("dve", 1)] if q == 3 else []))
                    return toks
                return f
            pry = [(ynb[:, i * 128:i * 128 + 128], mkev_y(cb)) for i in range(32)]
            evy = transpose_seq(pry, tok_yn)
            setw("zst_cons", [evz[-1]])
            setw("xbcs_cons", [y_toks[-1]])
            setw("dacs_cons", t3_)
            setw("dtp_cons", t2)
            last_ev = evy[-1]

        op_toks = []
        for d_ in range(16):
            sb = wsl[:, (d_ % 2) * 4096:(d_ % 2) * 4096 + 4096]
            tok_w = D("sync", lambda e, d_=d_, sb=sb: e.dma_start(
                out=sb.rearrange("p (c f) -> p c f", c=32),
                in_=wo_src[:, :, d_ * 128:d_ * 128 + 128].rearrange("c p f -> p c f")),
                waits=bar + w(("wsl", d_ % 2)), incs=[(f"D_wsl{d_ % 2}", 16)])
            xtb = xts[:, (d_ % 2) * SC:(d_ % 2) * SC + SC]
            tok_x = D("sync", lambda e, d_=d_, xtb=xtb, t0=t0: e.dma_start(
                out=xtb, in_=xT_src[d_][:, t0:t0 + SC]),
                waits=w(("xts", d_ % 2)), incs=[(f"D_xts{d_ % 2}", 16)])
            py = [p_yA, p_yB][d_ % 2]
            pykey = ("p_y", d_ % 2)
            for c2 in range(32):
                tokm = D("tensor", lambda e, py=py, sb=sb, c2=c2: e.matmul(
                    py[:, 0:SC], sb[:, c2 * 128:c2 * 128 + 128], yntb[:, c2 * SC:c2 * SC + SC],
                    start=(c2 == 0), stop=(c2 == 31)),
                    waits=(tok_w + [last_ev] + w(pykey) if c2 == 0 else []),
                    incs=([("pe", 1)] if c2 == 31 else []))
            setw(("wsl", d_ % 2), tokm)
            tok_ac = D("vector", lambda e, py=py, xtb=xtb: e.tensor_tensor(accst[:], py[:, 0:SC], xtb, ALU.add),
                       waits=tokm + tok_x + w("accst"), incs=[("dve", 1)])
            setw(pykey, tok_ac)
            setw(("xts", d_ % 2), tok_ac)
            tok_ad = D("sync", lambda e, d_=d_, t0=t0: e.dma_start(
                out=acc_dst[d_][:, t0:t0 + SC], in_=accst[:]), waits=tok_ac, incs=[("D_accst", 16)])
            setw("accst", tok_ad)
            op_toks.append(tok_ad[0])
        setw("yntb_cons", [op_toks[-1]])

    bar = P.barrier()
    st.close()
    return bar


def phase_ffn(k, prev):
    from contextlib import ExitStack
    P, A, D = k.P, k.A, k.P.do
    st = ExitStack()
    T = lambda n, s, d: k.T(n, s, d, st)
    PT = lambda n, s, d: k.PT(n, s, d, st)
    w, setw = k.w, k.setw

    xh = T("f_xh", [128, 16 * TH], F16)
    accb = T("f_accb", [128, 16 * TH], F32)
    intT = T("f_intT", [128, 8 * TH], F16)
    bufG = [T(f"f_bufG{i}", [128, 16 * 256], F16) for i in range(2)]
    bufU = [T(f"f_bufU{i}", [128, 16 * 256], F16) for i in range(2)]
    bufD = [T(f"f_bufD{i}", [128, 2 * 2048], F16) for i in range(4)]
    pw = T("f_pw", [128, 2048], I32)
    psc = T("f_psc", [128, 2048], F32)
    tA = T("f_tA", [128, 512], I32)
    tB = T("f_tB", [128, 512], I32)
    tC = T("f_tC", [128, 512], I32)
    br4 = T("f_br4", [128, 512], BF16)
    br6 = T("f_br6", [128, 512], BF16)
    bg = T("f_bg", [128, 512], BF16)
    bv = T("f_bv", [128, 512], BF16)
    gel = T("f_gel", [128, TH], F16)
    p_g = PT("f_pg", [128, 1024], F32)
    p_u = PT("f_pu", [128, 1024], F32)
    p_dA = PT("f_pdA", [128, 512], F32)
    p_dB = PT("f_pdB", [128, 512], F32)

    gwp_s = A["gwp"].rearrange("(c p) n -> c p n", p=128)
    uwp_s = A["uwp"].rearrange("(c p) n -> c p n", p=128)
    gsc_s = A["gsc"].rearrange("(c p) n -> c p n", p=128)
    usc_s = A["usc"].rearrange("(c p) n -> c p n", p=128)
    dwp_s = A["dwp"].rearrange("(j p) h -> j p h", p=128)
    dsc_s = A["dsc"].rearrange("(j p) h -> j p h", p=128)
    gdec_s = A["gdec"].rearrange("(c p) n -> c p n", p=128)
    udec_s = A["udec"].rearrange("(c p) n -> c p n", p=128)
    ddec_s = A["ddec"].rearrange("(j p) h -> j p h", p=128)
    xnt_src = A["xnt_d"].rearrange("(c p) t -> c p t", p=128)
    acc_src = A["acc_d"].rearrange("(d p) t -> d p t", p=128)
    out_dst = A["out"].rearrange("(d p) t -> d p t", p=128)

    def decode_unit(n0, wsrc, ssrc, dst, dstkey, kind, extraw):
        if kind == "gu":
            tok_w = D("sync", lambda e, n0=n0, ws=wsrc: e.dma_start(
                out=pw[:].rearrange("p (c n) -> p c n", c=8),
                in_=ws[:, :, n0:n0 + 256].rearrange("c p n -> p c n")),
                waits=extraw + w("pw"), incs=[("D_pw", 16)])
            tok_s = D("sync", lambda e, n0=n0, ss=ssrc: e.dma_start(
                out=psc[:].rearrange("p (c n) -> p c n", c=8),
                in_=ss[:, :, n0:n0 + 256].rearrange("c p n -> p c n")),
                waits=w("psc"), incs=[("D_psc", 16)])
        else:
            ju = n0 // 256
            tok_w = D("sync", lambda e, ju=ju, ws=wsrc: e.dma_start(out=pw[:], in_=ws[ju]),
                      waits=extraw + w("pw"), incs=[("D_pw", 16)])
            tok_s = D("sync", lambda e, ju=ju, ss=ssrc: e.dma_start(out=psc[:], in_=ss[ju]),
                      waits=w("psc"), incs=[("D_psc", 16)])
        t8 = []
        for ci in range(4):
            soff, wdt = ci * 512, 512
            srcp = pw[:, soff:soff + wdt]
            scl = psc[:, soff:soff + wdt]
            sclv = scl.rearrange("p (c n) -> p c n", n=256)
            a = tA[:]; b2 = tB[:]; c3 = tC[:]; bh = tB[:]
            r4_ = br4[:]; r6_ = br6[:]; g_ = bg[:]; v_ = bv[:]
            vv = v_.rearrange("p (c n) -> p c n", n=256)
            if kind == "gu":
                dall = dst.rearrange("p (c two n) -> p c two n", two=2, n=256)
                dl = dall[:, 2 * ci:2 * ci + 2, 0, :]
                dh = dall[:, 2 * ci:2 * ci + 2, 1, :]
            else:
                dl = dst[:, ci * 512:ci * 512 + 512].rearrange("p (c n) -> p c n", n=256)
                dh = dst[:, 2048 + ci * 512:2048 + ci * 512 + 512].rearrange("p (c n) -> p c n", n=256)
            for pl in range(2):
                wt = (tok_w + w("dtmp") if (ci == 0 and pl == 0) else [])
                if pl == 0:
                    D("vector", lambda e, srcp=srcp, a=a: e.tensor_scalar(a, srcp, 7, None, ALU.bitwise_and),
                      waits=wt)
                    t2 = D("vector", lambda e, srcp=srcp, b2=b2: e.tensor_scalar(b2, srcp, 8, None,
                           ALU.bitwise_and), incs=[("dve", 1)])
                else:
                    tsh = D("vector", lambda e, srcp=srcp, c3=c3: e.tensor_scalar(c3, srcp, 4, None,
                            ALU.logical_shift_right), waits=wt,
                            incs=([("dve", 1)] if ci == 3 else []))
                    if ci == 3:
                        setw("pw", tsh)
                    D("vector", lambda e, c3=c3, bh=bh: e.tensor_scalar(bh, c3, 8, None, ALU.bitwise_and))
                    t2 = D("vector", lambda e, c3=c3: e.tensor_scalar(c3, c3, 7, None, ALU.bitwise_and),
                           incs=[("dve", 1)])
                am = a if pl == 0 else c3
                asn = b2 if pl == 0 else bh
                aw = k.last.pop(("dec_act", pl), None)
                ty1 = D("scalar", lambda e, am=am, r4_=r4_: e.activation(r4_, am, AF.Prelu, bias=k.negfour[:],
                        alpha=0.5), waits=t2 + ([aw] if aw else []), incs=[("act", 1)])
                ty2 = D("scalar", lambda e, am=am, r6_=r6_: e.activation(r6_, am, AF.Relu, bias=k.negsix[:]),
                        incs=[("act", 1)])
                k.last[("dec_act", pl)] = ty2[0]
                D("vector", lambda e, asn=asn, g_=g_: e.tensor_scalar(g_, asn, -0.25, 1.0, ALU.mult, ALU.add))
                D("vector", lambda e, r4_=r4_, r6_=r6_, v_=v_: e.scalar_tensor_tensor(
                    v_, r4_, 2.0, r6_, ALU.add, ALU.add), waits=ty1 + ty2)
                D("vector", lambda e, v_=v_, g_=g_: e.tensor_tensor(v_, v_, g_, ALU.mult))
                ds = dl if pl == 0 else dh
                t8 = D("vector", lambda e, vv=vv, sclv=sclv, ds=ds: e.tensor_tensor(ds, vv, sclv, ALU.mult),
                       waits=(tok_s + w(dstkey) if (ci == 0 and pl == 0) else []),
                       incs=[("dve", 1)])
        setw("psc", t8)
        setw("dtmp", t8)
        return t8

    for H in range(2):
        t0 = H * TH
        tok_xh = D("sync", lambda e, t0=t0: e.dma_start(
            out=xh[:].bitcast(BF16).rearrange("p (c t) -> p c t", c=16),
            in_=xnt_src[:, :, t0:t0 + TH].rearrange("c p t -> p c t")),
            waits=prev, incs=[("D_xh", 16)])
        tok_xh = D("scalar", lambda e: e.copy(xh[:], xh[:].bitcast(BF16)),
                   waits=tok_xh, incs=[("act", 1)])
        tok_ab = D("sync", lambda e, t0=t0: e.dma_start(
            out=accb[:].rearrange("p (d t) -> p d t", d=16),
            in_=acc_src[:, :, t0:t0 + TH].rearrange("d p t -> p d t")),
            waits=prev, incs=[("D_accb", 16)])
        tD_ = []
        decode_both = os.environ.get("FFN_DECODE_BOTH") == "1"
        for sup in range(16):
            int_toks = []
            for u4 in range(4):
                n0 = 1024 * sup + 256 * u4
                ug = sup * 4 + u4
                par = ug % 2
                bG, bU = bufG[par], bufU[par]
                if H == 0 or decode_both:
                    tG = decode_unit(n0, gwp_s, gsc_s, bG, ("bufG", par), "gu",
                                     (prev if ug == 0 else []) + w(("bufG_st", par)))
                    tGs = D("sync", lambda e, n0=n0, bG=bG: e.dma_start(
                        out=gdec_s[:, :, n0:n0 + 256].rearrange("c p n -> p c n"),
                        in_=bG[:].rearrange("p (c n) -> p c n", c=16)), waits=tG, incs=[("D_gst", 16)])
                    setw(("bufG_st", par), tGs)
                    tU = decode_unit(n0, uwp_s, usc_s, bU, ("bufU", par), "gu", w(("bufU_st", par)))
                    tUs = D("sync", lambda e, n0=n0, bU=bU: e.dma_start(
                        out=udec_s[:, :, n0:n0 + 256].rearrange("c p n -> p c n"),
                        in_=bU[:].rearrange("p (c n) -> p c n", c=16)), waits=tU, incs=[("D_ust", 16)])
                    setw(("bufU_st", par), tUs)
                    tD_ = decode_unit(n0, dwp_s, dsc_s, bufD[u4], ("bufD", u4), "d",
                                      w("down_cons") + w(("bufD_st", u4)))
                    tDs = D("sync", lambda e, n0=n0, u4=u4: e.dma_start(
                        out=ddec_s[n0 // 128:n0 // 128 + 2].rearrange("j p h -> p j h"),
                        in_=bufD[u4][:].rearrange("p (j h) -> p j h", j=2)), waits=tD_, incs=[("D_dst", 16)])
                    setw(("bufD_st", u4), tDs)
                else:
                    tG = D("sync", lambda e, n0=n0, bG=bG: e.dma_start(
                        out=bG[:].rearrange("p (c n) -> p c n", c=16),
                        in_=gdec_s[:, :, n0:n0 + 256].rearrange("c p n -> p c n")),
                        waits=(prev if ug == 0 else []) + w(("bufG", par)), incs=[("D_gld", 16)])
                    tU = D("sync", lambda e, n0=n0, bU=bU: e.dma_start(
                        out=bU[:].rearrange("p (c n) -> p c n", c=16),
                        in_=udec_s[:, :, n0:n0 + 256].rearrange("c p n -> p c n")),
                        waits=w(("bufU", par)), incs=[("D_uld", 16)])
                    tD_ = D("sync", lambda e, n0=n0, u4=u4: e.dma_start(
                        out=bufD[u4][:].rearrange("p (j h) -> p j h", j=2),
                        in_=ddec_s[n0 // 128:n0 // 128 + 2].rearrange("j p h -> p j h")),
                        waits=w("down_cons") + w(("bufD_st", u4)), incs=[("D_dld", 16)])
                    setw(("bufD_st", u4), tD_)
                for jb in range(2):
                    tg = []
                    tu = []
                    for half in range(2):
                        for kk in range(16):
                            tg = D("tensor", lambda e, kk=kk, jb=jb, hf=half, bG=bG: e.matmul(
                                p_g[:, hf * 512:hf * 512 + 512],
                                bG[:, kk * 256 + jb * 128:kk * 256 + jb * 128 + 128],
                                xh[:, kk * TH + hf * 512:kk * TH + hf * 512 + 512],
                                start=(kk == 0), stop=(kk == 15)),
                                waits=(tG + tok_xh + w("p_g") if (kk == 0 and half == 0) else []),
                                incs=([("pe", 1)] if (kk == 15 and half == 1) else []))
                    for half in range(2):
                        for kk in range(16):
                            tu = D("tensor", lambda e, kk=kk, jb=jb, hf=half, bU=bU: e.matmul(
                                p_u[:, hf * 512:hf * 512 + 512],
                                bU[:, kk * 256 + jb * 128:kk * 256 + jb * 128 + 128],
                                xh[:, kk * TH + hf * 512:kk * TH + hf * 512 + 512],
                                start=(kk == 0), stop=(kk == 15)),
                                waits=(tU + w("p_u") if (kk == 0 and half == 0) else []),
                                incs=([("pe", 1)] if (kk == 15 and half == 1) else []))
                    if jb == 1:
                        setw(("bufG", par), tg)
                        setw(("bufU", par), tu)
                    tok_gl = D("scalar", lambda e: e.activation(gel[:], p_g[:], AF.Gelu_apprx_tanh),
                               waits=tg + w("gel"), incs=[("act", 1)])
                    setw("p_g", tok_gl)
                    jg = 2 * u4 + jb
                    tok_it = D("vector", lambda e, jg=jg: e.tensor_tensor(
                        intT[:, jg * TH:jg * TH + TH], gel[:], p_u[:], ALU.mult),
                        waits=tok_gl + tu + w(("intT", jg)), incs=[("dve", 1)])
                    setw("gel", tok_it)
                    setw("p_u", tok_it)
                    int_toks.append(tok_it[0])
            dn_toks = []
            for ht in range(16):
                for tp in range(2):
                    par2 = (ht * 2 + tp) % 2
                    pd = [p_dA, p_dB][par2]
                    pdkey = ("p_d", par2)
                    tdn = []
                    for jp in range(8):
                        u4p, pl = jp // 2, jp % 2
                        tdn = D("tensor", lambda e, pd=pd, u4p=u4p, pl=pl, ht=ht, tp=tp, jp=jp: e.matmul(
                            pd[:], bufD[u4p][:, pl * 2048 + ht * 128:pl * 2048 + ht * 128 + 128],
                            intT[:, (2 * u4p + pl) * TH + tp * 512:(2 * u4p + pl) * TH + tp * 512 + 512],
                            start=(jp == 0), stop=(jp == 7)),
                            waits=(int_toks + w(pdkey) + tD_ if jp == 0 else []),
                            incs=([("pe", 1)] if jp == 7 else []))
                    tok_ae = D("vector", lambda e, pd=pd, ht=ht, tp=tp: e.tensor_tensor(
                        accb[:, ht * TH + tp * 512:ht * TH + tp * 512 + 512],
                        accb[:, ht * TH + tp * 512:ht * TH + tp * 512 + 512], pd[:], ALU.add),
                        waits=tdn + tok_ab, incs=[("dve", 1)])
                    setw(pdkey, tok_ae)
                    dn_toks.append(tok_ae[0])
            setw("down_cons", [dn_toks[-1]])
            for jg in range(8):
                setw(("intT", jg), [dn_toks[-1]])
        D("sync", lambda e, t0=t0: e.dma_start(
            out=out_dst[:, :, t0:t0 + TH].rearrange("d p t -> p d t"),
            in_=accb[:].rearrange("p (d t) -> p d t", d=16)),
            waits=[dn_toks[-1]], incs=[("D_outs", 16)])
        prev = P.barrier()
    D("sync", lambda e: None, waits=P.dma_flush_waits())
    st.close()


_NC_CACHE = None


def kernel(**inputs):
    global _NC_CACHE
    x = np.asarray(inputs["L0_attn_hidden_states"], np.float32)
    shared = host_prep(inputs)
    if _NC_CACHE is None:
        _NC_CACHE = build()
    nc = _NC_CACHE
    in_maps = []
    for b in range(8):
        m = dict(shared)
        m["x"] = np.ascontiguousarray(x[b])
        m["xT"] = np.ascontiguousarray(x[b].T)
        in_maps.append(m)
    res = run_bass_kernel_spmd(nc, in_maps, core_ids=list(range(8)))
    out = np.empty((8, SEQ, DM), np.float32)
    for b in range(8):
        out[b] = np.asarray(res.results[b]["out"]).T
    return out


# version 09 (final)
# speedup vs baseline: 1.1576x; 1.0943x over previous
import os
import numpy as np
import ml_dtypes
import concourse.bass as bass
import concourse.mybir as mybir
from concourse.bass_utils import run_bass_kernel_spmd

F32 = mybir.dt.float32
F16 = mybir.dt.float16
BF16 = mybir.dt.bfloat16
I32 = mybir.dt.int32
ALU = mybir.AluOpType
AF = mybir.ActivationFunctionType
AXX = mybir.AxisListType.X
bfdt = ml_dtypes.bfloat16

DM, SEQ, DIN, DST, HEADS, PD, CHUNK = 2048, 2048, 4096, 128, 64, 64, 128
DIP, CONVD, MLP = 8512, 4352, 16384
SC, NSC = 256, 8
CPS = SC // CHUNK
TH = 1024
NEG = -1.0e30

DBG = []


class Prog:
    ENGS = ("sync", "scalar", "vector", "tensor", "gpsimd")

    def __init__(self, nc):
        self.nc = nc
        self.items = {e: [] for e in self.ENGS}
        self.sems = {}
        self.counts = {}
        for s in ("dma", "dmg", "pe", "dve", "act", "ph"):
            self.sems[s] = nc._stack.enter_context(nc.semaphore(s))
            self.counts[s] = 0

    def ensure(self, s):
        if s not in self.sems:
            self.sems[s] = self.nc._stack.enter_context(self.nc.semaphore(s))
            self.counts[s] = 0

    def do(self, eng, fn, waits=(), incs=()):
        waits = tuple(w for w in waits if w is not None)
        for s, v in waits:
            self.ensure(s)
            assert v <= self.counts[s], f"wait {s}>={v} not yet produced ({self.counts[s]})"
        toks = []
        for s, a in incs:
            self.ensure(s)
            self.counts[s] += a
            toks.append((s, self.counts[s]))
        self.items[eng].append((fn, waits, tuple(incs)))
        return toks

    def dma_flush_waits(self):
        return [(s, self.counts[s]) for s in self.sems
                if s.startswith("D_") and self.counts[s] > 0]

    def barrier(self):
        self.do("sync", None, waits=self.dma_flush_waits())
        base = self.counts["ph"]
        for e in self.ENGS:
            self.do(e, None, incs=[("ph", 1)])
        tgt = base + len(self.ENGS)
        for e in self.ENGS:
            self.do(e, None, waits=[("ph", tgt)])
        return [("ph", tgt)]

    def emit(self, block):
        for e in self.ENGS:
            items = self.items[e]
            if not items:
                continue

            def mk(items):
                def body(proxy):
                    for fn, waits, incs in items:
                        for s, v in waits:
                            proxy.wait_ge(self.sems[s], v)
                        inst = fn(proxy) if fn is not None else None
                        if inst is None:
                            for s, a in incs:
                                proxy.sem_inc(self.sems[s], a)
                        else:
                            for s, a in incs:
                                inst = inst.then_inc(self.sems[s], a)
                return body
            getattr(block, e)(mk(items))


def _perm(n):
    return np.concatenate([np.concatenate([np.arange(256 * i, 256 * i + 256, 2),
                                           np.arange(256 * i + 1, 256 * i + 256, 2)])
                           for i in range(n // 256)])


def host_prep(inp):
    f32 = np.float32
    d = {}
    dperm = _perm(DM)
    nperm = _perm(MLP)
    d["winT"] = np.ascontiguousarray(np.asarray(inp["L0_attn_W_in"], f32).T[dperm, :])
    d["woutT"] = np.ascontiguousarray(np.asarray(inp["L0_attn_W_out"], f32).T)
    wconv = np.asarray(inp["L0_attn_W_conv"], f32).reshape(CONVD, 4)
    d["wconv"] = np.ascontiguousarray(wconv.reshape(34, 128, 4).transpose(1, 0, 2)).reshape(128, 34 * 4)
    d["bconv"] = np.ascontiguousarray(np.asarray(inp["L0_attn_b_conv"], f32).reshape(34, 128).T)
    d["alog"] = np.ascontiguousarray(np.asarray(inp["L0_attn_A_log"], f32).reshape(HEADS, 1))
    d["dvec"] = np.ascontiguousarray(np.asarray(inp["L0_attn_D"], f32).reshape(1, HEADS))
    d["dtbias"] = np.ascontiguousarray(np.asarray(inp["L0_attn_dt_bias"], f32).reshape(HEADS, 1))
    d["nwc"] = np.ascontiguousarray(np.asarray(inp["L0_attn_norm_weight"], f32).reshape(32, 128).T)
    d["gwp"] = np.ascontiguousarray(np.asarray(inp["L0_ffn_gate_W_packed"])[nperm, :].T.astype(np.int32))
    d["uwp"] = np.ascontiguousarray(np.asarray(inp["L0_ffn_up_W_packed"])[nperm, :].T.astype(np.int32))
    d["gsc"] = np.ascontiguousarray(np.repeat(np.asarray(inp["L0_ffn_gate_scales"], f32)[nperm, :].T, 16, axis=0))
    d["usc"] = np.ascontiguousarray(np.repeat(np.asarray(inp["L0_ffn_up_scales"], f32)[nperm, :].T, 16, axis=0))
    d["dwp"] = np.ascontiguousarray(np.asarray(inp["L0_ffn_down_W_packed"]).T.astype(np.int32))
    d["dsc"] = np.ascontiguousarray(np.repeat(np.asarray(inp["L0_ffn_down_scales"], f32).T, 16, axis=0))
    d["eyeb"] = np.eye(128, dtype=bfdt)
    d["eyef"] = np.eye(128, dtype=np.float32)
    s, l = np.meshgrid(np.arange(128), np.arange(128), indexing="ij")
    d["maskb"] = np.where(s <= l, 0.0, NEG).astype(f32)
    return d


IN_SPECS = [
    ("x", [SEQ, DM], F32), ("xT", [DM, SEQ], F32),
    ("winT", [DM, DIP], F32), ("woutT", [DIN, DM], F32),
    ("wconv", [128, 34 * 4], F32), ("bconv", [128, 34], F32),
    ("alog", [HEADS, 1], F32), ("dvec", [1, HEADS], F32), ("dtbias", [HEADS, 1], F32),
    ("nwc", [128, 32], F32),
    ("gwp", [1024, MLP], I32), ("uwp", [1024, MLP], I32),
    ("gsc", [1024, MLP], F32), ("usc", [1024, MLP], F32),
    ("dwp", [MLP // 2, DM], I32), ("dsc", [MLP // 2, DM], F32),
    ("eyeb", [128, 128], BF16), ("eyef", [128, 128], F32),
    ("maskb", [128, 128], F32),
]

SCRATCH_SPECS = [
    ("xnt_d", [DM, SEQ], BF16), ("winT_bf", [DM, DIP], BF16),
    ("woutT_bf", [DIN, DM], BF16), ("acc_d", [DM, SEQ], F32),
    ("dacs_d", [HEADS, SC], F32),
    ("gdec", [DM, MLP], F16), ("udec", [DM, MLP], F16), ("ddec", [MLP, DM], F16),
    ("xn_d", [SEQ, DM], BF16), ("st_d", [128, 128], F32), ("y_d", [SEQ, DIN], F32),
]


class K:
    def __init__(self):
        from contextlib import ExitStack
        nc = bass.Bass("TRN2", target_bir_lowering=False)
        nc._stack = ExitStack()
        self.nc = nc
        self.A = {}
        for name, shape, dt in IN_SPECS:
            self.A[name] = nc.dram_tensor(name, shape, dt, kind="ExternalInput").ap()
        self.A["out"] = nc.dram_tensor("out", [DM, SEQ], F32, kind="ExternalOutput").ap()
        for name, shape, dt in SCRATCH_SPECS:
            kind = "ExternalOutput" if name in DBG else "Internal"
            self.A[name] = nc.dram_tensor(name, shape, dt, kind=kind).ap()
        self.P = Prog(nc)
        self.last = {}

    def T(self, n, s, d, stack=None):
        return (stack or self.nc._stack).enter_context(self.nc.sbuf_tensor(n, s, d))

    def PT(self, n, s, d, stack=None):
        return (stack or self.nc._stack).enter_context(self.nc.psum_tensor(n, s, d))

    def w(self, key):
        t = self.last.get(key)
        return [t] if t is not None else []

    def setw(self, key, toks):
        if toks:
            self.last[key] = toks[-1]


def build():
    k = K()
    nc, P, A, D = k.nc, k.P, k.A, k.P.do

    k.eye_b = k.T("c_eye_b", [128, 128], BF16)
    k.eye_f = k.T("c_eye_f", [128, 128], F32)
    k.mask = k.T("c_mask", [128, 128], F32)
    k.eps = k.T("c_eps", [128, 1], F32)
    k.one = k.T("c_one", [128, 1], F32)
    k.negtwo = k.T("c_negtwo", [128, 1], F32)
    k.negsix = k.T("c_negsix", [128, 1], F32)
    k.negfour = k.T("c_negfour", [128, 1], F32)
    k.onesf = k.T("c_onesf", [1, 128], F32)
    for tile, src in [(k.eye_b, "eyeb"), (k.eye_f, "eyef"), (k.mask, "maskb")]:
        D("sync", lambda e, t=tile, s=src: e.dma_start(out=t[:], in_=A[s][:]), incs=[("D_cst", 16)])
    for tile, val in [(k.eps, 1.0e-5), (k.one, 1.0), (k.negtwo, -2.0), (k.negsix, -6.0),
                      (k.negfour, -4.0)]:
        D("vector", lambda e, t=tile, v=val: e.memset(t[:], v), incs=[("dve", 1)])
    D("vector", lambda e: e.memset(k.onesf[:], 1.0), incs=[("dve", 1)])
    k.consts = [("D_cst", P.counts["D_cst"]), ("dve", P.counts["dve"])]

    lim = int(os.environ.get("PH_LIMIT", "3"))
    tok = phase_ln(k)
    if lim >= 1:
        tok = phase_wconv(k, tok)
    if lim >= 2:
        tok = phase_mamba(k, tok)
    if lim >= 3:
        phase_ffn(k, tok)
    else:
        phase_stub_out(k, tok)

    with nc.Block() as block:
        P.emit(block)
    return nc


def phase_stub_out(k, prev):
    from contextlib import ExitStack
    P, A, D = k.P, k.A, k.P.do
    st = ExitStack()
    z = k.T("stub_z", [128, SEQ], F32, st)
    tok = D("vector", lambda e: e.memset(z[:], 0.0), waits=prev, incs=[("dve", 1)])
    out_dst = A["out"].rearrange("(d p) t -> d p t", p=128)
    for i in range(16):
        tok = D("sync", lambda e, i=i: e.dma_start(out=out_dst[i], in_=z[:]),
                waits=tok, incs=[("D_outs", 16)])
    D("sync", lambda e: None, waits=P.dma_flush_waits())
    st.close()


def phase_ln(k):
    from contextlib import ExitStack
    P, A, D = k.P, k.A, k.P.do
    st = ExitStack()
    p0x = k.T("p0_x", [128, 2 * DM], F32, st)
    p0sq = k.T("p0_sq", [128, DM], F32, st)
    p0xn = k.T("p0_xn", [128, DM], BF16, st)
    p0st = k.T("p0_st", [128, 8], F32, st)
    p0tr = k.T("p0_tr", [128, 16 * 128], BF16, st)
    p0std = k.T("p0_std", [128, 128], F32, st)
    p0psA = k.PT("p0_psA", [128, 1024], BF16, st)
    p0psB = k.PT("p0_psB", [128, 1024], BF16, st)
    x_t = A["x"].rearrange("(a p) d -> a p d", p=128)
    xnt_dst = A["xnt_d"].rearrange("(c p) t -> c p t", p=128)
    prev_st = None
    prev_xn = None
    prev_ev = None
    for i in range(16):
        buf = p0x[:, (i % 2) * DM:(i % 2) * DM + DM]
        tok_ld = D("sync", lambda e, b=buf, i=i: e.dma_start(out=b, in_=x_t[i]),
                   waits=([prev_xn] if i >= 2 else []), incs=[(f"D_p0x{i % 2}", 16)])
        t_sum, t_sq, t_mean, t_var, t_rstd, t_m2 = (p0st[:, j:j + 1] for j in range(6))
        tok_su = D("vector", lambda e, b=buf: e.reduce_sum(t_sum, b, axis=AXX), waits=tok_ld,
                   incs=[("dve", 1)])
        tok_sq = D("scalar", lambda e, b=buf: e.activation(p0sq[:], b, AF.Square, accum_out=t_sq),
                   waits=tok_ld, incs=[("act", 1)])
        tok_mn = D("vector", lambda e: e.tensor_scalar(t_mean, t_sum, 1.0 / DM, None, ALU.mult),
                   waits=tok_sq + tok_su, incs=[("dve", 1)])
        tok_m2 = D("vector", lambda e: e.tensor_tensor(t_m2, t_mean, t_mean, ALU.mult),
                   waits=tok_mn, incs=[("dve", 1)])
        tok_va = D("vector", lambda e: e.tensor_scalar(t_var, t_sq, 1.0 / DM, None, ALU.mult),
                   incs=[("dve", 1)])
        tok_v = D("vector", lambda e: e.tensor_tensor(t_var, t_var, t_m2, ALU.subtract),
                  waits=tok_m2 + tok_va, incs=[("dve", 1)])
        tok_ln = D("scalar", lambda e: e.activation(t_rstd, t_var, AF.Ln, bias=k.eps[:]),
                   waits=tok_v + k.consts, incs=[("act", 1)])
        tok_r = D("scalar", lambda e: e.activation(t_rstd, t_rstd, AF.Exp, scale=-0.5),
                  waits=tok_ln, incs=[("act", 1)])
        tok_xn = D("vector", lambda e, b=buf: e.tensor_scalar(p0xn[:], b, t_mean, t_rstd,
                   ALU.subtract, ALU.mult), waits=tok_r + tok_mn + ([prev_st] if prev_st else []),
                   incs=[("dve", 1)])
        prev_xn = tok_xn[0]
        if os.environ.get("P0DBG") == "1":
            tok_sd = D("vector", lambda e, i=i: e.tensor_copy(p0std[:, i * 8:i * 8 + 8], p0st[:]),
                       waits=tok_xn, incs=[("dve", 1)])
            prev_xn = tok_sd[0]
            xn_dst = A["xn_d"].rearrange("(a p) d -> a p d", p=128)
            tok_xd = D("sync", lambda e, i=i: e.dma_start(out=xn_dst[i], in_=p0xn[:]),
                       waits=tok_xn, incs=[("D_dbg", 16)])
            prev_xn = tok_xd[0]
        evs = []
        for kk in range(16):
            c, pl = kk // 2, kk % 2
            src = p0xn[:, 256 * c + pl:256 * c + 256:2]
            gi = kk // 4
            pt_ = [p0psA, p0psB][gi % 2]
            pslot = pt_[:, (kk % 4) * 128:(kk % 4) * 128 + 128]
            w = tok_xn + k.consts
            if kk % 4 == 0 and gi >= 2:
                w = w + [evs[gi - 2]]
            elif kk % 4 == 0 and gi < 2 and prev_ev is not None:
                w = w + [prev_ev]
            tok_tr = D("tensor", lambda e, d_=pslot, s=src: e.transpose(d_, s, k.eye_b[:]),
                       waits=w, incs=[("pe", 1)])
            if kk % 4 == 3:
                tok_ev = D("scalar", lambda e, gi=gi, p_=pt_: e.copy(
                    p0tr[:, gi * 512:gi * 512 + 512], p_[:, 0:512]),
                    waits=tok_tr, incs=[("act", 1)])
                evs.append(tok_ev[0])
        prev_ev = evs[-1]
        tok_st = D("sync", lambda e, i=i: e.dma_start(
            out=xnt_dst[:, :, 128 * i:128 * i + 128].rearrange("c p t -> p c t"),
            in_=p0tr[:].rearrange("p (c t) -> p c t", c=16)),
            waits=[evs[-1]], incs=[("D_p0st", 16)])
        prev_st = tok_st[0]
    if os.environ.get("P0DBG") == "1":
        D("sync", lambda e: e.dma_start(out=A["st_d"][:], in_=p0std[:]),
          waits=[("dve", P.counts["dve"])], incs=[("D_dbg", 16)])
    tok = P.barrier()
    st.close()
    return tok


def phase_wconv(k, prev):
    from contextlib import ExitStack
    P, A, D = k.P, k.A, k.P.do
    st = ExitStack()
    p1f = k.T("p1_f", [128, DIP], F32, st)
    p1b = k.T("p1_b", [128, DIP], BF16, st)
    prevtok = None
    for srcT, dstT, wd, n in [(A["winT"], A["winT_bf"], DIP, 16), (A["woutT"], A["woutT_bf"], DM, 32)]:
        s_t = srcT.rearrange("(a p) f -> a p f", p=128)
        d_t = dstT.rearrange("(a p) f -> a p f", p=128)
        for i in range(n):
            tok_ld = D("sync", lambda e, i=i, s=s_t, wd=wd: e.dma_start(out=p1f[:, 0:wd], in_=s[i]),
                       waits=prev + ([prevtok] if prevtok else []), incs=[("D_p1l", 16)])
            tok_cv = D("scalar", lambda e, wd=wd: e.copy(p1b[:, 0:wd], p1f[:, 0:wd]),
                       waits=tok_ld, incs=[("act", 1)])
            tok_st = D("sync", lambda e, i=i, d=d_t, wd=wd: e.dma_start(out=d[i], in_=p1b[:, 0:wd]),
                       waits=tok_cv, incs=[("D_p1s", 16)])
            prevtok = tok_st[0]
    tok = P.barrier()
    st.close()
    return tok


def phase_mamba(k, prev):
    from contextlib import ExitStack
    P, A, D = k.P, k.A, k.P.do
    st = ExitStack()
    T = lambda n, s, d: k.T(n, s, d, st)
    PT = lambda n, s, d: k.PT(n, s, d, st)
    w, setw = k.w, k.setw

    hs = T("m_hs", [128, DIN], F32)
    hsb = T("m_hsb", [128, DIN], BF16)
    xbcs = T("m_xbcs", [128, 34 * SC], BF16)
    zst = T("m_zst", [128, 32 * SC], BF16)
    xnsl = T("m_xnsl", [128, 16 * SC], BF16)
    wsl = T("m_wsl", [128, 2 * 4096], BF16)
    flg = T("m_flg", [1, 2 * 1024], F32)
    dtp = T("m_dtp", [64, SC], F32)
    dts = T("m_dts", [64, SC], F32)
    dca = T("m_dca", [64, 2 * (128 + CHUNK)], F32)
    dcb = T("m_dcb", [64, 2 * (128 + CHUNK)], F32)
    mht = T("m_mht", [128, 2 * 1024], BF16)
    segs = T("m_segs", [128, 1024], F32)
    xch = T("m_xch", [128, DIN], BF16)
    ut = T("m_ut", [128, DIN], BF16)
    upt = T("m_upt", [128, DIN], BF16)
    yb = T("m_yb", [128, DIN], F32)
    ynb = T("m_ynb", [128, DIN], BF16)
    yntb = T("m_yntb", [128, 32 * SC], BF16)
    cbtm = T("m_cbtm", [128, 128], BF16)
    blt = T("m_blt", [128, 128], BF16)
    zs = T("m_zs", [128, DIN], BF16)
    sm = T("m_sm", [128, 512], F32)
    nwcol = T("m_nwcol", [128, 32], F32)
    cvs = T("m_cvs", [128, SC + 4], F32)
    cvs2 = T("m_cvs2", [128, SC], F32)
    halo = T("m_halo", [128, 34 * 3], F32)
    colw = T("m_colw", [128, 34 * 4], F32)
    colb = T("m_colb", [128, 34], F32)
    acol = T("m_acol", [64, 4], F32)
    accst = T("m_accst", [128, SC], F32)
    xts = T("m_xts", [128, 2 * SC], F32)
    rowsc = T("m_rowsc", [1, 128], F32)
    scr5 = T("m_scr5", [128, 512], F32)
    p_yA = PT("p_yA", [128, 512], F32)
    p_yB = PT("p_yB", [128, 512], F32)
    p_sgA = PT("p_sgA", [128, 1024], F32)
    p_sgB = PT("p_sgB", [128, 1024], F32)
    p_trA = PT("p_trA", [128, 1024], BF16)
    p_trB = PT("p_trB", [128, 1024], BF16)
    p_tr = [p_trA, p_trB]
    dAcsT = sm[:, 0:64]
    sdoT = sm[:, 64:128]
    decayT = sm[:, 128:192]
    dtT = sm[:, 192:256]
    cdb = sm[:, 256:320]
    Db = sm[:, 320:384]
    rms = sm[:, 384:385]
    rstd = sm[:, 385:386]

    D("sync", lambda e: e.dma_start(out=colw[:], in_=A["wconv"][:]), waits=prev, incs=[("D_minit", 16)])
    D("sync", lambda e: e.dma_start(out=colb[:], in_=A["bconv"][:]), incs=[("D_minit", 16)])
    D("sync", lambda e: e.dma_start(out=acol[:, 0:1], in_=A["alog"][:]), incs=[("D_minit", 16)])
    D("sync", lambda e: e.dma_start(out=acol[:, 1:2], in_=A["dtbias"][:]), incs=[("D_minit", 16)])
    D("sync", lambda e: e.dma_start(out=rowsc[:, 0:64], in_=A["dvec"][:]), incs=[("D_minit", 16)])
    D("sync", lambda e: e.dma_start(out=nwcol[:], in_=A["nwc"][:]), incs=[("D_minit", 16)])
    cdma = [("D_minit", P.counts["D_minit"])]
    D("vector", lambda e: e.memset(hs[:], 0.0), waits=prev, incs=[("dve", 1)])
    D("vector", lambda e: e.memset(halo[:], 0.0), incs=[("dve", 1)])
    D("vector", lambda e: e.memset(dca[:], 0.0), incs=[("dve", 1)])
    D("vector", lambda e: e.memset(dcb[:], 0.0), incs=[("dve", 1)])
    tok_a = D("scalar", lambda e: e.activation(acol[:, 2:3], acol[:, 0:1], AF.Exp), waits=cdma, incs=[("act", 1)])
    D("vector", lambda e: e.tensor_scalar(acol[:, 2:3], acol[:, 2:3], -1.0, None, ALU.mult),
      waits=tok_a, incs=[("dve", 1)])
    tok_mm = D("tensor", lambda e: e.matmul(p_yA[:, 0:64], k.onesf[:], rowsc[:, 0:64], start=True, stop=True),
               waits=cdma + k.consts, incs=[("pe", 1)])
    D("vector", lambda e: e.tensor_copy(Db, p_yA[:, 0:64]), waits=tok_mm, incs=[("dve", 1)])
    bar = P.barrier()

    xnt_src = A["xnt_d"].rearrange("(c p) t -> c p t", p=128)
    win_src = A["winT_bf"].rearrange("(c p) f -> c p f", p=128)
    wo_src = A["woutT_bf"].rearrange("(c p) d -> c p d", p=128)
    xT_src = A["xT"].rearrange("(d p) t -> d p t", p=128)
    acc_dst = A["acc_d"].rearrange("(d p) t -> d p t", p=128)

    ptr_key = [("ptr", 0), ("ptr", 1)]

    def transpose_seq(pairs, first_waits):
        evs = []
        n = len(pairs)
        for i in range(n):
            gi = i // 4
            pt = p_tr[gi % 2]
            slot = pt[:, (i % 4) * 128:(i % 4) * 128 + 128]
            wt = []
            if i == 0:
                wt = list(first_waits) + w(ptr_key[0]) + w(ptr_key[1])
            if i % 4 == 0 and gi >= 2:
                wt = wt + [evs[gi - 2]]
            src = pairs[i][0]
            tok_tr = D("tensor", lambda e, d_=slot, s=src: e.transpose(d_, s, k.eye_b[:]),
                       waits=wt, incs=[("pe", 1)])
            if i % 4 == 3 or i == n - 1:
                fn = pairs[i][1]
                tok_ev = fn(pt, gi, tok_tr)
                evs.append(tok_ev[0])
        if len(evs) >= 2:
            setw(ptr_key[(len(evs) - 2) % 2], [evs[-2]])
        setw(ptr_key[(len(evs) - 1) % 2], [evs[-1]])
        return evs

    for sc in range(NSC):
        t0 = sc * SC
        start_tok = bar if sc == 0 else []
        tok_xn = D("sync", lambda e, t0=t0: e.dma_start(
            out=xnsl[:].rearrange("p (c t) -> p c t", c=16),
            in_=xnt_src[:, :, t0:t0 + SC].rearrange("c p t -> p c t")),
            waits=start_tok + w("xnsl"), incs=[("D_xnsl", 16)])

        dtraw_tok = None
        tokm = []
        for fg in range(34):
            f0 = 256 * fg
            wdt = 256 if fg < 33 else 64
            sb = wsl[:, (fg % 2) * 4096:(fg % 2) * 4096 + 16 * wdt]
            tok_w = D("sync", lambda e, f0=f0, wdt=wdt, sb=sb: e.dma_start(
                out=sb.rearrange("p (c f) -> p c f", c=16),
                in_=win_src[:, :, f0:f0 + wdt].rearrange("c p f -> p c f")),
                waits=start_tok + w(("wsl", fg % 2)), incs=[(f"D_wsl{fg % 2}", 16)])
            for ft in range(2 if wdt == 256 else 1):
                fr = f0 + 128 * ft
                rows = min(128, DIP - fr)
                par = (fg * 2 + ft) % 2
                pt_ = [p_yA, p_yB][par]
                ps = pt_[0:rows, 0:SC]
                pkey = ("p_inp", par)
                for kk in range(16):
                    tokm = D("tensor", lambda e, ps=ps, sb=sb, kk=kk, ft=ft, wdt=wdt, rows=rows: e.matmul(
                        ps, sb[:, kk * wdt + 128 * ft: kk * wdt + 128 * ft + rows],
                        xnsl[:, kk * SC:(kk + 1) * SC],
                        start=(kk == 0), stop=(kk == 15)),
                        waits=(tok_xn + tok_w + w(pkey) + w(("p_y", par)) if kk == 0 else []),
                        incs=([("pe", 1)] if kk == 15 else []))
                ftile = fr // 128
                if fr + rows <= DIN:
                    tok_e = D("scalar", lambda e, ps=ps, ftile=ftile: e.activation(
                        zst[:, ftile * SC:(ftile + 1) * SC], ps, AF.Silu),
                        waits=tokm + w("zst_cons"), incs=[("act", 1)])
                    setw(pkey, tok_e)
                elif fr < DIN + CONVD:
                    ct = ftile - 32
                    tok_h = D("vector", lambda e, ct=ct: e.tensor_copy(cvs[:, 0:3], halo[:, ct * 3:ct * 3 + 3]),
                              waits=tokm + w("cvs") + w(("halo", ct)), incs=[("dve", 1)])
                    tok_c = D("vector", lambda e, ps=ps: e.tensor_copy(cvs[:, 3:3 + SC], ps), incs=[("dve", 1)])
                    setw(pkey, tok_c)
                    tok_h2 = D("vector", lambda e, ct=ct: e.tensor_copy(halo[:, ct * 3:ct * 3 + 3],
                               cvs[:, SC:SC + 3]), waits=tok_c, incs=[("dve", 1)])
                    setw(("halo", ct), tok_h2)
                    tok_t = D("vector", lambda e, ct=ct: e.tensor_scalar(
                        cvs2[:], cvs[:, 0:SC], colw[:, ct * 4:ct * 4 + 1], None, ALU.mult),
                        waits=w("cvs2") + tok_h + tok_h2)
                    for j in (1, 2, 3):
                        tok_t = D("vector", lambda e, ct=ct, j=j: e.scalar_tensor_tensor(
                            cvs2[:], cvs[:, j:j + SC], colw[:, ct * 4 + j:ct * 4 + j + 1],
                            cvs2[:], ALU.mult, ALU.add),
                            incs=([("dve", 1)] if j == 3 else []))
                    setw("cvs", tok_t)
                    tok_s = D("scalar", lambda e, ct=ct: e.activation(
                        xbcs[:, ct * SC:(ct + 1) * SC], cvs2[:], AF.Silu,
                        bias=colb[:, ct:ct + 1]), waits=tok_t, incs=[("act", 1)])
                    setw("cvs2", tok_s)
                else:
                    tok_c = D("vector", lambda e, ps=ps: e.tensor_copy(dtp[:], ps),
                              waits=tokm, incs=[("dve", 1)])
                    setw(pkey, tok_c)
                    dtraw_tok = tok_c
            setw(("wsl", fg % 2), tokm)
        inproj_act = [("act", P.counts["act"])]

        tok_b1 = D("vector", lambda e: e.tensor_scalar(dtp[:], dtp[:], acol[:, 1:2], None, ALU.add),
                   waits=dtraw_tok, incs=[("dve", 1)])
        tok = D("scalar", lambda e: e.activation(dts[:], dtp[:], AF.Abs), waits=tok_b1, incs=[("act", 1)])
        tok = D("scalar", lambda e: e.activation(dts[:], dts[:], AF.Exp, scale=-1.0), waits=tok,
                incs=[("act", 1)])
        tok = D("scalar", lambda e: e.activation(dts[:], dts[:], AF.Ln, bias=k.one[0:64, :]), waits=tok,
                incs=[("act", 1)])
        tok_mx = D("vector", lambda e: e.tensor_scalar(dtp[:], dtp[:], 0.0, None, ALU.max), waits=tok,
                   incs=[("dve", 1)])
        tok3 = D("vector", lambda e: e.tensor_tensor(dtp[:], dtp[:], dts[:], ALU.add), waits=tok_mx,
                 incs=[("dve", 1)])
        tok_da = tok3
        for ch in range(CPS):
            tok_da = D("vector", lambda e, ch=ch: e.tensor_scalar(
                dca[:, 256 * ch + 128:256 * ch + 256], dtp[:, ch * CHUNK:(ch + 1) * CHUNK],
                acol[:, 2:3], None, ALU.mult),
                waits=tok_da + (w("dacs_cons") + w("dacsb_done") if ch == 0 else []), incs=[("dve", 1)])
        tok_cs = tok_da
        for shp in range(7):
            s_ = 1 << shp
            sb_, db_ = (dca, dcb) if shp % 2 == 0 else (dcb, dca)
            for ch in range(CPS):
                b = 256 * ch + 128
                tok_cs = D("vector", lambda e, b=b, s_=s_, sb_=sb_, db_=db_: e.tensor_tensor(
                    db_[:, b:b + CHUNK], sb_[:, b:b + CHUNK], sb_[:, b - s_:b + CHUNK - s_], ALU.add),
                    waits=tok_cs, incs=[("dve", 1)])
        dacs_buf = dcb
        dacs_v = lambda ch: dacs_buf[:, 256 * ch + 128:256 * ch + 256]
        tok_bd = D("gpsimd", lambda e: e.dma_start(
            out=A["dacs_d"][:].rearrange("h (c l) -> h c l", c=2),
            in_=dacs_buf[:].rearrange("h (c q) -> h c q", c=2)[:, :, 128:256]),
            waits=tok_cs + w(("flg", 0)) + w(("flg", 1)), incs=[("D_dacsb", 16)])
        setw("dacsb_done", tok_bd)

        for ch in range(CPS):
            cb = ch * CHUNK
            t1 = D("tensor", lambda e, ch=ch: e.transpose(p_yA[:, 0:64], dacs_v(ch), k.eye_f[0:64, 0:64]),
                   waits=tok_cs + w(("p_y", 0)), incs=[("pe", 1)])
            t1e = D("vector", lambda e: e.tensor_copy(dAcsT, p_yA[:, 0:64]), waits=t1, incs=[("dve", 1)])
            t2 = D("tensor", lambda e, cb=cb: e.transpose(p_yB[:, 0:64], dtp[:, cb:cb + CHUNK], k.eye_f[0:64, 0:64]),
                   waits=tok3 + w(("p_y", 1)), incs=[("pe", 1)])
            t2e = D("vector", lambda e: e.tensor_copy(dtT, p_yB[:, 0:64]), waits=t2, incs=[("dve", 1)])
            tok_sdo = D("scalar", lambda e: e.activation(sdoT, dAcsT, AF.Exp), waits=t1e, incs=[("act", 1)])
            t3_ = D("tensor", lambda e, ch=ch: e.transpose(p_yA[0:1, 64:128],
                    dacs_v(ch)[:, CHUNK - 1:CHUNK], k.eye_f[0:64, 0:64]), waits=t1e, incs=[("pe", 1)])
            t3e = D("vector", lambda e: e.tensor_copy(rowsc[:, 64:128], p_yA[0:1, 64:128]),
                    waits=t3_, incs=[("dve", 1)])
            tok_cd = D("scalar", lambda e: e.activation(rowsc[:, 0:64], rowsc[:, 64:128], AF.Exp),
                       waits=t3e, incs=[("act", 1)])
            t4 = D("tensor", lambda e: e.matmul(p_yA[:, 64:128], k.onesf[:], rowsc[:, 0:64],
                   start=True, stop=True), waits=tok_cd, incs=[("pe", 1)])
            t4e = D("vector", lambda e: e.tensor_copy(cdb, p_yA[:, 64:128]), waits=t4, incs=[("dve", 1)])
            t5 = D("tensor", lambda e: e.matmul(p_yB[:, 64:128], k.onesf[:], rowsc[:, 64:128],
                   start=True, stop=True), waits=t3e + t2e, incs=[("pe", 1)])
            t5e = D("vector", lambda e: e.tensor_tensor(decayT, p_yB[:, 64:128], dAcsT, ALU.subtract),
                    waits=t5 + t1e, incs=[("dve", 1)])
            tok_dce = D("scalar", lambda e: e.activation(decayT, decayT, AF.Exp), waits=t5e, incs=[("act", 1)])
            t6 = D("tensor", lambda e, cb=cb: e.matmul(p_yA[:, 128:256],
                   xbcs[:, 32 * SC + cb:32 * SC + cb + CHUNK],
                   xbcs[:, 33 * SC + cb:33 * SC + cb + CHUNK], start=True, stop=True),
                   waits=inproj_act + t4e, incs=[("pe", 1)])
            t6e = D("vector", lambda e: e.tensor_copy(cbtm[:], p_yA[:, 128:256]), waits=t6, incs=[("dve", 1)])
            smalls = t6e + t4e + [t5e[0], tok_dce[0], t2e[0], tok_sdo[0]]
            setw(("p_y", 0), t6e)
            setw(("p_y", 1), t5e)

            def mkev_x(dst):
                def f(ptx, gi, tok_tr):
                    return D("scalar", lambda e, ptx=ptx, gi=gi: e.copy(dst[:, gi * 512:gi * 512 + 512],
                             ptx[:, 0:512]), waits=tok_tr, incs=[("act", 1)])
                return f
            prx = [(xbcs[:, i * SC + cb:i * SC + cb + CHUNK], mkev_x(xch)) for i in range(32)]
            evx = transpose_seq(prx, inproj_act + w("xch_cons"))
            prz = [(zst[:, i * SC + cb:i * SC + cb + CHUNK], mkev_x(zs)) for i in range(32)]
            evz = transpose_seq(prz, inproj_act + w("zs_cons"))
            wt = w(ptr_key[0]) + w(ptr_key[1])
            tok_bl = D("tensor", lambda e, cb=cb: e.transpose(p_trA[:, 0:128],
                       xbcs[:, 32 * SC + cb:32 * SC + cb + CHUNK], k.eye_b[:]), waits=wt, incs=[("pe", 1)])
            tok_ble = D("vector", lambda e: e.tensor_copy(blt[:], p_trA[:, 0:128]), waits=tok_bl, incs=[("dve", 1)])
            setw(ptr_key[0], tok_ble)

            tok_u = D("vector", lambda e: e.tensor_tensor(
                ut[:].rearrange("p (h q) -> p h q", h=64), xch[:].rearrange("p (h q) -> p h q", h=64),
                dtT.unsqueeze(2).broadcast_to([128, 64, 64]), ALU.mult),
                waits=[evx[-1]] + t2e + w("ut_cons"), incs=[("dve", 1)])
            tok_up = D("vector", lambda e: e.tensor_tensor(
                upt[:].rearrange("p (h q) -> p h q", h=64), ut[:].rearrange("p (h q) -> p h q", h=64),
                decayT.unsqueeze(2).broadcast_to([128, 64, 64]), ALU.mult),
                waits=tok_u + tok_dce + w("upt_cons"), incs=[("dve", 1)])
            tok_hsb = D("vector", lambda e: e.tensor_copy(hsb[:], hs[:]),
                        waits=w("hs_upd") + w("hsb_cons"), incs=[("dve", 1)])

            y_toks = []
            sg_tok = []
            for g in range(8):
                pg = [p_sgA, p_sgB][g % 2]
                py = [p_yA, p_yB][g % 2]
                sgkey = ("p_sg", g % 2)
                pykey = ("p_y", g % 2)
                fslot = flg[:, (g % 2) * 1024:(g % 2) * 1024 + 1024]
                tok_fl = D("gpsimd", lambda e, g=g, cb=cb, fs=fslot: e.dma_start(
                    out=fs.rearrange("a (h l) -> a h l", h=8),
                    in_=A["dacs_d"][g * 8:g * 8 + 8, cb:cb + CHUNK].unsqueeze(0)),
                    waits=tok_bd + w(("flg", g % 2)), incs=[(f"D_flg{g % 2}", 16)])
                for jj in range(2):
                    sg_tok = D("tensor", lambda e, pg=pg, jj=jj, fs=fslot: e.matmul(
                        pg[:, jj * 512:jj * 512 + 512], k.onesf[:],
                        fs[:, jj * 512:jj * 512 + 512],
                        start=True, stop=True),
                        waits=(tok_fl + w(sgkey) if jj == 0 else []),
                        incs=([("pe", 1)] if jj == 1 else []))
                tok_sm = D("vector", lambda e, pg=pg, g=g: e.tensor_tensor(
                    segs[:].rearrange("p (h l) -> p h l", h=8),
                    pg[:].rearrange("p (h l) -> p h l", h=8),
                    dAcsT[:, g * 8:g * 8 + 8].unsqueeze(2).broadcast_to([128, 8, 128]),
                    ALU.subtract), waits=sg_tok + t1e + w("segs"), incs=[("dve", 1)])
                tok_sm = D("vector", lambda e: e.tensor_tensor(
                    segs[:].rearrange("p (h l) -> p h l", h=8),
                    segs[:].rearrange("p (h l) -> p h l", h=8),
                    k.mask[:].unsqueeze(1).broadcast_to([128, 8, 128]),
                    ALU.add), incs=[("dve", 1)])
                setw(sgkey, tok_sm)
                setw(("flg", g % 2), sg_tok)
                mh = mht[:, (g % 2) * 1024:(g % 2) * 1024 + 1024]
                tok_ex = D("scalar", lambda e, mh=mh: e.activation(mh, segs[:], AF.Exp),
                           waits=tok_sm + w(("mht", g % 2)), incs=[("act", 1)])
                setw("segs", tok_ex)
                tok_mh = D("vector", lambda e, mh=mh: e.tensor_tensor(
                    mh.rearrange("p (h l) -> p h l", h=8), mh.rearrange("p (h l) -> p h l", h=8),
                    cbtm[:].unsqueeze(1).broadcast_to([128, 8, 128]), ALU.mult),
                    waits=tok_ex + t6e, incs=[("dve", 1)])
                tok_yo = D("tensor", lambda e, py=py, g=g, cb=cb: e.matmul(
                    py[:], xbcs[:, 33 * SC + cb:33 * SC + cb + CHUNK],
                    hsb[:, g * 512:g * 512 + 512], start=True, stop=True),
                    waits=tok_hsb + w(pykey), incs=[("pe", 1)])
                tok_sc = D("vector", lambda e, py=py, g=g: e.tensor_tensor(
                    py[:].rearrange("p (h q) -> p h q", h=8), py[:].rearrange("p (h q) -> p h q", h=8),
                    sdoT[:, g * 8:g * 8 + 8].unsqueeze(2).broadcast_to([128, 8, 64]), ALU.mult),
                    waits=tok_yo + tok_sdo, incs=[("dve", 1)])
                tok_yd = []
                for hh in range(8):
                    tok_yd = D("tensor", lambda e, py=py, mh=mh, g=g, hh=hh: e.matmul(
                        py[:, hh * 64:hh * 64 + 64], mh[:, hh * 128:hh * 128 + 128],
                        ut[:, g * 512 + hh * 64:g * 512 + hh * 64 + 64],
                        start=False, stop=(hh == 7)),
                        waits=(tok_sc + tok_mh + tok_u if hh == 0 else []),
                        incs=([("pe", 1)] if hh == 7 else []))
                setw(("mht", g % 2), tok_yd)
                tok_d2 = D("vector", lambda e, g=g: e.tensor_tensor(
                    scr5[:].rearrange("p (h q) -> p h q", h=8),
                    xch[:, g * 512:g * 512 + 512].rearrange("p (h q) -> p h q", h=8),
                    Db[:, g * 8:g * 8 + 8].unsqueeze(2).broadcast_to([128, 8, 64]), ALU.mult),
                    waits=[evx[-1]] + w("scr5"))
                tok_ye = D("vector", lambda e, py=py, g=g: e.tensor_tensor(
                    yb[:, g * 512:g * 512 + 512], py[:], scr5[:], ALU.add),
                    waits=tok_yd + w("yb_cons"), incs=[("dve", 1)])
                setw("scr5", tok_ye)
                setw(pykey, tok_ye)
                y_toks.append(tok_ye[0])
            setw("ut_cons", tok_yd)
            setw("xch_cons", [y_toks[-1]])

            st_toks = []
            for g in range(8):
                py = [p_yA, p_yB][g % 2]
                pykey = ("p_y", g % 2)
                tok_s1 = D("tensor", lambda e, py=py, g=g: e.matmul(
                    py[:], blt[:], upt[:, g * 512:g * 512 + 512], start=True, stop=True),
                    waits=tok_ble + tok_up + w(pykey), incs=[("pe", 1)])
                D("vector", lambda e, g=g: e.tensor_tensor(
                    hs[:, g * 512:g * 512 + 512].rearrange("p (h q) -> p h q", h=8),
                    hs[:, g * 512:g * 512 + 512].rearrange("p (h q) -> p h q", h=8),
                    cdb[:, g * 8:g * 8 + 8].unsqueeze(2).broadcast_to([128, 8, 64]), ALU.mult),
                    waits=t4e + tok_hsb)
                tok_h2 = D("vector", lambda e, py=py, g=g: e.tensor_tensor(
                    hs[:, g * 512:g * 512 + 512], hs[:, g * 512:g * 512 + 512], py[:], ALU.add),
                    waits=tok_s1, incs=[("dve", 1)])
                setw(pykey, tok_h2)
                st_toks.append(tok_h2[0])
            setw("hs_upd", [st_toks[-1]])
            setw("upt_cons", [st_toks[-1]])
            setw("hsb_cons", [tok_yo[0] if False else y_toks[-1]])

            if os.environ.get("P2DBG") == "1":
                yd_dst = A["y_d"].rearrange("(a p) f -> a p f", p=128)
                tok_yD = D("sync", lambda e, sc=sc, ch=ch: e.dma_start(
                    out=yd_dst[sc * CPS + ch], in_=yb[:]), waits=[y_toks[-1]], incs=[("D_dbg", 16)])
            tok_yg = D("vector", lambda e: e.tensor_tensor(yb[:], yb[:], zs[:], ALU.mult),
                       waits=[y_toks[-1], evz[-1]], incs=[("dve", 1)])
            setw("zs_cons", tok_yg)
            tok_ss = D("scalar", lambda e: e.activation(ynb[:], yb[:], AF.Square, accum_out=rms),
                       waits=tok_yg, incs=[("act", 1)])
            tok_l2 = D("scalar", lambda e: e.activation(rstd, rms, AF.Ln, scale=1.0 / DIN, bias=k.eps[:]),
              waits=tok_ss, incs=[("act", 1)])
            tok_r2 = D("scalar", lambda e: e.activation(rstd, rstd, AF.Exp, scale=-0.5), waits=tok_l2,
                       incs=[("act", 1)])
            tok_yn = D("vector", lambda e: e.tensor_scalar(ynb[:], yb[:], rstd, None, ALU.mult),
                       waits=tok_r2, incs=[("dve", 1)])
            setw("yb_cons", tok_yn)

            def mkev_y(cb):
                def f(ptx, gi, tok_tr):
                    toks = []
                    for q in range(4):
                        i = gi * 4 + q
                        toks = D("vector", lambda e, ptx=ptx, i=i, q=q, cb=cb: e.tensor_scalar(
                            yntb[:, i * SC + cb:i * SC + cb + CHUNK], ptx[:, q * 128:q * 128 + 128],
                            nwcol[:, i:i + 1], None, ALU.mult),
                            waits=(tok_tr + w("yntb_cons") if q == 0 else []),
                            incs=([("dve", 1)] if q == 3 else []))
                    return toks
                return f
            pry = [(ynb[:, i * 128:i * 128 + 128], mkev_y(cb)) for i in range(32)]
            evy = transpose_seq(pry, tok_yn)
            setw("zst_cons", [evz[-1]])
            setw("xbcs_cons", [y_toks[-1]])
            setw("dacs_cons", t3_)
            setw("dtp_cons", t2)
            last_ev = evy[-1]

        op_toks = []
        for d_ in range(16):
            sb = wsl[:, (d_ % 2) * 4096:(d_ % 2) * 4096 + 4096]
            tok_w = D("sync", lambda e, d_=d_, sb=sb: e.dma_start(
                out=sb.rearrange("p (c f) -> p c f", c=32),
                in_=wo_src[:, :, d_ * 128:d_ * 128 + 128].rearrange("c p f -> p c f")),
                waits=bar + w(("wsl", d_ % 2)), incs=[(f"D_wsl{d_ % 2}", 16)])
            xtb = xts[:, (d_ % 2) * SC:(d_ % 2) * SC + SC]
            tok_x = D("sync", lambda e, d_=d_, xtb=xtb, t0=t0: e.dma_start(
                out=xtb, in_=xT_src[d_][:, t0:t0 + SC]),
                waits=w(("xts", d_ % 2)), incs=[(f"D_xts{d_ % 2}", 16)])
            py = [p_yA, p_yB][d_ % 2]
            pykey = ("p_y", d_ % 2)
            for c2 in range(32):
                tokm = D("tensor", lambda e, py=py, sb=sb, c2=c2: e.matmul(
                    py[:, 0:SC], sb[:, c2 * 128:c2 * 128 + 128], yntb[:, c2 * SC:c2 * SC + SC],
                    start=(c2 == 0), stop=(c2 == 31)),
                    waits=(tok_w + [last_ev] + w(pykey) if c2 == 0 else []),
                    incs=([("pe", 1)] if c2 == 31 else []))
            setw(("wsl", d_ % 2), tokm)
            tok_ac = D("vector", lambda e, py=py, xtb=xtb: e.tensor_tensor(accst[:], py[:, 0:SC], xtb, ALU.add),
                       waits=tokm + tok_x + w("accst"), incs=[("dve", 1)])
            setw(pykey, tok_ac)
            setw(("xts", d_ % 2), tok_ac)
            tok_ad = D("sync", lambda e, d_=d_, t0=t0: e.dma_start(
                out=acc_dst[d_][:, t0:t0 + SC], in_=accst[:]), waits=tok_ac, incs=[("D_accst", 16)])
            setw("accst", tok_ad)
            op_toks.append(tok_ad[0])
        setw("yntb_cons", [op_toks[-1]])

    bar = P.barrier()
    st.close()
    return bar


def phase_ffn(k, prev):
    from contextlib import ExitStack
    P, A, D = k.P, k.A, k.P.do
    st = ExitStack()
    T = lambda n, s, d: k.T(n, s, d, st)
    PT = lambda n, s, d: k.PT(n, s, d, st)
    w, setw = k.w, k.setw

    xh = T("f_xh", [128, 16 * TH], F16)
    accb = T("f_accb", [128, 16 * TH], F32)
    intT = T("f_intT", [128, 8 * TH], F16)
    bufG = [T(f"f_bufG{i}", [128, 16 * 256], F16) for i in range(2)]
    bufU = [T(f"f_bufU{i}", [128, 16 * 256], F16) for i in range(2)]
    bufD = [T(f"f_bufD{i}", [128, 2 * 2048], F16) for i in range(4)]
    pw = T("f_pw", [128, 2048], I32)
    psc = T("f_psc", [128, 2048], F16)
    tA = T("f_tA", [128, 512], I32)
    tB = T("f_tB", [128, 512], I32)
    tC = T("f_tC", [128, 512], I32)
    br4 = [T("f_br4a", [128, 512], BF16), T("f_br4b", [128, 512], BF16)]
    br6 = [T("f_br6a", [128, 512], BF16), T("f_br6b", [128, 512], BF16)]
    bg = [T("f_bga", [128, 512], BF16), T("f_bgb", [128, 512], BF16)]
    bv = T("f_bv", [128, 512], BF16)
    gel = T("f_gel", [128, TH], F16)
    p_g = PT("f_pg", [128, 1024], F32)
    p_u = PT("f_pu", [128, 1024], F32)
    p_dA = PT("f_pdA", [128, 512], F32)
    p_dB = PT("f_pdB", [128, 512], F32)

    gwp_s = A["gwp"].rearrange("(c p) n -> c p n", p=128)
    uwp_s = A["uwp"].rearrange("(c p) n -> c p n", p=128)
    gsc_s = A["gsc"].rearrange("(c p) n -> c p n", p=128)
    usc_s = A["usc"].rearrange("(c p) n -> c p n", p=128)
    dwp_s = A["dwp"].rearrange("(j p) h -> j p h", p=128)
    dsc_s = A["dsc"].rearrange("(j p) h -> j p h", p=128)
    gdec_s = A["gdec"].rearrange("(c p) n -> c p n", p=128)
    udec_s = A["udec"].rearrange("(c p) n -> c p n", p=128)
    ddec_s = A["ddec"].rearrange("(j p) h -> j p h", p=128)
    xnt_src = A["xnt_d"].rearrange("(c p) t -> c p t", p=128)
    acc_src = A["acc_d"].rearrange("(d p) t -> d p t", p=128)
    out_dst = A["out"].rearrange("(d p) t -> d p t", p=128)

    wdt0 = 512

    def decode_unit(n0, wsrc, ssrc, dst, dstkey, kind, extraw):
        if kind == "gu":
            tok_w = D("sync", lambda e, n0=n0, ws=wsrc: e.dma_start(
                out=pw[:].rearrange("p (c n) -> p c n", c=8),
                in_=ws[:, :, n0:n0 + 256].rearrange("c p n -> p c n")),
                waits=extraw + w("pw"), incs=[("D_pw", 16)])
            tok_s = D("gpsimd", lambda e, n0=n0, ss=ssrc: e.dma_start(
                out=psc[:].rearrange("p (c n) -> p c n", c=8),
                in_=ss[:, :, n0:n0 + 256].rearrange("c p n -> p c n")),
                waits=w("psc"), incs=[("D_psc", 16)])
        else:
            ju = n0 // 256
            tok_w = D("sync", lambda e, ju=ju, ws=wsrc: e.dma_start(out=pw[:], in_=ws[ju]),
                      waits=extraw + w("pw"), incs=[("D_pw", 16)])
            tok_s = D("gpsimd", lambda e, ju=ju, ss=ssrc: e.dma_start(out=psc[:], in_=ss[ju]),
                      waits=w("psc"), incs=[("D_psc", 16)])
        t8 = []
        pend = None

        def tail(p):
            nonlocal t8
            r4_, r6_, g_, ds, vv, sclv, wt8, actw = p
            D("vector", lambda e, r4_=r4_, r6_=r6_: e.scalar_tensor_tensor(
                bv[:], r4_, 2.0, r6_, ALU.add, ALU.add), waits=actw)
            D("vector", lambda e, g_=g_: e.tensor_tensor(bv[:], bv[:], g_, ALU.mult))
            t8 = D("vector", lambda e, vv=vv, sclv=sclv, ds=ds: e.tensor_tensor(ds, vv, sclv, ALU.mult),
                   waits=wt8, incs=[("dve", 1)])

        for idx in range(8):
            ci, pl = idx // 2, idx % 2
            soff = ci * 512
            srcp = pw[:, soff:soff + wdt0]
            scl = psc[:, soff:soff + wdt0]
            sclv = scl.rearrange("p (c n) -> p c n", n=256)
            vv = bv[:].rearrange("p (c n) -> p c n", n=256)
            par = idx % 2
            r4_ = br4[par][:]; r6_ = br6[par][:]; g_ = bg[par][:]
            if kind == "gu":
                dall = dst.rearrange("p (c two n) -> p c two n", two=2, n=256)
                ds = dall[:, 2 * ci:2 * ci + 2, 0, :] if pl == 0 else dall[:, 2 * ci:2 * ci + 2, 1, :]
            else:
                doff = ci * 512 if pl == 0 else 2048 + ci * 512
                ds = dst[:, doff:doff + 512].rearrange("p (c n) -> p c n", n=256)
            wt = (tok_w + w("dtmp") if idx == 0 else []) + w(("deca", pl))
            if pl == 0:
                D("vector", lambda e, srcp=srcp: e.tensor_scalar(tA[:], srcp, 7, None, ALU.bitwise_and),
                  waits=wt)
                t2 = D("vector", lambda e, srcp=srcp: e.tensor_scalar(tB[:], srcp, 8, None,
                       ALU.bitwise_and), incs=[("dve", 1)])
                am, asn = tA[:], tB[:]
            else:
                D("vector", lambda e, srcp=srcp: e.tensor_scalar(tC[:], srcp, 4, None,
                  ALU.logical_shift_right), waits=wt,
                  incs=([("dve", 1)] if ci == 3 else []))
                if ci == 3:
                    setw("pw", [("dve", P.counts["dve"])])
                D("vector", lambda e: e.tensor_scalar(tB[:], tC[:], 8, None, ALU.bitwise_and))
                t2 = D("vector", lambda e: e.tensor_scalar(tC[:], tC[:], 7, None, ALU.bitwise_and),
                       incs=[("dve", 1)])
                am, asn = tC[:], tB[:]
            ty1 = D("scalar", lambda e, am=am, r4_=r4_: e.activation(r4_, am, AF.Prelu, bias=k.negfour[:],
                    alpha=0.5), waits=t2, incs=[("act", 1)])
            ty2 = D("scalar", lambda e, am=am, r6_=r6_: e.activation(r6_, am, AF.Relu, bias=k.negsix[:]),
                    incs=[("act", 1)])
            setw(("deca", pl), ty2)
            D("vector", lambda e, asn=asn, g_=g_: e.tensor_scalar(g_, asn, -0.25, 1.0, ALU.mult, ALU.add))
            if pend is not None:
                tail(pend)
            wt8 = (tok_s + w(dstkey)) if idx == 0 else []
            pend = (r4_, r6_, g_, ds, vv, sclv, wt8, ty1 + ty2)
        tail(pend)
        setw("psc", t8)
        setw("dtmp", t8)
        return t8

    for H in range(2):
        t0 = H * TH
        tok_xh = D("sync", lambda e, t0=t0: e.dma_start(
            out=xh[:].bitcast(BF16).rearrange("p (c t) -> p c t", c=16),
            in_=xnt_src[:, :, t0:t0 + TH].rearrange("c p t -> p c t")),
            waits=prev, incs=[("D_xh", 16)])
        tok_xh = D("scalar", lambda e: e.copy(xh[:], xh[:].bitcast(BF16)),
                   waits=tok_xh, incs=[("act", 1)])
        tok_ab = D("sync", lambda e, t0=t0: e.dma_start(
            out=accb[:].rearrange("p (d t) -> p d t", d=16),
            in_=acc_src[:, :, t0:t0 + TH].rearrange("d p t -> p d t")),
            waits=prev, incs=[("D_accb", 16)])
        tD_ = []
        decode_both = os.environ.get("FFN_DECODE_BOTH") == "1"
        for sup in range(16):
            int_toks = []
            for u4 in range(4):
                n0 = 1024 * sup + 256 * u4
                ug = sup * 4 + u4
                par = ug % 2
                bG, bU = bufG[par], bufU[par]
                if H == 0 or decode_both:
                    tG = decode_unit(n0, gwp_s, gsc_s, bG, ("bufG", par), "gu",
                                     (prev if ug == 0 else []) + w(("bufG_st", par)))
                    tGs = D("sync", lambda e, n0=n0, bG=bG: e.dma_start(
                        out=gdec_s[:, :, n0:n0 + 256].rearrange("c p n -> p c n"),
                        in_=bG[:].rearrange("p (c n) -> p c n", c=16)), waits=tG, incs=[("D_gst", 16)])
                    setw(("bufG_st", par), tGs)
                    tU = decode_unit(n0, uwp_s, usc_s, bU, ("bufU", par), "gu", w(("bufU_st", par)))
                    tUs = D("sync", lambda e, n0=n0, bU=bU: e.dma_start(
                        out=udec_s[:, :, n0:n0 + 256].rearrange("c p n -> p c n"),
                        in_=bU[:].rearrange("p (c n) -> p c n", c=16)), waits=tU, incs=[("D_ust", 16)])
                    setw(("bufU_st", par), tUs)
                    tD_ = decode_unit(n0, dwp_s, dsc_s, bufD[u4], ("bufD", u4), "d",
                                      w("down_cons") + w(("bufD_st", u4)))
                    tDs = D("sync", lambda e, n0=n0, u4=u4: e.dma_start(
                        out=ddec_s[n0 // 128:n0 // 128 + 2].rearrange("j p h -> p j h"),
                        in_=bufD[u4][:].rearrange("p (j h) -> p j h", j=2)), waits=tD_, incs=[("D_dst", 16)])
                    setw(("bufD_st", u4), tDs)
                else:
                    tG = D("sync", lambda e, n0=n0, bG=bG: e.dma_start(
                        out=bG[:].rearrange("p (c n) -> p c n", c=16),
                        in_=gdec_s[:, :, n0:n0 + 256].rearrange("c p n -> p c n")),
                        waits=(prev if ug == 0 else []) + w(("bufG", par)), incs=[("D_gld", 16)])
                    tU = D("sync", lambda e, n0=n0, bU=bU: e.dma_start(
                        out=bU[:].rearrange("p (c n) -> p c n", c=16),
                        in_=udec_s[:, :, n0:n0 + 256].rearrange("c p n -> p c n")),
                        waits=w(("bufU", par)), incs=[("D_uld", 16)])
                    tD_ = D("sync", lambda e, n0=n0, u4=u4: e.dma_start(
                        out=bufD[u4][:].rearrange("p (j h) -> p j h", j=2),
                        in_=ddec_s[n0 // 128:n0 // 128 + 2].rearrange("j p h -> p j h")),
                        waits=w("down_cons") + w(("bufD_st", u4)), incs=[("D_dld", 16)])
                    setw(("bufD_st", u4), tD_)
                for jb in range(2):
                    tg = []
                    tu = []
                    for half in range(2):
                        for kk in range(16):
                            tg = D("tensor", lambda e, kk=kk, jb=jb, hf=half, bG=bG: e.matmul(
                                p_g[:, hf * 512:hf * 512 + 512],
                                bG[:, kk * 256 + jb * 128:kk * 256 + jb * 128 + 128],
                                xh[:, kk * TH + hf * 512:kk * TH + hf * 512 + 512],
                                start=(kk == 0), stop=(kk == 15)),
                                waits=(tG + tok_xh + w("p_g") if (kk == 0 and half == 0) else []),
                                incs=([("pe", 1)] if (kk == 15 and half == 1) else []))
                    for half in range(2):
                        for kk in range(16):
                            tu = D("tensor", lambda e, kk=kk, jb=jb, hf=half, bU=bU: e.matmul(
                                p_u[:, hf * 512:hf * 512 + 512],
                                bU[:, kk * 256 + jb * 128:kk * 256 + jb * 128 + 128],
                                xh[:, kk * TH + hf * 512:kk * TH + hf * 512 + 512],
                                start=(kk == 0), stop=(kk == 15)),
                                waits=(tU + w("p_u") if (kk == 0 and half == 0) else []),
                                incs=([("pe", 1)] if (kk == 15 and half == 1) else []))
                    if jb == 1:
                        setw(("bufG", par), tg)
                        setw(("bufU", par), tu)
                    tok_gl = D("scalar", lambda e: e.activation(gel[:], p_g[:], AF.Gelu_apprx_tanh),
                               waits=tg + w("gel"), incs=[("act", 1)])
                    setw("p_g", tok_gl)
                    jg = 2 * u4 + jb
                    tok_it = D("vector", lambda e, jg=jg: e.tensor_tensor(
                        intT[:, jg * TH:jg * TH + TH], gel[:], p_u[:], ALU.mult),
                        waits=tok_gl + tu + w(("intT", jg)), incs=[("dve", 1)])
                    setw("gel", tok_it)
                    setw("p_u", tok_it)
                    int_toks.append(tok_it[0])
            dn_toks = []
            for ht in range(16):
                for tp in range(2):
                    par2 = (ht * 2 + tp) % 2
                    pd = [p_dA, p_dB][par2]
                    pdkey = ("p_d", par2)
                    tdn = []
                    for jp in range(8):
                        u4p, pl = jp // 2, jp % 2
                        tdn = D("tensor", lambda e, pd=pd, u4p=u4p, pl=pl, ht=ht, tp=tp, jp=jp: e.matmul(
                            pd[:], bufD[u4p][:, pl * 2048 + ht * 128:pl * 2048 + ht * 128 + 128],
                            intT[:, (2 * u4p + pl) * TH + tp * 512:(2 * u4p + pl) * TH + tp * 512 + 512],
                            start=(jp == 0), stop=(jp == 7)),
                            waits=(int_toks + w(pdkey) + tD_ if jp == 0 else []),
                            incs=([("pe", 1)] if jp == 7 else []))
                    tok_ae = D("vector", lambda e, pd=pd, ht=ht, tp=tp: e.tensor_tensor(
                        accb[:, ht * TH + tp * 512:ht * TH + tp * 512 + 512],
                        accb[:, ht * TH + tp * 512:ht * TH + tp * 512 + 512], pd[:], ALU.add),
                        waits=tdn + tok_ab, incs=[("dve", 1)])
                    setw(pdkey, tok_ae)
                    dn_toks.append(tok_ae[0])
            setw("down_cons", [dn_toks[-1]])
            for jg in range(8):
                setw(("intT", jg), [dn_toks[-1]])
        D("sync", lambda e, t0=t0: e.dma_start(
            out=out_dst[:, :, t0:t0 + TH].rearrange("d p t -> p d t"),
            in_=accb[:].rearrange("p (d t) -> p d t", d=16)),
            waits=[dn_toks[-1]], incs=[("D_outs", 16)])
        prev = P.barrier()
    D("sync", lambda e: None, waits=P.dma_flush_waits())
    st.close()


_NC_CACHE = None


def kernel(**inputs):
    global _NC_CACHE
    x = np.asarray(inputs["L0_attn_hidden_states"], np.float32)
    shared = host_prep(inputs)
    if _NC_CACHE is None:
        _NC_CACHE = build()
    nc = _NC_CACHE
    in_maps = []
    for b in range(8):
        m = dict(shared)
        m["x"] = np.ascontiguousarray(x[b])
        m["xT"] = np.ascontiguousarray(x[b].T)
        in_maps.append(m)
    res = run_bass_kernel_spmd(nc, in_maps, core_ids=list(range(8)))
    out = np.empty((8, SEQ, DM), np.float32)
    for b in range(8):
        out[b] = np.asarray(res.results[b]["out"]).T
    return out
